# version 18
# baseline (speedup 1.0000x reference)
import numpy as np
from contextlib import ExitStack

import concourse.bass as bass
import concourse.mybir as mybir

F32 = mybir.dt.float32
BF16 = mybir.dt.bfloat16
I32 = mybir.dt.int32
AF = mybir.ActivationFunctionType
ALU = mybir.AluOpType
AX = mybir.AxisListType

D = 2048
S = 2048
NT = D // 128
SC = 512
NSC = S // SC
SCALE = 1.0 / float(np.sqrt(128.0))

ENGINES = ("sp", "pool", "pe", "act", "dve")
DMA_UNIT = {"cc": 1}


class Step:
    __slots__ = ("idx", "engine", "fn", "reads", "writes", "kind", "stream",
                 "dma_count_after", "dma_count_before", "dma_slot",
                 "inc_needed", "waits", "milestone", "tag", "_deps")

    def __init__(self, idx, engine, fn, reads, writes, kind, stream, tag):
        self.idx, self.engine, self.fn = idx, engine, fn
        self.reads, self.writes = list(reads), list(writes)
        self.kind, self.stream, self.tag = kind, stream, tag
        self.dma_count_after = 0
        self.inc_needed = False
        self.waits = []
        self.milestone = 0


NSLOT = 6


class Sched:
    def __init__(self):
        self.steps = []
        self.last_write = {}
        self.readers = {}
        self.dma_counts = {}
        self.stream_seq = {}

    def step(self, engine, fn, reads=(), writes=(), dma=None, n_dma=0, tag=""):
        st = Step(len(self.steps), engine, fn, reads, writes,
                  "dma" if dma else "compute", dma, tag)
        if dma:
            seq = self.stream_seq.get(dma, 0)
            self.stream_seq[dma] = seq + 1
            slot = seq % NSLOT
            key = (dma, slot)
            st.dma_slot = key
            st.dma_count_before = self.dma_counts.get(key, 0)
            c = st.dma_count_before + n_dma
            self.dma_counts[key] = c
            st.dma_count_after = c
        deps = {}
        for b in st.reads:
            w = self.last_write.get(b)
            if w is not None:
                deps[w.idx] = w
        for b in st.writes:
            w = self.last_write.get(b)
            if w is not None:
                deps[w.idx] = w
            for r in self.readers.get(b, ()):
                deps[r.idx] = r
        st._deps = [deps[i] for i in sorted(deps)]
        for b in st.reads:
            self.readers.setdefault(b, []).append(st)
        for b in st.writes:
            self.last_write[b] = st
            self.readers[b] = []
        self.steps.append(st)
        return st

    def emit(self, nc, es):
        eng_sem = {e: es.enter_context(nc.semaphore(f"sg_{e}")) for e in ENGINES}
        dma_sems = {k: es.enter_context(nc.semaphore(f"sd_{k[0]}{k[1]}"))
                    for k in self.dma_counts}
        for st in self.steps:
            raw = []
            if st.kind == "dma" and st.dma_count_before > 0:
                unit = DMA_UNIT.get(st.stream, 16)
                raw.append(("dma", st.dma_slot, unit * st.dma_count_before))
            for d in st._deps:
                if d.kind == "dma":
                    unit = DMA_UNIT.get(d.stream, 16)
                    raw.append(("dma", d.dma_slot, unit * d.dma_count_after))
                elif d.engine == st.engine == "pe":
                    continue
                else:
                    d.inc_needed = True
                    raw.append(("pend", d, 0))
            st.waits = raw
        inc_count = {e: 0 for e in ENGINES}
        for st in self.steps:
            if st.inc_needed and st.kind == "compute":
                inc_count[st.engine] += 1
            st.milestone = inc_count[st.engine]
        for st in self.steps:
            best = {}
            for kind, key, val in st.waits:
                if kind == "pend":
                    if key.kind == "dma":
                        continue
                    kind, key, val = "eng", key.engine, key.milestone
                k = (kind, key)
                if val > best.get(k, 0):
                    best[k] = val
            st.waits = [(k[0], k[1], v) for k, v in best.items()]

        per_engine = {e: [] for e in ENGINES}
        for st in self.steps:
            per_engine[st.engine].append(st)

        block = es.enter_context(nc.Block())

        def make_section(ename):
            hw = {}

            def body(eng):
                for st in per_engine[ename]:
                    for kind, key, val in st.waits:
                        k = (kind, key)
                        if val == 0 or hw.get(k, 0) >= val:
                            continue
                        hw[k] = val
                        sem = eng_sem[key] if kind == "eng" else dma_sems[key]
                        eng.wait_ge(sem, val)
                    last = st.fn(eng)
                    if st.kind == "dma":
                        assert isinstance(last, (list, tuple)) and last, \
                            f"dma step {st.tag} must return dma instrs"
                        unit = DMA_UNIT.get(st.stream, 16)
                        for ins in last:
                            ins.then_inc(dma_sems[st.dma_slot], unit)
                    elif st.inc_needed:
                        assert last is not None, f"step {st.tag} needs last instr"
                        last.then_inc(eng_sem[st.engine], 1)
            return body

        for ename, attr in (("sp", "sync"), ("pool", "gpsimd"), ("pe", "tensor"),
                            ("act", "scalar"), ("dve", "vector")):
            if per_engine[ename]:
                getattr(block, attr)(make_section(ename))
        return eng_sem, dma_sems


def keyspan(name, c0, n):
    return [(name, c0 + k) for k in range(n)]


class Ctx:
    pass


def alloc_common(nc, es):
    c = Ctx()
    c.nc = nc
    c.g = Sched()
    sbuf = lambda n, sh, dt: es.enter_context(nc.sbuf_tensor(n, sh, dt))
    psum = lambda n, sh, dt: es.enter_context(nc.psum_tensor(n, sh, dt))
    c.BIGA = sbuf("BIGA", [128, NT * S], BF16)
    c.BIGB = sbuf("BIGB", [128, NT * S], BF16)
    c.QCH = sbuf("QCH", [128, NT * SC], BF16)
    c.EBUF = sbuf("EBUF", [128, NT * SC], BF16)
    c.WBUF = sbuf("WBUF", [128, 2 * S], BF16)
    c.XTI = sbuf("XTI", [128, 2 * S], F32)
    c.PANA = sbuf("PANA", [128, S], F32)
    c.PANB = sbuf("PANB", [128, S], F32)
    c.CAM = sbuf("CAM", [128, SC], F32)
    c.BCOL = sbuf("BCOL", [128, 64], F32)
    c.SCOL = sbuf("SCOL", [128, 144], F32)
    c.PERM = sbuf("PERM", [128, 128], BF16)
    c.IDN = sbuf("IDN", [128, 128], BF16)
    c.SUT = sbuf("SUT", [128, 128], BF16)
    c.ONS = sbuf("ONS", [128, 128], BF16)
    c.MCOL = sbuf("MCOL", [128, 16], BF16)
    c.QBF = sbuf("QBF", [128, 512], BF16)
    c.BQKV = sbuf("BQKV", [128, 48], F32)
    c.ONES1F = sbuf("ONES1F", [1, 128], F32)
    c.ONE = sbuf("ONE", [128, 8], BF16)
    c.PS_A = psum("PS_A", [128, 2048], F32)
    c.PS_B = psum("PS_B", [128, 2048], F32)
    return c


def biga(c, i):
    return c.BIGA[:, i * S:(i + 1) * S]


def bigb(c, i):
    return c.BIGB[:, i * S:(i + 1) * S]


def build_preload(c, t):
    g = c.g

    def fn(eng):
        ins = [
            eng.dma_start(out=c.ONE[:, 0:1], in_=t["ones_col"][:, :]),
            eng.dma_start(out=c.BCOL[:, 30:32], in_=t["eps_col"][:, :]),
        ]
        return ins
    g.step("sp", fn, writes=[("ONE",), ("BCOL", 30)], dma="ld",
           n_dma=2, tag="pre")


def build_ln(c, xsrc, rows_d, yd=None, eps=1e-5, tag="ln"):
    nc, g = c.nc, c.g
    for i in range(NT):
        xb = biga(c, i)
        kxb = keyspan("BIGA", 4 * i, 4)
        sl = i % 2
        xq = c.EBUF[:, sl * S:(sl + 1) * S]
        kxq = keyspan("EBUF", 4 * sl, 4)
        g.step("pool", lambda eng, i=i, xb=xb: [
            eng.dma_start(out=xb, in_=xsrc[128 * i:128 * (i + 1), :])],
            reads=[("xm", i)] + [("xm2", i, s_) for s_ in range(4)],
            writes=kxb, dma="wt", n_dma=1, tag=f"{tag}_lx{i}")
        g.step("act", lambda eng, xb=xb, xq=xq: eng.activation(xq, xb, AF.Square),
               reads=kxb, writes=kxq, tag=f"{tag}_sq{i}")

        def mm(eng, i=i, xb=xb, xq=xq):
            last = None
            for ch in range(4):
                last = eng.matmul(c.PS_B[0:1, 512 * ch:512 * ch + 512],
                                  c.ONE[:, 0:1], xb[:, 512 * ch:512 * ch + 512],
                                  start=(i == 0), stop=(i == NT - 1))
                last = eng.matmul(c.PS_A[0:1, 512 * ch:512 * ch + 512],
                                  c.ONE[:, 0:1], xq[:, 512 * ch:512 * ch + 512],
                                  start=(i == 0), stop=(i == NT - 1))
            return last
        g.step("pe", mm, reads=kxb + kxq + [("ONE",)],
               writes=keyspan("PS_B", 0, 4) + keyspan("PS_A", 0, 4),
               tag=f"{tag}_mm{i}")

    mean_r = c.PANA[0:1, :]
    a_r = c.PANB[0:1, :]
    b_r = c.XTI[0:1, 0:2048]
    g.step("dve", lambda eng: eng.tensor_scalar(
        mean_r, c.PS_B[0:1, :], 1.0 / D, None, ALU.mult),
        reads=keyspan("PS_B", 0, 4), writes=[("PANA",)], tag=f"{tag}_mean")
    g.step("dve", lambda eng: eng.tensor_mul(a_r, mean_r, mean_r),
           reads=[("PANA",)], writes=[("PANB",)], tag=f"{tag}_m2")
    g.step("dve", lambda eng: eng.scalar_tensor_tensor(
        a_r, c.PS_A[0:1, :], 1.0 / D, a_r, ALU.mult, ALU.subtract),
        reads=keyspan("PS_A", 0, 4) + [("PANB",)], writes=[("PANB",)],
        tag=f"{tag}_var")
    epscol = 30 if eps == 1e-5 else 31
    g.step("act", lambda eng: eng.activation(a_r, a_r, AF.Ln,
                                             bias=c.BCOL[0:1, epscol:epscol + 1]),
           reads=[("PANB",), ("BCOL", 30)], writes=[("PANB",)], tag=f"{tag}_lnv")
    g.step("act", lambda eng: eng.activation(a_r, a_r, AF.Exp, scale=-0.5),
           reads=[("PANB",)], writes=[("PANB",)], tag=f"{tag}_rstd")
    g.step("dve", lambda eng: eng.scalar_tensor_tensor(
        b_r, mean_r, -1.0, a_r, ALU.mult, ALU.mult),
        reads=[("PANA",), ("PANB",)], writes=[("XTI", 0)], tag=f"{tag}_bb")
    g.step("sp", lambda eng: [
        eng.dma_start(out=rows_d[0:1, :], in_=a_r),
        eng.dma_start(out=rows_d[1:2, :], in_=b_r)],
        reads=[("PANB",), ("XTI", 0)], writes=[("rows_d", 0), ("rows_d", 1)],
        dma="st", n_dma=2, tag=f"{tag}_rst")
    pan_a = c.PANA[:, :]
    pan_b = c.PANB[:, :]
    g.step("sp", lambda eng: [
        eng.dma_start(out=pan_a, in_=rows_d[0:1, :].partition_broadcast(128)),
        eng.dma_start(out=pan_b, in_=rows_d[1:2, :].partition_broadcast(128))],
        reads=[("rows_d", 0), ("rows_d", 1), ("PANA",), ("PANB",)],
        writes=[("PANA",), ("PANB",)], dma="ld", n_dma=2, tag=f"{tag}_bc")
    for i in range(NT):
        xb = biga(c, i)
        kxb = keyspan("BIGA", 4 * i, 4)
        apeng = "dve" if i % 2 == 0 else "pool"
        g.step(apeng, lambda eng, xb=xb: eng.tensor_tensor(
            xb, xb, pan_a, ALU.mult),
            reads=kxb + [("PANA",)], writes=kxb, tag=f"{tag}_ap1{i}")
        g.step(apeng, lambda eng, xb=xb: eng.tensor_tensor(
            xb, xb, pan_b, ALU.add),
            reads=kxb + [("PANB",)], writes=kxb, tag=f"{tag}_ap2{i}")
        if yd is not None:
            g.step("sp", lambda eng, i=i, xb=xb: [
                eng.dma_start(out=yd[128 * i:128 * (i + 1), :], in_=xb)],
                reads=kxb, writes=[("yd", i)], dma="st", n_dma=1,
                tag=f"{tag}_sy{i}")


def build_l0_attn(c, t, xsrc):
    nc, g = c.nc, c.g
    w = t["l0_w_qkv"]
    xm = t["xm"]
    qT_d, vT_d = t["qT_d"], t["vT_d"]

    g.step("pool", lambda eng: [
        eng.dma_start(out=c.BQKV[:, :], in_=t["l0_b_qkv"][:, :]),
        eng.dma_start(out=c.BCOL[:, 2:18], in_=t["l0_b_proj"][:, :])],
        writes=[("BQKV",), ("BCOL", 2)], dma="wt", n_dma=2, tag="a_bld")
    g.step("pool", lambda eng: [
        eng.dma_start(out=c.BCOL[:, 0:1], in_=t["l0_cam_gate"][:, :]),
        eng.dma_start(out=c.PANB[64:65, 0:2], in_=t["l0_cam_w"][0:1, :])],
        writes=[("BCOL", 0), ("PANB",)], dma="wt", n_dma=2, tag="a_cld")
    g.step("dve", lambda eng: eng.scalar_tensor_tensor(
        c.PANB[64:65, 2:3], c.PANB[64:65, 1:2], 0.5, c.PANB[64:65, 0:1],
        ALU.mult, ALU.add), reads=[("PANB",)], writes=[("PANB",)], tag="a_lw0")
    g.step("act", lambda eng: eng.activation(
        c.PANB[64:65, 2:3], c.PANB[64:65, 2:3], AF.Sigmoid),
        reads=[("PANB",)], writes=[("PANB",)], tag="a_lw1")
    g.step("sp", lambda eng: [
        eng.dma_start(out=t["rows_d"][2:3, 0:1], in_=c.PANB[64:65, 2:3])],
        reads=[("PANB",)], writes=[("rows_d", 2)], dma="st", n_dma=1, tag="a_lw2")
    g.step("sp", lambda eng: [
        eng.dma_start(out=c.BCOL[:, 1:2],
                      in_=t["rows_d"][2:3, 0:1].partition_broadcast(128))],
        reads=[("rows_d", 2)], writes=[("BCOL", 1)], dma="ld", n_dma=1, tag="a_lw3")

    wslots = [(c.WBUF[:, 0:S], keyspan("WBUF", 0, 4)),
              (c.WBUF[:, S:2 * S], keyspan("WBUF", 4, 4)),
              (c.EBUF[:, 0:S], keyspan("EBUF", 0, 4)),
              (c.EBUF[:, S:2 * S], keyspan("EBUF", 4, 4)),
              (c.EBUF[:, 2 * S:3 * S], keyspan("EBUF", 8, 4)),
              (c.EBUF[:, 3 * S:4 * S], keyspan("EBUF", 12, 4))]
    for j in range(48):
        wtile, kw = wslots[j % 6]
        src = w.rearrange("(a p) (j f) -> j p a f", p=128, f=128)[j]
        wt3 = wtile.rearrange("p (a f) -> p a f", f=128)
        g.step("pool", lambda eng, src=src, wt3=wt3: [
            eng.dma_start(out=wt3, in_=src)],
            reads=[], writes=kw, dma="wt", n_dma=1, tag=f"a_wl{j}")
        ps = c.PS_A if j % 2 == 0 else c.PS_B
        pskeys = keyspan("PS_A" if j % 2 == 0 else "PS_B", 0, 4)

        for d in range(NT):
            def qkvmm_d(eng, j=j, ps=ps, wtile=wtile, d=d):
                lhsT = wtile[:, d * 128:(d + 1) * 128]
                last = None
                for ch in range(4):
                    last = eng.matmul(
                        ps[:, 512 * ch:512 * ch + 512], lhsT,
                        biga(c, d)[:, 512 * ch:512 * ch + 512],
                        start=(d == 0), stop=(d == NT - 1))
                return last
            g.step("pe", qkvmm_d,
                   reads=kw + keyspan("BIGA", 4 * d, 4),
                   writes=pskeys, tag=f"a_mm{j}_{d}")
        if 16 <= j < 32:
            dst = bigb(c, j - 16)
            dkeys = keyspan("BIGB", 4 * (j - 16), 4)
            g.step("act", lambda eng, ps=ps, dst=dst, j=j: eng.activation(
                dst, ps[:, :], AF.Identity, bias=c.BQKV[:, j:j + 1]),
                reads=pskeys + [("BQKV",)], writes=dkeys, tag=f"a_evK{j}")
        else:
            ssl = j % 2
            stg = c.QCH[:, ssl * S:(ssl + 1) * S]
            skeys = keyspan("QCH", 4 * ssl, 4)
            g.step("act", lambda eng, ps=ps, stg=stg, j=j: eng.activation(
                stg, ps[:, :], AF.Identity, bias=c.BQKV[:, j:j + 1]),
                reads=pskeys + [("BQKV",)], writes=skeys, tag=f"a_ev{j}")
            dd = qT_d if j < 16 else vT_d
            r = j if j < 16 else j - 32
            g.step("sp", lambda eng, stg=stg, dd=dd, r=r: [
                eng.dma_start(out=dd[128 * r:128 * (r + 1), :], in_=stg)],
                reads=skeys, writes=[("qT_d" if j < 16 else "vT_d", r)],
                dma="st", n_dma=1, tag=f"a_sp{j}")

    for tt in range(NT):
        g.step("sp", lambda eng, tt=tt: [
            eng.dma_start(out=biga(c, tt), in_=vT_d[0:2048, 128 * tt:128 * (tt + 1)],
                          transpose=True)],
            reads=[("vT_d", r) for r in range(16)] + keyspan("BIGA", 4 * tt, 4),
            writes=keyspan("BIGA", 4 * tt, 4), dma="tp", n_dma=1, tag=f"a_vt{tt}")

    for sc in range(NSC):
        s0 = SC * sc
        qsrc = qT_d.rearrange("(a p) s -> p a s", p=128)[:, :, s0:s0 + SC]
        qdst = c.QCH[:, :].rearrange("p (a s) -> p a s", s=512)
        g.step("sp", lambda eng, qsrc=qsrc, qdst=qdst: [
            eng.dma_start(out=qdst, in_=qsrc)],
            reads=[("qT_d", r) for r in range(16)] + keyspan("QCH", 0, 16),
            writes=keyspan("QCH", 0, 16), dma="ld", n_dma=1, tag=f"a_ql{sc}")
        qch = lambda e: c.QCH[:, e * SC:(e + 1) * SC]

        for tg in range(4):
            pst = c.PS_A if tg % 2 == 0 else c.PS_B
            pstn = "PS_A" if tg % 2 == 0 else "PS_B"

            def bsmm(eng, tg=tg, pst=pst):
                last = None
                for k4 in range(4):
                    tt = 4 * tg + k4
                    last = eng.matmul(pst[:, 512 * k4:512 * k4 + 512],
                                      bigb(c, 0)[:, 128 * tt:128 * (tt + 1)],
                                      qch(0), start=True, stop=True)
                return last
            g.step("pe", bsmm,
                   reads=keyspan("BIGB", 0, 4) + keyspan("QCH", 0, 1),
                   writes=keyspan(pstn, 0, 4), tag=f"a_bs{sc}_{tg}")
            ebt = c.EBUF[:, 4 * tg * SC:(4 * tg + 4) * SC]
            g.step("act", lambda eng, ebt=ebt, pst=pst: eng.activation(
                ebt, pst[:, :], AF.Exp, scale=SCALE),
                reads=keyspan(pstn, 0, 4),
                writes=keyspan("EBUF", 4 * tg, 4), tag=f"a_be{sc}_{tg}")
        def bdmm(eng):
            last = None
            for tt in range(NT):
                last = eng.matmul(c.PS_B[0:1, 0:512], c.ONE[:, 0:1],
                                  c.EBUF[:, tt * SC:(tt + 1) * SC],
                                  start=(tt == 0), stop=(tt == NT - 1))
            return last
        g.step("pe", bdmm, reads=keyspan("EBUF", 0, 16) + [("ONE",)],
               writes=[("PS_B", 0)], tag=f"a_bd{sc}")
        g.step("dve", lambda eng: eng.reciprocal(c.PANB[0:1, 0:512], c.PS_B[0:1, 0:512]),
               reads=[("PS_B", 0)], writes=[("PANB",)], tag=f"a_br{sc}")
        g.step("sp", lambda eng: [
            eng.dma_start(out=t["rows_d"][3:4, 0:512], in_=c.PANB[0:1, 0:512])],
            reads=[("PANB",)], writes=[("rows_d", 3)], dma="st", n_dma=1,
            tag=f"a_brs{sc}")
        g.step("sp", lambda eng: [
            eng.dma_start(out=c.PANA[:, 0:512],
                          in_=t["rows_d"][3:4, 0:512].partition_broadcast(128))],
            reads=[("rows_d", 3), ("PANA",)], writes=[("PANA",)], dma="ld",
            n_dma=1, tag=f"a_brb{sc}")
        def camm(eng):
            last = None
            for tt in range(NT):
                last = eng.matmul(c.PS_B[:, 512:1024], biga(c, tt)[:, 0:128],
                                  c.EBUF[:, tt * SC:(tt + 1) * SC],
                                  start=(tt == 0), stop=(tt == NT - 1))
            return last
        g.step("pe", camm, reads=keyspan("BIGA", 0, 64) + keyspan("EBUF", 0, 16),
               writes=[("PS_B", 1)], tag=f"a_cm{sc}")
        g.step("dve", lambda eng: eng.scalar_tensor_tensor(
            c.CAM[:, :], c.PS_B[:, 512:1024], c.BCOL[:, 0:1], c.PANA[:, 0:512],
            ALU.mult, ALU.mult),
            reads=[("PS_B", 1), ("BCOL", 0), ("PANA",)], writes=[("CAM",)],
            tag=f"a_cn{sc}")
        g.step("act", lambda eng: eng.activation(c.CAM[:, :], c.CAM[:, :], AF.Tanh),
               reads=[("CAM",)], writes=[("CAM",)], tag=f"a_ct{sc}")
        g.step("dve", lambda eng: eng.tensor_scalar(
            c.WBUF[:, 0:512], c.CAM[:, :], c.BCOL[:, 1:2], None, ALU.mult),
            reads=[("CAM",), ("BCOL", 1)], writes=[("WBUF", 0)], tag=f"a_cl{sc}")

        for tg in range(4):
            pst = c.PS_A if tg % 2 == 0 else c.PS_B
            pstn = "PS_A" if tg % 2 == 0 else "PS_B"

            def fsmm(eng, tg=tg, pst=pst):
                last = None
                for k4 in range(4):
                    tt = 4 * tg + k4
                    psc = pst[:, 512 * k4:512 * k4 + 512]
                    for e in range(NT):
                        last = eng.matmul(psc, bigb(c, e)[:, 128 * tt:(tt + 1) * 128],
                                          qch(e), start=(e == 0), stop=False)
                    last = eng.matmul(psc, bigb(c, 0)[:, 128 * tt:(tt + 1) * 128],
                                      c.WBUF[:, 0:512], start=False, stop=True)
                return last
            g.step("pe", fsmm,
                   reads=keyspan("BIGB", 0, 64) + keyspan("QCH", 0, 16) + [("WBUF", 0)],
                   writes=keyspan(pstn, 0, 4), tag=f"a_fs{sc}_{tg}")
            ebt = c.EBUF[:, 4 * tg * SC:(4 * tg + 4) * SC]
            g.step("act", lambda eng, ebt=ebt, pst=pst: eng.activation(
                ebt, pst[:, :], AF.Exp, scale=SCALE),
                reads=keyspan(pstn, 0, 4),
                writes=keyspan("EBUF", 4 * tg, 4), tag=f"a_fe{sc}_{tg}")
        g.step("pe", bdmm, reads=keyspan("EBUF", 0, 16) + [("ONE",)],
               writes=[("PS_B", 0)], tag=f"a_fd{sc}")
        g.step("dve", lambda eng: eng.reciprocal(c.PANB[0:1, 0:512], c.PS_B[0:1, 0:512]),
               reads=[("PS_B", 0)], writes=[("PANB",)], tag=f"a_fr{sc}")
        g.step("sp", lambda eng: [
            eng.dma_start(out=t["rows_d"][3:4, 0:512], in_=c.PANB[0:1, 0:512])],
            reads=[("PANB",)], writes=[("rows_d", 3)], dma="st", n_dma=1,
            tag=f"a_frs{sc}")
        g.step("sp", lambda eng: [
            eng.dma_start(out=c.PANA[:, 0:512],
                          in_=t["rows_d"][3:4, 0:512].partition_broadcast(128))],
            reads=[("rows_d", 3), ("PANA",)], writes=[("PANA",)], dma="ld",
            n_dma=1, tag=f"a_frb{sc}")

        for e in range(NT):
            bank = e % 2
            psc, pk = (c.PS_B[:, 1024 + 512 * bank:1536 + 512 * bank], ("PS_B", 2 + bank))

            def avmm(eng, e=e, psc=psc):
                last = None
                for tt in range(NT):
                    last = eng.matmul(psc, biga(c, tt)[:, 128 * e:(e + 1) * 128],
                                      c.EBUF[:, tt * SC:(tt + 1) * SC],
                                      start=(tt == 0), stop=(tt == NT - 1))
                return last
            g.step("pe", avmm, reads=keyspan("BIGA", 0, 64) + keyspan("EBUF", 0, 16),
                   writes=[pk], tag=f"a_av{sc}_{e}")
            ctx_t = c.QCH[:, e * SC:(e + 1) * SC]
            g.step("dve", lambda eng, psc=psc, ctx_t=ctx_t: eng.tensor_mul(
                ctx_t, psc, c.PANA[:, 0:512]),
                reads=[pk, ("PANA",)], writes=[("QCH", e)], tag=f"a_an{sc}_{e}")

        pslots = [(c.WBUF[:, S:2 * S], keyspan("WBUF", 4, 4)),
                  (c.EBUF[:, 0:S], keyspan("EBUF", 0, 4)),
                  (c.EBUF[:, S:2 * S], keyspan("EBUF", 4, 4)),
                  (c.EBUF[:, 2 * S:3 * S], keyspan("EBUF", 8, 4))]
        for jo in range(NT):
            wtile, kw = pslots[jo % 4]
            src = t["l0_w_proj"].rearrange("(a p) (j f) -> j p a f", p=128, f=128)[jo]
            wt3 = wtile.rearrange("p (a f) -> p a f", f=128)
            g.step("pool", lambda eng, src=src, wt3=wt3: [
                eng.dma_start(out=wt3, in_=src)],
                writes=kw, dma="wt", n_dma=1, tag=f"a_pw{sc}_{jo}")
            psc, pk = (c.PS_A[:, 512 * (jo % 2):512 * (jo % 2) + 512], ("PS_A", jo % 2))

            def pmm(eng, wtile=wtile, psc=psc, jo=jo):
                last = None
                for e in range(NT):
                    last = eng.matmul(psc, wtile[:, e * 128:(e + 1) * 128],
                                      c.QCH[:, e * SC:(e + 1) * SC],
                                      start=(e == 0), stop=(e == NT - 1))
                return last
            g.step("pe", pmm, reads=kw + keyspan("QCH", 0, 16) + [("BROW",), ("ONER",)],
                   writes=[pk], tag=f"a_pm{sc}_{jo}")
            xsl = jo % 2
            xt = c.XTI[:, xsl * S + 0: xsl * S + 512]
            g.step("sp", lambda eng, jo=jo, xt=xt, s0=s0: [
                eng.dma_start(out=xt, in_=xsrc[128 * jo:128 * (jo + 1), s0:s0 + SC])],
                reads=[("xm2", jo, sc)], writes=[("XTI", xsl)], dma="ld", n_dma=1,
                tag=f"a_rx{sc}_{jo}")
            g.step("dve", lambda eng, xt=xt, psc=psc, jo=jo: eng.scalar_tensor_tensor(
                xt, psc, c.BCOL[:, 2 + jo:3 + jo], xt, ALU.add, ALU.add),
                reads=[pk, ("XTI", xsl), ("BCOL", 2)], writes=[("XTI", xsl)],
                tag=f"a_ra{sc}_{jo}")
            g.step("sp", lambda eng, jo=jo, xt=xt, s0=s0: [
                eng.dma_start(out=xm[128 * jo:128 * (jo + 1), s0:s0 + SC], in_=xt)],
                reads=[("XTI", xsl)], writes=[("xm2", jo, sc), ("xm", jo)],
                dma="st", n_dma=1, tag=f"a_rs{sc}_{jo}")


def finalize(c, outs_dram_keys):
    g = c.g
    g.step("pool", lambda eng: None, reads=list(outs_dram_keys), tag="final")


HCH = [(0, 16), (16, 32), (32, 44)]


def build_l0_ffn(c, t, xsrc):
    nc, g = c.nc, c.g
    xm, dpart = t["xm"], t["dpart"]

    g.step("pool", lambda eng: [
        eng.dma_start(out=c.SCOL[:, 0:104], in_=t["l0_scol"][:, :])],
        writes=[("SCOL",)], dma="wt", n_dma=1, tag="b_scol")

    def load_int_block(wname, wsrc, hh, nh_cols, xsl, esl, tagn):
        src3 = wsrc.rearrange("(a p) (j f) -> j p a f", p=128, f=128)[hh]
        ebuf = c.EBUF[:, esl * S:(esl + 1) * S]
        eb3 = ebuf.rearrange("p (a f) -> p a f", f=128)
        ek = keyspan("EBUF", 4 * esl, 4)
        g.step("pool", lambda eng: [eng.dma_start(out=eb3, in_=src3)],
               reads=[], writes=ek, dma="wt", n_dma=1, tag=f"b_wl{tagn}")
        g.step("dve", lambda eng: eng.tensor_scalar(
            ebuf, ebuf, -128.0, None, ALU.add),
            reads=ek, writes=ek, tag=f"b_wc{tagn}")
        return ebuf

    for hci, (h0, h1) in enumerate(HCH):
        nh = h1 - h0
        for hh in range(h0, h1):
            hl = hh - h0
            xsl = hh % 2
            wg = load_int_block("g", t["l0_wgT"], hh, 128, xsl, (hh % 2) * 2 + 0,
                                f"g{hh}")
            wu = load_int_block("u", t["l0_wuT"], hh, 128, xsl, (hh % 2) * 2 + 1,
                                f"u{hh}")
            kwg = keyspan("EBUF", 4 * ((hh % 2) * 2 + 0), 4)
            kwu = keyspan("EBUF", 4 * ((hh % 2) * 2 + 1), 4)

            def gmm(eng, wg=wg):
                last = None
                for d in range(NT):
                    lhsT = wg[:, d * 128:(d + 1) * 128]
                    for ch in range(4):
                        last = eng.matmul(c.PS_A[:, 512 * ch:512 * ch + 512],
                                          lhsT,
                                          biga(c, d)[:, 512 * ch:512 * ch + 512],
                                          start=(d == 0), stop=(d == NT - 1))
                return last
            g.step("pe", gmm, reads=kwg + keyspan("BIGA", 0, 64),
                   writes=keyspan("PS_A", 0, 4), tag=f"b_gm{hh}")

            def umm(eng, wu=wu):
                last = None
                for d in range(NT):
                    lhsT = wu[:, d * 128:(d + 1) * 128]
                    for ch in range(4):
                        last = eng.matmul(c.PS_B[:, 512 * ch:512 * ch + 512],
                                          lhsT,
                                          biga(c, d)[:, 512 * ch:512 * ch + 512],
                                          start=(d == 0), stop=(d == NT - 1))
                return last
            g.step("pe", umm, reads=kwu + keyspan("BIGA", 0, 64),
                   writes=keyspan("PS_B", 0, 4), tag=f"b_um{hh}")

            for ch in range(4):
                psg = c.PS_A[:, 512 * ch:512 * ch + 512]
                psu = c.PS_B[:, 512 * ch:512 * ch + 512]
                g.step("act", lambda eng, psg=psg, hh=hh: eng.activation(
                    c.CAM[:, :], psg, AF.Silu, scale=c.SCOL[:, hh:hh + 1]),
                    reads=[("PS_A", ch), ("SCOL",)], writes=[("CAM",)],
                    tag=f"b_si{hh}_{ch}")
                atile = bigb(c, hl)[:, 512 * ch:512 * ch + 512]
                g.step("dve", lambda eng, psu=psu, atile=atile, hh=hh:
                       eng.scalar_tensor_tensor(
                           atile, psu, c.SCOL[:, 44 + hh:45 + hh], c.CAM[:, :],
                           ALU.mult, ALU.mult),
                       reads=[("PS_B", ch), ("CAM",), ("SCOL",)],
                       writes=[("BIGB", 4 * hl + ch)], tag=f"b_ac{hh}_{ch}")

        for jo in range(NT):
            src3 = t["l0_wdT"].rearrange("(a p) (j f) -> j p a f", p=128, f=128)
            blk = src3[jo][:, h0:h1, :]
            esl = jo % 2
            wd = c.EBUF[:, esl * S:esl * S + nh * 128]
            wd3 = wd.rearrange("p (a f) -> p a f", f=128)
            ek = keyspan("EBUF", 4 * esl, 4)
            g.step("pool", lambda eng, wd3=wd3, blk=blk: [
                eng.dma_start(out=wd3, in_=blk)],
                reads=[], writes=ek, dma="wt", n_dma=1,
                tag=f"b_dl{hci}_{jo}")
            g.step("dve", lambda eng, wd=wd: eng.tensor_scalar(
                wd, wd, -128.0, None, ALU.add),
                reads=ek, writes=ek, tag=f"b_dc{hci}_{jo}")

            ps = c.PS_A if jo % 2 == 0 else c.PS_B
            pskeys = keyspan("PS_A" if jo % 2 == 0 else "PS_B", 0, 4)

            def dmm(eng, wd=wd, ps=ps, nh=nh, h0=h0):
                last = None
                for a in range(nh):
                    lhsT = wd[:, a * 128:(a + 1) * 128]
                    for ch in range(4):
                        last = eng.matmul(ps[:, 512 * ch:512 * ch + 512],
                                          lhsT,
                                          bigb(c, a)[:, 512 * ch:512 * ch + 512],
                                          start=(a == 0), stop=(a == nh - 1))
                return last
            g.step("pe", dmm,
                   reads=keyspan("EBUF", 4 * esl, 4) + keyspan("BIGB", 0, 4 * nh),
                   writes=pskeys, tag=f"b_dm{hci}_{jo}")

            if hci == 0:
                g.step("act", lambda eng, ps=ps: eng.copy(c.PANA[:, :], ps[:, :]),
                       reads=pskeys, writes=[("PANA",)], tag=f"b_e0{hci}_{jo}")
                g.step("sp", lambda eng, jo=jo: [
                    eng.dma_start(out=dpart[128 * jo:128 * (jo + 1), :], in_=c.PANA[:, :])],
                    reads=[("PANA",)], writes=[("dpart", jo)], dma="st", n_dma=1,
                    tag=f"b_s0{hci}_{jo}")
            elif hci == 1:
                g.step("act", lambda eng, ps=ps: eng.copy(c.PANA[:, :], ps[:, :]),
                       reads=pskeys, writes=[("PANA",)], tag=f"b_e1{hci}_{jo}")
                g.step("pool", lambda eng, jo=jo: [
                    eng.dma_start(out=c.PANA[:, :],
                                  in_=dpart[128 * jo:128 * (jo + 1), :],
                                  accum_op=ALU.add)],
                    reads=[("dpart", jo), ("PANA",)], writes=[("PANA",)],
                    dma="wt", n_dma=1, tag=f"b_l1{hci}_{jo}")
                g.step("sp", lambda eng, jo=jo: [
                    eng.dma_start(out=dpart[128 * jo:128 * (jo + 1), :], in_=c.PANA[:, :])],
                    reads=[("PANA",)], writes=[("dpart", jo)], dma="st", n_dma=1,
                    tag=f"b_s1{hci}_{jo}")
            else:
                g.step("act", lambda eng, ps=ps: eng.copy(c.PANA[:, :], ps[:, :]),
                       reads=pskeys, writes=[("PANA",)], tag=f"b_e2{hci}_{jo}")
                g.step("pool", lambda eng, jo=jo: [
                    eng.dma_start(out=c.PANA[:, :],
                                  in_=dpart[128 * jo:128 * (jo + 1), :],
                                  accum_op=ALU.add)],
                    reads=[("dpart", jo), ("PANA",)], writes=[("PANA",)],
                    dma="wt", n_dma=1, tag=f"b_l2{hci}_{jo}")
                g.step("sp", lambda eng, jo=jo: [
                    eng.dma_start(out=c.PANB[:, :], in_=xsrc[128 * jo:128 * (jo + 1), :])],
                    reads=[("xm", jo)] + [("xm2", jo, s_) for s_ in range(4)] +
                    [("PANB",)], writes=[("PANB",)],
                    dma="ld", n_dma=1, tag=f"b_lx{hci}_{jo}")
                g.step("dve", lambda eng, jo=jo: eng.scalar_tensor_tensor(
                    c.PANB[:, :], c.PANA[:, :], c.SCOL[:, 88 + jo:89 + jo],
                    c.PANB[:, :], ALU.mult, ALU.add),
                    reads=[("PANA",), ("PANB",), ("SCOL",)], writes=[("PANB",)],
                    tag=f"b_f{hci}_{jo}")
                g.step("sp", lambda eng, jo=jo: [
                    eng.dma_start(out=xm[128 * jo:128 * (jo + 1), :], in_=c.PANB[:, :])],
                    reads=[("PANB",)], writes=[("xm", jo)] +
                    [("xm2", jo, sc_) for sc_ in range(4)],
                    dma="st", n_dma=1, tag=f"b_sx{hci}_{jo}")


def _rope_norm(c, g, t, j, psq_name, dst, dkeys, tagp):
    ps = c.PS_A
    qbf = c.EBUF[:, 0:S]
    g.step("act", lambda eng: eng.copy(qbf, ps[:, :]),
           reads=keyspan("PS_A", 0, 4), writes=keyspan("EBUF", 0, 4),
           tag=f"{tagp}_qb{j}")

    def permm(eng):
        last = None
        for ch in range(4):
            last = eng.matmul(c.PS_B[:, 512 * ch:512 * ch + 512],
                              c.PERM[:, :], qbf[:, 512 * ch:512 * ch + 512],
                              start=True, stop=True)
        return last
    g.step("pe", permm, reads=keyspan("EBUF", 0, 4) + [("perm64",)],
           writes=keyspan("PS_B", 0, 4), tag=f"{tagp}_pm{j}")
    x0 = c.XTI[:, 0:S]
    x1 = c.XTI[:, S:2 * S]
    g.step("dve", lambda eng: eng.tensor_mul(x0, c.PS_B[:, :], c.PANB[:, :]),
           reads=keyspan("PS_B", 0, 4) + [("PANB",)], writes=[("XTI", 0)],
           tag=f"{tagp}_t1{j}")
    g.step("dve", lambda eng: eng.tensor_mul(x1, qbf, c.PANA[:, :]),
           reads=keyspan("EBUF", 0, 4) + [("PANA",)], writes=[("XTI", 1)],
           tag=f"{tagp}_t2{j}")
    g.step("dve", lambda eng: eng.tensor_add(x0, x0, x1),
           reads=[("XTI", 0), ("XTI", 1)], writes=[("XTI", 0)],
           tag=f"{tagp}_rp{j}")
    sq = c.EBUF[:, S:2 * S]
    g.step("act", lambda eng: eng.activation(sq, x0, AF.Square),
           reads=[("XTI", 0)], writes=keyspan("EBUF", 4, 4), tag=f"{tagp}_sq{j}")

    def ssqmm(eng):
        last = None
        for ch in range(4):
            last = eng.matmul(c.PS_B[0:1, 512 * ch:512 * ch + 512],
                              c.ONE[:, 0:1], sq[:, 512 * ch:512 * ch + 512],
                              start=True, stop=True)
        return last
    g.step("pe", ssqmm, reads=keyspan("EBUF", 4, 4) + [("ONE",)],
           writes=keyspan("PS_B", 0, 4), tag=f"{tagp}_sm{j}")
    row = x1[0:1, 0:S]
    g.step("act", lambda eng: eng.activation(
        row, c.PS_B[0:1, :], AF.Ln, scale=1.0 / 128.0,
        bias=c.BCOL[0:1, 31:32]),
        reads=keyspan("PS_B", 0, 4) + [("BCOL", 30)], writes=[("XTI", 1)],
        tag=f"{tagp}_ln{j}")
    g.step("act", lambda eng: eng.activation(row, row, AF.Exp, scale=-0.5),
           reads=[("XTI", 1)], writes=[("XTI", 1)], tag=f"{tagp}_ex{j}")

    g.step("sp", lambda eng: [
        eng.dma_start(out=t["rows_d"][6:7, :], in_=row)],
        reads=[("XTI", 1)], writes=[("rows_d", 6)], dma="st", n_dma=1,
        tag=f"{tagp}_rs{j}")
    g.step("sp", lambda eng: [
        eng.dma_start(out=x1, in_=t["rows_d"][6:7, :].partition_broadcast(128))],
        reads=[("rows_d", 6)], writes=[("XTI", 1)], dma="ld", n_dma=1,
        tag=f"{tagp}_bc{j}")
    g.step("dve", lambda eng: eng.tensor_mul(dst, x0, x1),
           reads=[("XTI", 0), ("XTI", 1)], writes=dkeys,
           tag=f"{tagp}_no{j}")


def build_l1_attn(c, t, xsrc):
    nc, g = c.nc, c.g
    xm, qT_d = t["xm"], t["qT_d"]
    g.step("sp", lambda eng: [
        eng.dma_start(out=c.PANA[:, :], in_=t["l1_c2T"][:, :]),
        eng.dma_start(out=c.PANB[:, :], in_=t["l1_s2T"][:, :])],
        reads=[("PANA",), ("PANB",)], writes=[("PANA",), ("PANB",)],
        dma="ld", n_dma=2, tag="c_rp")
    g.step("pool", lambda eng: [
        eng.dma_start(out=c.PERM[:, :], in_=t["l1_perm64"][:, :]),
        eng.dma_start(out=c.ONES1F[:, :], in_=t["l1_ones1f"][:, :])],
        writes=[("perm64",), ("ones1f",)], dma="wt", n_dma=2,
        tag="c_pml")

    cslots = [(c.WBUF[:, 0:S], keyspan("WBUF", 0, 4)),
              (c.WBUF[:, S:2 * S], keyspan("WBUF", 4, 4)),
              (c.EBUF[:, 3 * S:4 * S], keyspan("EBUF", 12, 4))]
    for j in range(20):
        wtile, kw = cslots[j % 3]
        if j < 16:
            wsrc = t["l1_wq"].rearrange("(a p) (j f) -> j p a f", p=128, f=128)[j]
        else:
            wsrc = t["l1_wk"].rearrange("(a p) (j f) -> j p a f", p=128, f=128)[j - 16]
        wt3 = wtile.rearrange("p (a f) -> p a f", f=128)
        g.step("pool", lambda eng, wsrc=wsrc, wt3=wt3: [
            eng.dma_start(out=wt3, in_=wsrc)],
            writes=kw, dma="wt", n_dma=1, tag=f"c_wl{j}")

        def qmm(eng, wtile=wtile):
            last = None
            for d in range(NT):
                lhsT = wtile[:, d * 128:(d + 1) * 128]
                for ch in range(4):
                    last = eng.matmul(c.PS_A[:, 512 * ch:512 * ch + 512], lhsT,
                                      biga(c, d)[:, 512 * ch:512 * ch + 512],
                                      start=(d == 0), stop=(d == NT - 1))
            return last
        g.step("pe", qmm, reads=kw + keyspan("BIGA", 0, 64),
               writes=keyspan("PS_A", 0, 4), tag=f"c_qm{j}")
        if j < 16:
            dst = c.EBUF[:, 2 * S:3 * S]
            dkeys = keyspan("EBUF", 8, 4)
            _rope_norm(c, g, t, j, "PS_A", dst, dkeys, "c_q")
            g.step("sp", lambda eng, j=j, dst=dst: [
                eng.dma_start(out=qT_d[128 * j:128 * (j + 1), :], in_=dst)],
                reads=dkeys, writes=[("qT_d", j)], dma="st", n_dma=1,
                tag=f"c_qs{j}")
        else:
            gkv = j - 16
            dst = c.BIGB[:, gkv * S:(gkv + 1) * S]
            dkeys = keyspan("BIGB", 4 * gkv, 4)
            _rope_norm(c, g, t, j, "PS_A", dst, dkeys, "c_k")

    wsl = 0
    wtile = c.WBUF[:, 0:512]
    kwv = keyspan("WBUF", 0, 1)
    wv3 = c.QCH[:, :].rearrange("p (a f) -> p a f", f=512)
    g.step("pool", lambda eng: [
        eng.dma_start(out=wv3, in_=t["l1_wv"].rearrange("(a p) f -> p a f", p=128))],
        reads=keyspan("QCH", 0, 16), writes=keyspan("QCH", 0, 16),
        dma="wt", n_dma=1, tag="c_wv")
    for sb in range(16):
        bank = sb % 2
        psv = c.PS_B[:, 512 * bank:512 * bank + 512]
        pkv = ("PS_B", bank)

        def vmm(eng, sb=sb, psv=psv):
            last = None
            for d in range(NT):
                last = eng.matmul(psv, biga(c, d)[:, 128 * sb:128 * (sb + 1)],
                                  c.QCH[:, d * 512:(d + 1) * 512],
                                  start=(d == 0), stop=(d == NT - 1))
            return last
        g.step("pe", vmm, reads=keyspan("BIGA", 0, 64) + keyspan("QCH", 0, 16),
               writes=[pkv], tag=f"c_vm{sb}")
        vdst = c.BIGB[:, (16 + sb) * 512:(17 + sb) * 512]
        g.step("act", lambda eng, psv=psv, vdst=vdst: eng.copy(vdst, psv),
               reads=[pkv], writes=[("BIGB", 16 + sb)], tag=f"c_ve{sb}")

    for sc in range(NSC):
        s0 = SC * sc
        nvalid = 4 * (sc + 1)
        qsrc = qT_d.rearrange("(a p) s -> p a s", p=128)[:, :, s0:s0 + SC]
        qdst = c.QCH[:, :].rearrange("p (a s) -> p a s", s=512)
        g.step("sp", lambda eng, qsrc=qsrc, qdst=qdst: [
            eng.dma_start(out=qdst, in_=qsrc)],
            reads=[("qT_d", r) for r in range(16)] + keyspan("QCH", 0, 16),
            writes=keyspan("QCH", 0, 16), dma="ld", n_dma=1, tag=f"c_ql{sc}")
        for h in range(16):
            gkv = h // 4
            qch_h = c.QCH[:, h * SC:(h + 1) * SC]
            for T in range(nvalid):
                bank = T % 4
                psc, pk = c.PS_A[:, 512 * bank:512 * bank + 512], ("PS_A", bank)
                lhsT = c.BIGB[:, gkv * S + 128 * T: gkv * S + 128 * (T + 1)]
                g.step("pe", lambda eng, lhsT=lhsT, psc=psc, qch_h=qch_h:
                       eng.matmul(psc, lhsT, qch_h, start=True, stop=True),
                       reads=keyspan("BIGB", 4 * gkv, 4) + [("QCH", h)],
                       writes=[pk], tag=f"c_s{sc}_{h}_{T}")
                ebt = c.EBUF[:, T * SC:(T + 1) * SC]
                g.step("act", lambda eng, psc=psc, ebt=ebt: eng.activation(
                    ebt, psc, AF.Exp, scale=SCALE),
                    reads=[pk], writes=[("EBUF", T)], tag=f"c_e{sc}_{h}_{T}")
                if T >= 4 * sc:
                    base = -128 * (T - 4 * sc)
                    g.step("pool", lambda eng, ebt=ebt, base=base:
                           eng.affine_select(ebt, ebt, [[1, 512]], ALU.is_ge,
                                             0.0, base=base,
                                             channel_multiplier=-1),
                           reads=[("EBUF", T)], writes=[("EBUF", T)],
                           tag=f"c_m{sc}_{h}_{T}")
            avb = h % 2
            def avmm(eng, gkv=gkv, nvalid=nvalid, avb=avb):
                last = None
                for T in range(nvalid):
                    lhsT = c.BIGB[:, (16 + T) * 512 + 128 * gkv:
                                  (16 + T) * 512 + 128 * (gkv + 1)]
                    last = eng.matmul(c.PS_B[:, 512 * avb:512 * avb + 512], lhsT,
                                      c.EBUF[:, T * SC:(T + 1) * SC],
                                      start=(T == 0), stop=(T == nvalid - 1))
                return last
            g.step("pe", avmm,
                   reads=[("BIGB", 16 + T) for T in range(nvalid)] +
                   keyspan("EBUF", 0, nvalid),
                   writes=[("PS_B", avb)], tag=f"c_av{sc}_{h}")

            def dnmm(eng, nvalid=nvalid):
                last = None
                for T in range(nvalid):
                    last = eng.matmul(c.PS_B[0:1, 1024:1536], c.ONE[:, 0:1],
                                      c.EBUF[:, T * SC:(T + 1) * SC],
                                      start=(T == 0), stop=(T == nvalid - 1))
                return last
            g.step("pe", dnmm, reads=keyspan("EBUF", 0, nvalid) + [("ONE",)],
                   writes=[("PS_B", 2)], tag=f"c_d{sc}_{h}")
            rrow = c.XTI[0:1, 0:512]
            g.step("dve", lambda eng, rrow=rrow: eng.reciprocal(
                rrow, c.PS_B[0:1, 1024:1536]),
                reads=[("PS_B", 2)], writes=[("XTI", 0)], tag=f"c_r{sc}_{h}")
            g.step("sp", lambda eng, rrow=rrow: [
                eng.dma_start(out=t["rows_d"][6:7, 0:512], in_=rrow)],
                reads=[("XTI", 0)], writes=[("rows_d", 6)], dma="st", n_dma=1,
                tag=f"c_rs{sc}_{h}")
            camh = c.CAM[:, 256 * avb:256 * avb + 256]
            pan = c.CAM[:, :] if avb == 0 else c.PANB[:, 0:512]
            pank = ("CAM",) if avb == 0 else ("PANB",)
            g.step("sp", lambda eng, pan=pan: [
                eng.dma_start(out=pan,
                              in_=t["rows_d"][6:7, 0:512].partition_broadcast(128))],
                reads=[("rows_d", 6), pank], writes=[pank], dma="ld", n_dma=1,
                tag=f"c_rb{sc}_{h}")
            g.step("dve", lambda eng, qch_h=qch_h, avb=avb, pan=pan: eng.tensor_mul(
                qch_h, c.PS_B[:, 512 * avb:512 * avb + 512], pan),
                reads=[("PS_B", avb), pank], writes=[("QCH", h)],
                tag=f"c_cn{sc}_{h}")

        cpslots = [(c.WBUF[:, S:2 * S], keyspan("WBUF", 4, 4)),
                   (biga(c, 0), keyspan("BIGA", 0, 4)),
                   (biga(c, 1), keyspan("BIGA", 4, 4)),
                   (biga(c, 2), keyspan("BIGA", 8, 4))]
        for jo in range(NT):
            wtile, kw = cpslots[jo % 4]
            src = t["l1_wo"].rearrange("(a p) (j f) -> j p a f", p=128, f=128)[jo]
            wt3 = wtile.rearrange("p (a f) -> p a f", f=128)
            g.step("pool", lambda eng, src=src, wt3=wt3: [
                eng.dma_start(out=wt3, in_=src)],
                writes=kw, dma="wt", n_dma=1, tag=f"c_pw{sc}_{jo}")
            psc, pk = c.PS_A[:, 512 * (jo % 2):512 * (jo % 2) + 512], ("PS_A", jo % 2)

            def pmm(eng, wtile=wtile, psc=psc):
                last = None
                for e in range(NT):
                    last = eng.matmul(psc, wtile[:, e * 128:(e + 1) * 128],
                                      c.QCH[:, e * SC:(e + 1) * SC],
                                      start=(e == 0), stop=(e == NT - 1))
                return last
            g.step("pe", pmm, reads=kw + keyspan("QCH", 0, 16),
                   writes=[pk], tag=f"c_pm{sc}_{jo}")
            xsl = jo % 2
            xt = c.XTI[:, xsl * S:xsl * S + 512]
            g.step("sp", lambda eng, jo=jo, xt=xt, s0=s0: [
                eng.dma_start(out=xt, in_=xsrc[128 * jo:128 * (jo + 1), s0:s0 + SC])],
                reads=[("xm2", jo, sc)], writes=[("XTI", xsl)], dma="ld", n_dma=1,
                tag=f"c_rx{sc}_{jo}")
            g.step("dve", lambda eng, xt=xt, psc=psc: eng.tensor_add(xt, xt, psc),
                   reads=[pk, ("XTI", xsl)], writes=[("XTI", xsl)],
                   tag=f"c_ra{sc}_{jo}")
            g.step("sp", lambda eng, jo=jo, xt=xt, s0=s0: [
                eng.dma_start(out=xm[128 * jo:128 * (jo + 1), s0:s0 + SC], in_=xt)],
                reads=[("XTI", xsl)], writes=[("xm2", jo, sc), ("xm", jo)],
                dma="st", n_dma=1, tag=f"c_rs{sc}_{jo}")



def _transpose_pair_group(c, g, pe_srcs, ps, pskeys, rkeys, tag):
    def fn(eng):
        last = None
        n = len(pe_srcs)
        for idx, (lhsT, off) in enumerate(pe_srcs):
            last = eng.matmul(ps[:, off:off + 128], lhsT, c.IDN[:, :],
                              start=(idx == 0), stop=(idx == n - 1))
        return last
    g.step("pe", fn, reads=rkeys + [("IDN",)], writes=pskeys, tag=tag)




def _dbg_dump(c, t, name, src_ap, rkeys, idx, cast_f32=True):
    if name not in t:
        return
    g = c.g
    g.step("sp", lambda eng: [eng.dma_start(out=t[name], in_=src_ap)],
           reads=rkeys, writes=[("dbg", idx)], dma="st", n_dma=1,
           tag=f"dbg_{name}")

def build_l1_ffn(c, t, xsrc):
    nc, g = c.nc, c.g
    yln, xm, outT = t["yln"], t["xm"], t["outT"]
    rows_d = t["rows_d"]
    EPS2 = ("BCOL", 30)

    def U(arena, u, n=1):
        return arena[:, 512 * u:512 * (u + n)]

    QCHU = lambda u, n=1: U(c.QCH, u, n)
    EBU = lambda u, n=1: U(c.EBUF, u, n)

    def BB(u, n=1):
        return c.BIGB[:, 512 * u:512 * (u + n)]

    def psel_f32(T):
        return BB(T).bitcast(F32)

    g.step("pool", lambda eng: [
        eng.dma_start(out=c.IDN[:, :], in_=t["l2_idn"][:, :]),
        eng.dma_start(out=c.SUT[:, :], in_=t["l2_sut"][:, :]),
        eng.dma_start(out=c.ONS[:, :], in_=t["l2_ons"][:, :]),
        eng.dma_start(out=c.SCOL[:, 0:16], in_=t["l2_wr"][:, :]),
        eng.dma_start(out=c.SCOL[:, 16:32], in_=t["l2_bq"][:, :]),
        eng.dma_start(out=c.SCOL[:, 32:48], in_=t["l2_bkv"][:, :]),
        eng.dma_start(out=c.SCOL[:, 48:64], in_=t["l2_ln2c"][:, :])],
        writes=[("IDN",), ("SUT",), ("ONS",), ("SCOL",)], dma="wt", n_dma=7,
        tag="d_cl")
    g.step("dve", lambda eng: eng.tensor_copy(c.MCOL[:, 0:16], c.SCOL[:, 0:16]),
           reads=[("SCOL",)], writes=[("MCOL",)], tag="d_wrb")
    g.step("sp", lambda eng: [
        eng.dma_start(out=c.CAM[:, 256:512],
                      in_=t["l2_iota256"][0:1, :].partition_broadcast(128))],
        reads=[("CAM",)], writes=[("CAM",)], dma="ld", n_dma=1, tag="d_io")

    def rowmm(eng):
        last = None
        for d in range(NT):
            for ch in range(4):
                last = eng.matmul(c.PS_B[0:1, 512 * ch:512 * ch + 512],
                                  c.MCOL[:, d:d + 1],
                                  biga(c, d)[:, 512 * ch:512 * ch + 512],
                                  start=(d == 0), stop=(d == NT - 1))
        return last
    g.step("pe", rowmm, reads=keyspan("BIGA", 0, 64) + [("MCOL",)],
           writes=keyspan("PS_B", 0, 4), tag="d_srm")
    srow = c.XTI[0:1, 0:2048]
    g.step("dve", lambda eng: eng.tensor_copy(srow, c.PS_B[0:1, :]),
           reads=keyspan("PS_B", 0, 4), writes=[("XTI", 0)], tag="d_sre")
    g.step("sp", lambda eng: [
        eng.dma_start(out=rows_d[4:5, :], in_=srow)],
        reads=[("XTI", 0)], writes=[("rows_d", 4)], dma="st", n_dma=1, tag="d_srs")
    g.step("sp", lambda eng: [
        eng.dma_start(out=c.PANA[:, :],
                      in_=rows_d[4:5, :].partition_broadcast(128))],
        reads=[("rows_d", 4), ("PANA",)], writes=[("PANA",)], dma="ld",
        n_dma=1, tag="d_srb")

    def colmm(eng):
        last = None
        for d in range(NT):
            for T in range(NT):
                last = eng.matmul(c.PS_A[:, T:T + 1],
                                  biga(c, d)[:, 128 * T:128 * (T + 1)],
                                  c.MCOL[:, d:d + 1],
                                  start=(d == 0 and T == 0),
                                  stop=(d == NT - 1 and T == NT - 1))
        return last
    g.step("pe", colmm, reads=keyspan("BIGA", 0, 64) + [("MCOL",)],
           writes=[("PS_A", 0)], tag="d_scm")
    g.step("dve", lambda eng: eng.tensor_copy(c.SCOL[:, 64:80], c.PS_A[:, 0:16]),
           reads=[("PS_A", 0)], writes=[("SCOL",)], tag="d_sce")

    trash = c.XTI[:, S:2 * S]
    for T in range(NT):
        g.step("dve", lambda eng, T=T: eng.tensor_scalar(
            trash, c.PANA[:, :], c.SCOL[:, 64 + T:65 + T], 0.0, ALU.is_gt,
            ALU.add, accum_out=c.SCOL[:, 80 + T:81 + T]),
            reads=[("PANA",), ("SCOL",)], writes=[("XTI", 1), ("SCOL",)],
            tag=f"d_rk{T}")
    g.step("dve", lambda eng: eng.tensor_scalar(
        c.SCOL[:, 96:112], c.SCOL[:, 80:96], 256.0, None, ALU.is_lt),
        reads=[("SCOL",)], writes=[("SCOL",)], tag="d_msk")
    g.step("dve", lambda eng: eng.tensor_copy(c.MCOL[:, 0:16], c.SCOL[:, 96:112]),
           reads=[("SCOL",)], writes=[("MCOL",)], tag="d_mbf")

    def cummm(eng):
        last = None
        for T in range(NT):
            for Uu in range(T + 1):
                lhsT = c.SUT[:, :] if Uu == T else c.ONS[:, :]
                last = eng.matmul(c.PS_B[:, T:T + 1], lhsT, c.MCOL[:, Uu:Uu + 1],
                                  start=(T == 0 and Uu == 0),
                                  stop=(T == NT - 1 and Uu == NT - 1))
        return last
    g.step("pe", cummm, reads=[("MCOL",), ("SUT",), ("ONS",)],
           writes=[("PS_B", 0)], tag="d_cum")
    g.step("dve", lambda eng: eng.tensor_copy(c.SCOL[:, 112:128], c.PS_B[:, 0:16]),
           reads=[("PS_B", 0)], writes=[("SCOL",)], tag="d_cue")
    _dbg_dump(c, t, "dbg_scol", c.SCOL[:, :], [("SCOL",)], 0)

    for T in range(NT):
        g.step("dve", lambda eng, T=T: eng.tensor_scalar(
            psel_f32(T), c.CAM[:, 256:512], c.SCOL[:, 112 + T:113 + T],
            c.SCOL[:, 96 + T:97 + T], ALU.is_equal, ALU.mult),
            reads=[("CAM",), ("SCOL",)], writes=[("BIGB", T)], tag=f"d_ps{T}")
        g.step("dve", lambda eng, T=T: eng.tensor_copy(BB(16 + T)[:, 0:256],
                                                       psel_f32(T)),
               reads=[("BIGB", T)], writes=[("BIGB", 16 + T)], tag=f"d_pb{T}")
        g.step("sp", lambda eng, T=T: [
            eng.dma_start(out=t["psel_d"][128 * T:128 * (T + 1), :],
                          in_=BB(16 + T)[:, 0:256])],
            reads=[("BIGB", 16 + T)], writes=[("psel_d", T)], dma="st", n_dma=1,
            tag=f"d_pss{T}")

    ct3 = c.PANB[:, :].rearrange("p (a f) -> p a f", f=128)
    g.step("sp", lambda eng: [
        eng.dma_start(out=ct3,
                      in_=t["l2_ctabd"].rearrange("(a p) f -> p a f", p=128))],
        reads=[("PANB",)], writes=[("PANB",)], dma="ld", n_dma=1, tag="d_ct")
    st3 = c.XTI[:, S:2 * S].rearrange("p (a f) -> p a f", f=128)
    g.step("sp", lambda eng: [
        eng.dma_start(out=st3,
                      in_=t["l2_stabd"].rearrange("(a p) f -> p a f", p=128))],
        reads=[("XTI", 1)], writes=[("XTI", 1)], dma="ld", n_dma=1, tag="d_st")

    def cgmm(eng):
        last = None
        for T in range(NT):
            last = eng.matmul(c.PS_A[:, 0:256], c.PANB[:, 128 * T:128 * (T + 1)],
                              psel_f32(T), start=(T == 0), stop=(T == NT - 1))
        return last
    g.step("pe", cgmm, reads=[("PANB",)] + keyspan("BIGB", 0, 16),
           writes=[("PS_A", 0)], tag="d_cg")

    def sgmm(eng):
        last = None
        for T in range(NT):
            last = eng.matmul(c.PS_A[:, 512:768],
                              c.XTI[:, S + 128 * T:S + 128 * (T + 1)],
                              psel_f32(T), start=(T == 0), stop=(T == NT - 1))
        return last
    g.step("pe", sgmm, reads=[("XTI", 1)] + keyspan("BIGB", 0, 16),
           writes=[("PS_A", 1)], tag="d_sg")
    g.step("dve", lambda eng: eng.tensor_copy(c.CAM[:, 0:256], c.PS_A[:, 0:256]),
           reads=[("PS_A", 0)], writes=[("CAM",)], tag="d_cge")
    g.step("dve", lambda eng: eng.tensor_copy(c.CAM[:, 256:512], c.PS_A[:, 512:768]),
           reads=[("PS_A", 1), ("CAM",)], writes=[("CAM",)], tag="d_sge")

    def ssmm(eng):
        last = None
        for T in range(NT):
            last = eng.matmul(c.PS_B[0:1, 0:256], c.SCOL[:, 64 + T:65 + T],
                              psel_f32(T), start=(T == 0), stop=(T == NT - 1))
        return last
    g.step("pe", ssmm, reads=[("SCOL",)] + keyspan("BIGB", 0, 16),
           writes=[("PS_B", 0)], tag="d_ss")
    gprow = c.XTI[0:1, 0:256]
    g.step("act", lambda eng: eng.activation(gprow, c.PS_B[0:1, 0:256], AF.Sigmoid),
           reads=[("PS_B", 0)], writes=[("XTI", 0)], tag="d_gp")
    g.step("sp", lambda eng: [
        eng.dma_start(out=rows_d[5:6, 0:256], in_=gprow)],
        reads=[("XTI", 0)], writes=[("rows_d", 5)], dma="st", n_dma=1, tag="d_gps")

    for T in range(NT):
        g.step("sp", lambda eng, T=T: [
            eng.dma_start(out=biga(c, T), in_=yln[0:2048, 128 * T:128 * (T + 1)],
                          transpose=True)],
            reads=[("yd", i) for i in range(NT)] + keyspan("BIGA", 4 * T, 4),
            writes=keyspan("BIGA", 4 * T, 4), dma="tp", n_dma=1, tag=f"d_yt{T}")

    for pc in range(2):
        for dch in range(4):
            psx, pkx = c.PS_B[:, 512 * (dch % 2):512 * (dch % 2) + 512], ("PS_B", dch % 2)

            def gmm(eng, pc=pc, dch=dch, psx=psx):
                last = None
                for T in range(NT):
                    last = eng.matmul(psx, BB(16 + T)[:, 128 * pc:128 * (pc + 1)],
                                      biga(c, T)[:, 512 * dch:512 * (dch + 1)],
                                      start=(T == 0), stop=(T == NT - 1))
                return last
            g.step("pe", gmm,
                   reads=keyspan("BIGB", 16, 16) + keyspan("BIGA", 0, 64),
                   writes=[pkx], tag=f"d_gm{pc}_{dch}")
            dst = QCHU(4 * pc + dch)
            g.step("act", lambda eng, psx=psx, dst=dst: eng.copy(dst, psx),
                   reads=[pkx], writes=[("QCH", 4 * pc + dch)], tag=f"d_ge{pc}_{dch}")

    if "dbg_sel" in t:
        g.step("sp", lambda eng: [
            eng.dma_start(out=t["dbg_sel"][0:128, :], in_=QCHU(0, 4)),
            eng.dma_start(out=t["dbg_sel"][128:256, :], in_=QCHU(4, 4))],
            reads=keyspan("QCH", 0, 8), writes=[("dbg", 1)], dma="st", n_dma=2,
            tag="dbg_sel")

    g.step("sp", lambda eng: [
        eng.dma_start(out=c.PANB[:, :],
                      in_=t["l2_ln1r"][0:1, :].partition_broadcast(128))],
        reads=[("PANB",)], writes=[("PANB",)], dma="ld", n_dma=1, tag="d_l1p")
    for pc in range(2):
        stile = QCHU(4 * pc, 4)
        scr = EBU(0, 4)
        g.step("act", lambda eng, stile=stile, scr=scr, pc=pc: eng.activation(
            scr, stile, AF.Square, accum_out=c.SCOL[:, 128 + pc:129 + pc]),
            reads=keyspan("QCH", 4 * pc, 4), writes=keyspan("EBUF", 0, 4) + [("SCOL",)],
            tag=f"d_hs{pc}")
        g.step("act", lambda eng, pc=pc: eng.activation(
            c.SCOL[:, 130 + pc:131 + pc], c.SCOL[:, 128 + pc:129 + pc], AF.Ln,
            scale=1.0 / D, bias=c.BCOL[0:128, 31:32]),
            reads=[("SCOL",), EPS2], writes=[("SCOL",)], tag=f"d_hl{pc}")
        g.step("act", lambda eng, pc=pc: eng.activation(
            c.SCOL[:, 130 + pc:131 + pc], c.SCOL[:, 130 + pc:131 + pc],
            AF.Exp, scale=-0.5),
            reads=[("SCOL",)], writes=[("SCOL",)], tag=f"d_he{pc}")
        tmp = c.XTI[:, S:2 * S]
        g.step("dve", lambda eng, stile=stile, tmp=tmp, pc=pc: eng.tensor_scalar(
            tmp, stile, c.SCOL[:, 130 + pc:131 + pc], None, ALU.mult),
            reads=keyspan("QCH", 4 * pc, 4) + [("SCOL",)], writes=[("XTI", 1)],
            tag=f"d_hm{pc}")
        htile = QCHU(8 + 4 * pc, 4)
        g.step("dve", lambda eng, tmp=tmp, htile=htile: eng.tensor_mul(
            htile, tmp, c.PANB[:, :]),
            reads=[("XTI", 1), ("PANB",)], writes=keyspan("QCH", 8 + 4 * pc, 4),
            tag=f"d_hw{pc}")

    for i2 in range(8):
        srcs = []
        rkeys = []
        for sub in range(2):
            i = 2 * i2 + sub
            for pc in range(2):
                lhsT = QCHU(8 + 4 * pc, 4)[:, 128 * i:128 * (i + 1)]
                srcs.append((lhsT, 256 * sub + 128 * pc))
                rkeys.append(("QCH", 8 + 4 * pc + i // 4))
        ps, pk = c.PS_A[:, 0:512], ("PS_A", 0)
        _transpose_pair_group(c, g, srcs, ps, [pk], list(set(rkeys)), f"d_ht{i2}")
        dst = EBU(i2)
        g.step("dve", lambda eng, ps=ps, dst=dst: eng.tensor_copy(dst, ps),
               reads=[pk], writes=[("EBUF", i2)], tag=f"d_hte{i2}")

    def hT(i):
        return EBU(i // 2)[:, 256 * (i % 2):256 * (i % 2) + 256]

    xti_bf = c.XTI[:, 0:S].bitcast(BF16)
    dslots = [(c.WBUF[:, 0:S], keyspan("WBUF", 0, 4)),
              (c.WBUF[:, S:2 * S], keyspan("WBUF", 4, 4)),
              (xti_bf[:, 0:2048], [("XTI", 0)]),
              (xti_bf[:, 2048:4096], [("XTI", 0)])]
    for j in range(32):
        wtile, kw = dslots[j % 4]
        if j < 16:
            wsrc = t["l2_wq"].rearrange("(a p) (j f) -> j p a f", p=128, f=128)[j]
        elif j < 24:
            wsrc = t["l2_wk"].rearrange("(a p) (j f) -> j p a f", p=128, f=128)[j - 16]
        else:
            wsrc = t["l2_wv"].rearrange("(a p) (j f) -> j p a f", p=128, f=128)[j - 24]
        wt3 = wtile.rearrange("p (a f) -> p a f", f=128)
        g.step("pool", lambda eng, wsrc=wsrc, wt3=wt3: [
            eng.dma_start(out=wt3, in_=wsrc)],
            writes=kw, dma="wt", n_dma=1, tag=f"d_wl{j}")
        bank = j % 4
        psq, pkq = c.PS_A[:, 512 * bank:512 * bank + 256], ("PS_A", bank)

        def qmm(eng, wtile=wtile, psq=psq):
            last = None
            for d in range(NT):
                last = eng.matmul(psq, wtile[:, 128 * d:128 * (d + 1)], hT(d),
                                  start=(d == 0), stop=(d == NT - 1))
            return last
        g.step("pe", qmm, reads=kw + keyspan("EBUF", 0, 8), writes=[pkq],
               tag=f"d_qm{j}")
        if j < 16:
            bcol = c.SCOL[:, 16 + j:17 + j]
        elif j < 24:
            bcol = c.SCOL[:, 32 + (j - 16):33 + (j - 16)]
        else:
            bcol = c.SCOL[:, 40 + (j - 24):41 + (j - 24)]
        qbf = c.QBF[:, 256 * (j % 2):256 * (j % 2) + 256]
        qbfk = ("QBF", j % 2)
        g.step("act", lambda eng, psq=psq, qbf=qbf, bcol=bcol: eng.activation(
            qbf, psq, AF.Identity, bias=bcol),
            reads=[pkq, ("SCOL",)], writes=[qbfk],
            tag=f"d_qb{j}")
        if j < 24:
            psw, pkw = c.PS_B[:, 512 * (bank % 2):512 * (bank % 2) + 256], ("PS_B", bank % 2)
            g.step("pe", lambda eng, qbf=qbf, psw=psw: eng.matmul(
                psw, c.PERM[:, :], qbf, start=True, stop=True),
                reads=[qbfk, ("IDN",), ("perm64",)], writes=[pkw],
                tag=f"d_qp{j}")
            t1 = c.XTI[:, S + 512 * bank:S + 512 * bank + 256]
            g.step("dve", lambda eng, psw=psw, t1=t1: eng.tensor_mul(
                t1, psw, c.CAM[:, 256:512]),
                reads=[pkw, ("CAM",)], writes=[("XTIr", bank)], tag=f"d_q1{j}")
            t2 = c.XTI[:, S + 512 * bank + 256:S + 512 * bank + 512]
            g.step("dve", lambda eng, qbf=qbf, t2=t2: eng.tensor_mul(
                t2, qbf, c.CAM[:, 0:256]),
                reads=[qbfk, ("CAM",)], writes=[("XTIr", bank)],
                tag=f"d_q2{j}")
            if j < 16:
                dst = EBU(8 + j // 2)[:, 256 * (j % 2):256 * (j % 2) + 256]
                dkey = ("EBUF", 8 + j // 2)
            else:
                jk = j - 16
                dst = QCHU(8 + jk // 2)[:, 256 * (jk % 2):256 * (jk % 2) + 256]
                dkey = ("QCH", 8 + jk // 2)
            g.step("dve", lambda eng, t1=t1, t2=t2, dst=dst: eng.tensor_add(
                dst, t1, t2),
                reads=[("XTIr", bank)], writes=[dkey], tag=f"d_qr{j}")
        else:
            jv = j - 24
            dst = QCHU(12 + jv // 2)[:, 256 * (jv % 2):256 * (jv % 2) + 256]
            g.step("dve", lambda eng, qbf=qbf, dst=dst: eng.tensor_copy(dst, qbf),
                   reads=[qbfk], writes=[("QCH", 12 + jv // 2)],
                   tag=f"d_vc{j}")

    if "dbg_q" in t:
        def qT_(h):
            return c.EBUF[:, 512 * (8 + h // 2) + 256 * (h % 2):
                          512 * (8 + h // 2) + 256 * (h % 2) + 256]
        g.step("sp", lambda eng: [
            eng.dma_start(out=t["dbg_q"][128 * h:128 * (h + 1), :], in_=qT_(h))
            for h in range(16)],
            reads=[("EBUF", 8 + u) for u in range(8)], writes=[("dbg", 2)],
            dma="st", n_dma=16, tag="dbg_q")
    build_l1_ffn_tail(c, t, xsrc)


def build_l1_ffn_tail(c, t, xsrc):
    nc, g = c.nc, c.g
    yln, xm, outT, rows_d = t["yln"], t["xm"], t["outT"], t["rows_d"]

    def BB(u, n=1):
        return c.BIGB[:, 512 * u:512 * (u + n)]

    QCHU = lambda u, n=1: c.QCH[:, 512 * u:512 * (u + n)]
    EBU = lambda u, n=1: c.EBUF[:, 512 * u:512 * (u + n)]

    def qT(h):
        return EBU(8 + h // 2)[:, 256 * (h % 2):256 * (h % 2) + 256]

    for pc in range(2):
        for half in range(2):
            srcs, rkeys = [], []
            for sub in range(4):
                jv = 4 * half + sub
                lhsT = QCHU(12 + jv // 2)[:, 256 * (jv % 2) + 128 * pc:
                                          256 * (jv % 2) + 128 * (pc + 1)]
                srcs.append((lhsT, 128 * sub))
                rkeys.append(("QCH", 12 + jv // 2))
            ps, pk = c.PS_A[:, 1024:1536], ("PS_A", 2)
            _transpose_pair_group(c, g, srcs, ps, [pk], list(set(rkeys)),
                                  f"d_vt{pc}_{half}")
            vst = c.QBF[:, 0:512]
            g.step("dve", lambda eng, ps=ps, vst=vst: eng.tensor_copy(vst, ps),
                   reads=[pk], writes=[("QBF", 0), ("QBF", 1)],
                   tag=f"d_ve{pc}_{half}")
            g.step("sp", lambda eng, pc=pc, half=half, vst=vst: [
                eng.dma_start(out=t["vin"][128 * pc:128 * (pc + 1),
                                           512 * half:512 * (half + 1)], in_=vst)],
                reads=[("QBF", 0), ("QBF", 1)], writes=[("vin", pc, half)],
                dma="st", n_dma=1, tag=f"d_vs{pc}_{half}")

    for jk in range(8):
        ksrc = QCHU(8 + jk // 2)[:, 256 * (jk % 2):256 * (jk % 2) + 256]
        g.step("sp", lambda eng, jk=jk, ksrc=ksrc: [
            eng.dma_start(out=t["kin"][128 * jk:128 * (jk + 1), :], in_=ksrc)],
            reads=[("QCH", 8 + jk // 2)], writes=[("kin", jk)], dma="st", n_dma=1,
            tag=f"d_ks{jk}")

    g.step("pool", lambda eng: [eng.collective_compute(
        "AllGather", ALU.bypass, replica_groups=[list(range(8))],
        ins=[t["kin"].opt()],
        outs=[t["kout"].opt()])],
        reads=[("kin", jk) for jk in range(8)], writes=[("kout",)],
        dma="cc", n_dma=1, tag="d_agk")
    g.step("pool", lambda eng: [eng.collective_compute(
        "AllGather", ALU.bypass, replica_groups=[list(range(8))],
        ins=[t["vin"].opt()], outs=[t["vout"].opt()])],
        reads=[("vin", pc, hf) for pc in range(2) for hf in range(2)],
        writes=[("vout",)], dma="cc", n_dma=1, tag="d_agv")

    for g2 in range(8):
        dst3 = BB(4 * g2, 4).rearrange("p (r f) -> p r f", f=256)
        src3 = t["kout"].rearrange("(r g e) f -> g e r f", g=8, e=128)[g2]
        g.step("sp", lambda eng, dst3=dst3, src3=src3: [
            eng.dma_start(out=dst3, in_=src3)],
            reads=[("kout",)] + keyspan("BIGB", 4 * g2, 4),
            writes=keyspan("BIGB", 4 * g2, 4), dma="ld", n_dma=1, tag=f"d_kg{g2}")
    for T in range(NT):
        g.step("sp", lambda eng, T=T: [
            eng.dma_start(out=BB(32 + 2 * T, 2),
                          in_=t["vout"][128 * T:128 * (T + 1), :])],
            reads=[("vout",)] + keyspan("BIGB", 32 + 2 * T, 2),
            writes=keyspan("BIGB", 32 + 2 * T, 2), dma="ld", n_dma=1,
            tag=f"d_vg{T}")
    mview = c.XTI[:, 0:S].bitcast(BF16)
    mview3 = mview.rearrange("p (a f) -> p a f", f=256)
    g.step("sp", lambda eng: [
        eng.dma_start(out=mview3, in_=t["l2_mask"].rearrange(
            "(a p) f -> p a f", p=128))],
        reads=[("XTI", 0)], writes=[("XTI", 0)], dma="ld", n_dma=1, tag="d_mk")

    def mtile(T):
        return mview[:, 256 * T:256 * (T + 1)]

    for h in range(16):
        g2 = h // 2
        for gw in range(2):
            psw_t = c.PS_A if gw == 0 else c.PS_B
            pswk = keyspan("PS_A" if gw == 0 else "PS_B", 0, 4)

            def dsmm(eng, g2=g2, h=h, gw=gw, psw_t=psw_t):
                last = None
                for k8 in range(8):
                    T = 8 * gw + k8
                    last = eng.matmul(psw_t[:, 256 * k8:256 * k8 + 256],
                                      BB(4 * g2, 4)[:, 128 * T:128 * (T + 1)],
                                      qT(h), start=(k8 % 2 == 0),
                                      stop=(k8 % 2 == 1))
                return last
            g.step("pe", dsmm,
                   reads=keyspan("BIGB", 4 * g2, 4) + [("EBUF", 8 + h // 2)],
                   writes=pswk, tag=f"d_as{h}_{gw}")
            ebt = EBU(4 * gw, 4)
            g.step("act", lambda eng, ebt=ebt, psw_t=psw_t: eng.activation(
                ebt, psw_t[:, :], AF.Exp, scale=SCALE),
                reads=pswk, writes=keyspan("EBUF", 4 * gw, 4),
                tag=f"d_ae{h}_{gw}")
            g.step("dve", lambda eng, ebt=ebt, gw=gw: eng.tensor_mul(
                ebt, ebt, mview[:, 2048 * gw:2048 * (gw + 1)]),
                reads=keyspan("EBUF", 4 * gw, 4) + [("XTI", 0)],
                writes=keyspan("EBUF", 4 * gw, 4),
                tag=f"d_am{h}_{gw}")

        def avmm(eng, g2=g2):
            last = None
            for T in range(NT):
                ebt = EBU(T // 2)[:, 256 * (T % 2):256 * (T % 2) + 256]
                lhsT = BB(32 + 2 * T, 2)[:, 128 * g2:128 * (g2 + 1)]
                last = eng.matmul(c.PS_B[:, 1536:1792], lhsT, ebt,
                                  start=(T == 0), stop=(T == NT - 1))
            return last
        g.step("pe", avmm,
               reads=keyspan("EBUF", 0, 8) + keyspan("BIGB", 32, 32),
               writes=[("PS_B", 3)], tag=f"d_aa{h}")

        def dnmm(eng):
            last = None
            for T in range(NT):
                ebt = EBU(T // 2)[:, 256 * (T % 2):256 * (T % 2) + 256]
                last = eng.matmul(c.PS_B[0:1, 1024:1280], c.ONE[:, 0:1], ebt,
                                  start=(T == 0), stop=(T == NT - 1))
            return last
        g.step("pe", dnmm, reads=keyspan("EBUF", 0, 8) + [("ONE",)],
               writes=[("PS_B", 2)], tag=f"d_ad{h}")
        rrow = c.XTI[0:1, S:S + 256]
        g.step("dve", lambda eng, rrow=rrow: eng.reciprocal(
            rrow, c.PS_B[0:1, 1024:1280]),
            reads=[("PS_B", 2)], writes=[("XTI", 1), ("XTIr", 0)], tag=f"d_ar{h}")
        g.step("sp", lambda eng, rrow=rrow: [
            eng.dma_start(out=t["rows_d"][7:8, 0:256], in_=rrow)],
            reads=[("XTI", 1)], writes=[("rows_d", 7)], dma="st", n_dma=1,
            tag=f"d_ars{h}")
        pan = c.CAM[:, 256 * (h % 2):256 * (h % 2) + 256]
        g.step("sp", lambda eng, pan=pan: [
            eng.dma_start(out=pan,
                          in_=t["rows_d"][7:8, 0:256].partition_broadcast(128))],
            reads=[("rows_d", 7), ("CAM",)], writes=[("CAM",)], dma="ld",
            n_dma=1, tag=f"d_ab{h}")
        g.step("dve", lambda eng, h=h, pan=pan: eng.tensor_mul(
            qT(h), c.PS_B[:, 1536:1792], pan),
            reads=[("PS_B", 3), ("CAM",)], writes=[("EBUF", 8 + h // 2)],
            tag=f"d_an{h}")

    if "dbg_ctx" in t:
        g.step("sp", lambda eng: [
            eng.dma_start(out=t["dbg_ctx"][128 * h:128 * (h + 1), :], in_=qT(h))
            for h in range(16)],
            reads=[("EBUF", 8 + u) for u in range(8)], writes=[("dbg", 3)],
            dma="st", n_dma=16, tag="dbg_ctx")

    oslots = [(c.WBUF[:, 0:S], keyspan("WBUF", 0, 4)),
              (c.WBUF[:, S:2 * S], keyspan("WBUF", 4, 4)),
              (BB(0, 4), keyspan("BIGB", 0, 4)),
              (BB(4, 4), keyspan("BIGB", 4, 4))]
    for jo in range(NT):
        wtile, kw = oslots[jo % 4]
        wsrc = t["l2_wo"].rearrange("(a p) (j f) -> j p a f", p=128, f=128)[jo]
        wt3 = wtile.rearrange("p (a f) -> p a f", f=128)
        g.step("pool", lambda eng, wsrc=wsrc, wt3=wt3: [
            eng.dma_start(out=wt3, in_=wsrc)],
            writes=kw, dma="wt", n_dma=1, tag=f"d_ow{jo}")
        bank = jo % 4
        pso, pko = c.PS_A[:, 512 * bank:512 * bank + 256], ("PS_A", bank)

        def omm(eng, wtile=wtile, pso=pso):
            last = None
            for e in range(NT):
                last = eng.matmul(pso, wtile[:, 128 * e:128 * (e + 1)], qT(e),
                                  start=(e == 0), stop=(e == NT - 1))
            return last
        g.step("pe", omm, reads=kw + keyspan("EBUF", 8, 8), writes=[pko],
               tag=f"d_om{jo}")
        adst = QCHU(8 + jo // 2)[:, 256 * (jo % 2):256 * (jo % 2) + 256]
        g.step("act", lambda eng, pso=pso, adst=adst: eng.copy(adst, pso),
               reads=[pko], writes=[("QCH", 8 + jo // 2)], tag=f"d_oe{jo}")

    def attnT(jo):
        return QCHU(8 + jo // 2)[:, 256 * (jo % 2):256 * (jo % 2) + 256]

    if "dbg_attn" in t:
        g.step("sp", lambda eng: [
            eng.dma_start(out=t["dbg_attn"][128 * jo:128 * (jo + 1), :],
                          in_=attnT(jo)) for jo in range(16)],
            reads=[("QCH", 8 + u) for u in range(8)], writes=[("dbg", 4)],
            dma="st", n_dma=16, tag="dbg_attn")

    for jo2 in range(8):
        srcs, rkeys = [], []
        for sub in range(2):
            jo = 2 * jo2 + sub
            for pc in range(2):
                lhsT = QCHU(4 * pc, 4)[:, 128 * jo:128 * (jo + 1)]
                srcs.append((lhsT, 256 * sub + 128 * pc))
                rkeys.append(("QCH", 4 * pc + jo // 4))
        ps, pk = c.PS_B[:, 1024:1536], ("PS_B", 2)
        _transpose_pair_group(c, g, srcs, ps, [pk], list(set(rkeys)),
                              f"d_xt{jo2}")
        for sub in range(2):
            jo = 2 * jo2 + sub
            x1t = c.XTI[:, 256 * jo:256 * (jo + 1)]
            xw = [("XTI", 0)] if jo < 8 else [("XTI", 1), ("XTIr", (jo - 8) // 2)]
            g.step("dve", lambda eng, ps=ps, sub=sub, x1t=x1t, jo=jo:
                   eng.tensor_add(x1t, ps[:, 256 * sub:256 * (sub + 1)],
                                  attnT(jo)),
                   reads=[pk, ("QCH", 8 + jo // 2)],
                   writes=xw,
                   tag=f"d_x1{jo}")

    def x1T(jo):
        return c.XTI[:, 256 * jo:256 * (jo + 1)]

    for jo in range(NT):
        scr = BB(24 + jo // 2)[:, 256 * (jo % 2):256 * (jo % 2) + 256]
        g.step("act", lambda eng, jo=jo, scr=scr: eng.activation(
            scr, x1T(jo), AF.Square),
            reads=[("XTI", 0) if jo < 8 else ("XTI", 1)],
            writes=[("BIGB", 24 + jo // 2)], tag=f"d_2s{jo}")

    def sq2mm(eng):
        last = None
        for jo in range(NT):
            scr = BB(24 + jo // 2)[:, 256 * (jo % 2):256 * (jo % 2) + 256]
            last = eng.matmul(c.PS_B[0:1, 1024:1280], c.ONE[:, 0:1], scr,
                              start=(jo == 0), stop=(jo == NT - 1))
        return last
    g.step("pe", sq2mm, reads=[("BIGB", 24 + u) for u in range(8)] + [("ONE",)],
           writes=[("PS_B", 2)], tag="d_2m")
    r2 = c.CAM[0:1, 256:512]
    g.step("act", lambda eng: eng.activation(
        r2, c.PS_B[0:1, 1024:1280], AF.Ln, scale=1.0 / D,
        bias=c.BCOL[0:1, 31:32]),
        reads=[("PS_B", 2), ("BCOL", 30)], writes=[("CAM",)], tag="d_2l")
    g.step("act", lambda eng: eng.activation(r2, r2, AF.Exp, scale=-0.5),
           reads=[("CAM",)], writes=[("CAM",)], tag="d_2e")
    g.step("sp", lambda eng: [
        eng.dma_start(out=rows_d[7:8, 0:256], in_=r2[0:1, 0:256])],
        reads=[("CAM",)], writes=[("rows_d", 7)], dma="st", n_dma=1, tag="d_2bs")
    g.step("sp", lambda eng: [
        eng.dma_start(out=c.CAM[:, 0:256],
                      in_=rows_d[7:8, 0:256].partition_broadcast(128))],
        reads=[("rows_d", 7), ("CAM",)], writes=[("CAM",)], dma="ld", n_dma=1,
        tag="d_2c")
    for jo in range(NT):
        tmp = c.QBF[:, 0:256]
        g.step("dve", lambda eng, jo=jo, tmp=tmp: eng.tensor_mul(
            tmp, x1T(jo), c.CAM[:, 0:256]),
            reads=[("XTI", 0) if jo < 8 else ("XTI", 1), ("CAM",)],
            writes=[("QBF", 0)], tag=f"d_2t{jo}")
        h2d = EBU(jo // 2)[:, 256 * (jo % 2):256 * (jo % 2) + 256]
        g.step("dve", lambda eng, jo=jo, tmp=tmp, h2d=h2d: eng.tensor_scalar(
            h2d, tmp, c.SCOL[:, 48 + jo:49 + jo], None, ALU.mult),
            reads=[("QBF", 0), ("SCOL",)], writes=[("EBUF", jo // 2)],
            tag=f"d_2w{jo}")

    if "dbg_h2" in t:
        g.step("sp", lambda eng: [
            eng.dma_start(out=t["dbg_h2"][128 * jo:128 * (jo + 1), :],
                          in_=EBU(jo // 2)[:, 256 * (jo % 2):256 * (jo % 2) + 256])
            for jo in range(16)],
            reads=[("EBUF", u) for u in range(8)], writes=[("dbg", 5)],
            dma="st", n_dma=16, tag="dbg_h2")

    def h2T(d):
        return EBU(d // 2)[:, 256 * (d % 2):256 * (d % 2) + 256]

    for jo in range(NT):
        g.step("sp", lambda eng, jo=jo: [
            eng.dma_start(out=t["h2in"][128 * jo:128 * (jo + 1), :], in_=h2T(jo))],
            reads=[("EBUF", jo // 2)], writes=[("h2in", jo)], dma="st", n_dma=1,
            tag=f"d_h2s{jo}")
    g.step("pool", lambda eng: [eng.collective_compute(
        "AllGather", ALU.bypass, replica_groups=[list(range(8))],
        ins=[t["h2in"].opt()], outs=[t["h2out"].opt()])],
        reads=[("h2in", jo) for jo in range(NT)], writes=[("h2out",)],
        dma="cc", n_dma=1, tag="d_agh")
    for dd in range(NT):
        dst3 = biga(c, dd).rearrange("p (r f) -> p r f", f=256)
        src3 = t["h2out"].rearrange("(r a p) f -> a p r f", r=8, p=128)[dd]
        g.step("sp", lambda eng, dst3=dst3, src3=src3: [
            eng.dma_start(out=dst3, in_=src3)],
            reads=[("h2out",)] + keyspan("BIGA", 4 * dd, 4),
            writes=keyspan("BIGA", 4 * dd, 4), dma="ld", n_dma=1,
            tag=f"d_h2l{dd}")

    gslots = [(c.WBUF[:, 0:S], keyspan("WBUF", 0, 4)),
              (BB(0, 4), keyspan("BIGB", 0, 4)),
              (BB(8, 4), keyspan("BIGB", 8, 4))]
    uslots = [(c.WBUF[:, S:2 * S], keyspan("WBUF", 4, 4)),
              (BB(4, 4), keyspan("BIGB", 4, 4)),
              (BB(12, 4), keyspan("BIGB", 12, 4))]
    for il in range(8):
        wtg, kwg = gslots[il % 3]
        gsrc = t["l2_wgs"].rearrange("(a p) (j f) -> j p a f", p=128, f=128)[il]
        wt3 = wtg.rearrange("p (a f) -> p a f", f=128)
        g.step("pool", lambda eng, gsrc=gsrc, wt3=wt3: [
            eng.dma_start(out=wt3, in_=gsrc)],
            writes=kwg, dma="wt", n_dma=1, tag=f"d_gw{il}")
        wtu, kwu = uslots[il % 3]
        usrc = t["l2_wus"].rearrange("(a p) (j f) -> j p a f", p=128, f=128)[il]
        wt3u = wtu.rearrange("p (a f) -> p a f", f=128)
        g.step("pool", lambda eng, usrc=usrc, wt3u=wt3u: [
            eng.dma_start(out=wt3u, in_=usrc)],
            writes=kwu, dma="wt", n_dma=1, tag=f"d_uw{il}")
        psg = c.PS_A if il % 2 == 0 else c.PS_B
        psgn = "PS_A" if il % 2 == 0 else "PS_B"

        def gmm(eng, wtg=wtg, psg=psg):
            last = None
            for d in range(NT):
                lhsT = wtg[:, 128 * d:128 * (d + 1)]
                for ch in range(4):
                    last = eng.matmul(psg[:, 512 * ch:512 * ch + 512], lhsT,
                                      biga(c, d)[:, 512 * ch:512 * ch + 512],
                                      start=(d == 0), stop=(d == NT - 1))
            return last
        g.step("pe", gmm, reads=kwg + keyspan("BIGA", 0, 64),
               writes=keyspan(psgn, 0, 4), tag=f"d_mg{il}")
        sil = c.XTI[:, S:2 * S]
        g.step("act", lambda eng, psg=psg, sil=sil: eng.activation(
            sil, psg[:, :], AF.Silu),
            reads=keyspan(psgn, 0, 4), writes=[("XTI", 1)], tag=f"d_ms{il}")
        psu = c.PS_B if il % 2 == 0 else c.PS_A
        psun = "PS_B" if il % 2 == 0 else "PS_A"

        def umm(eng, wtu=wtu, psu=psu):
            last = None
            for d in range(NT):
                lhsT = wtu[:, 128 * d:128 * (d + 1)]
                for ch in range(4):
                    last = eng.matmul(psu[:, 512 * ch:512 * ch + 512], lhsT,
                                      biga(c, d)[:, 512 * ch:512 * ch + 512],
                                      start=(d == 0), stop=(d == NT - 1))
            return last
        g.step("pe", umm, reads=kwu + keyspan("BIGA", 0, 64),
               writes=keyspan(psun, 0, 4), tag=f"d_mu{il}")
        adst = BB(32 + 4 * il, 4)
        adk = keyspan("BIGB", 32 + 4 * il, 4)
        g.step("dve", lambda eng, psu=psu, sil=sil, adst=adst: eng.tensor_mul(
            adst, psu[:, :], sil),
            reads=keyspan(psun, 0, 4) + [("XTI", 1)], writes=adk,
            tag=f"d_ma{il}")

    def aT(il):
        return BB(32 + 4 * il, 4)

    g.step("sp", lambda eng: [
        eng.dma_start(out=c.CAM[:, 256:512],
                      in_=rows_d[5:6, 0:256].partition_broadcast(128))],
        reads=[("rows_d", 5), ("CAM",)], writes=[("CAM",)], dma="ld", n_dma=1,
        tag="d_gpb")

    wdslots = [(c.WBUF[:, 0:S], keyspan("WBUF", 0, 4)),
               (c.WBUF[:, S:2 * S], keyspan("WBUF", 4, 4)),
               (BB(0, 4), keyspan("BIGB", 0, 4)),
               (BB(4, 4), keyspan("BIGB", 4, 4))]
    for jo in range(NT):
        wtd, kwd = wdslots[jo % 4]
        dsrc = t["l2_wds"].rearrange("(a p) (j f) -> j p a f", p=128, f=128)[jo]
        wt3d = wtd[:, 0:1024].rearrange("p (a f) -> p a f", f=128)
        g.step("pool", lambda eng, dsrc=dsrc, wt3d=wt3d: [
            eng.dma_start(out=wt3d, in_=dsrc)],
            writes=kwd, dma="wt", n_dma=1, tag=f"d_dw{jo}")
        pst = c.PS_A if jo % 2 == 0 else c.PS_B
        pstn = "PS_A" if jo % 2 == 0 else "PS_B"

        def dmm(eng, wtd=wtd, pst=pst):
            last = None
            for a in range(8):
                lhsT = wtd[:, 128 * a:128 * (a + 1)]
                for ch in range(4):
                    last = eng.matmul(pst[:, 512 * ch:512 * ch + 512], lhsT,
                                      aT(a)[:, 512 * ch:512 * ch + 512],
                                      start=(a == 0), stop=(a == 7))
            return last
        g.step("pe", dmm,
               reads=kwd + [("BIGB", 32 + u) for u in range(32)],
               writes=keyspan(pstn, 0, 4), tag=f"d_dm{jo}")
        stg = c.EBUF[:, (jo % 2) * S:(jo % 2) * S + S]
        stk = keyspan("EBUF", 4 * (jo % 2), 4)
        g.step("act", lambda eng, pst=pst, stg=stg: eng.copy(stg, pst[:, :]),
               reads=keyspan(pstn, 0, 4), writes=stk, tag=f"d_de{jo}")
        def dstore(eng, jo=jo, stg=stg):
            ins = []
            for ch in range(4):
                rb = ch // 1
                ins.append(eng.dma_start(
                    out=t["rsin"].rearrange("(b a p) f -> b a p f", b=8, p=128)
                    [2 * ch, jo] if False else
                    t["rsin"][2048 * (2 * ch) + 128 * jo:
                              2048 * (2 * ch) + 128 * (jo + 1), :],
                    in_=stg[:, 512 * ch:512 * ch + 256]))
                ins.append(eng.dma_start(
                    out=t["rsin"][2048 * (2 * ch + 1) + 128 * jo:
                                  2048 * (2 * ch + 1) + 128 * (jo + 1), :],
                    in_=stg[:, 512 * ch + 256:512 * ch + 512]))
            return ins
        g.step("sp", dstore, reads=stk, writes=[("rsin", jo)], dma="st",
               n_dma=8, tag=f"d_ds{jo}")
    g.step("pool", lambda eng: [eng.collective_compute(
        "ReduceScatter", ALU.add, replica_groups=[list(range(8))],
        ins=[t["rsin"].opt()], outs=[t["rsout"].opt()])],
        reads=[("rsin", jo) for jo in range(NT)], writes=[("rsout",)],
        dma="cc", n_dma=1, tag="d_rs")
    for jo in range(NT):
        mdst = BB(24 + jo // 2)[:, 256 * (jo % 2):256 * (jo % 2) + 256]
        g.step("sp", lambda eng, jo=jo, mdst=mdst: [
            eng.dma_start(out=mdst, in_=t["rsout"][128 * jo:128 * (jo + 1), :])],
            reads=[("rsout",)] + [("BIGB", 24 + jo // 2)],
            writes=[("BIGB", 24 + jo // 2)], dma="ld", n_dma=1, tag=f"d_ml{jo}")

    def mlpT(jo):
        return BB(24 + jo // 2)[:, 256 * (jo % 2):256 * (jo % 2) + 256]

    for jo in range(NT):
        tmpd = c.QBF[:, 0:256]
        g.step("dve", lambda eng, jo=jo, tmpd=tmpd: eng.tensor_add(
            tmpd, mlpT(jo), attnT(jo)),
            reads=[("BIGB", 24 + jo // 2), ("QCH", 8 + jo // 2)],
            writes=[("QBF", 0)], tag=f"d_dd{jo}")
        gpd = QCHU(jo // 2)[:, 256 * (jo % 2):256 * (jo % 2) + 256]
        g.step("dve", lambda eng, tmpd=tmpd, gpd=gpd: eng.tensor_mul(
            gpd, tmpd, c.CAM[:, 256:512]),
            reads=[("QBF", 0), ("CAM",)], writes=[("QCH", jo // 2)],
            tag=f"d_dg{jo}")

    def gpDT(jo):
        return QCHU(jo // 2)[:, 256 * (jo % 2):256 * (jo % 2) + 256]

    for pc in range(2):
        for quad in range(4):
            srcs, rkeys = [], []
            for sub in range(4):
                jo = 4 * quad + sub
                lhsT = gpDT(jo)[:, 128 * pc:128 * (pc + 1)]
                srcs.append((lhsT, 128 * sub))
                rkeys.append(("QCH", jo // 2))
            ps, pk = c.PS_A[:, 1024:1536], ("PS_A", 2)
            _transpose_pair_group(c, g, srcs, ps, [pk], list(set(rkeys)),
                                  f"d_gt{pc}_{quad}")
            dst = QCHU(8 + 4 * pc + quad)
            g.step("dve", lambda eng, ps=ps, dst=dst: eng.tensor_copy(dst, ps),
                   reads=[pk], writes=[("QCH", 8 + 4 * pc + quad)],
                   tag=f"d_ge2{pc}_{quad}")

    def gpd_tok(pc):
        return QCHU(8 + 4 * pc, 4)

    for pc in range(2):
        g.step("sp", lambda eng, pc=pc: [
            eng.dma_start(out=BB(4 * pc, 4),
                          in_=t["psel_d"][0:2048, 128 * pc:128 * (pc + 1)],
                          transpose=True)],
            reads=[("psel_d", T) for T in range(NT)] + keyspan("BIGB", 4 * pc, 4),
            writes=keyspan("BIGB", 4 * pc, 4), dma="tp", n_dma=1, tag=f"d_pt{pc}")

    for jo in range(NT):
        pst = c.PS_A if jo % 2 == 0 else c.PS_B
        pstn = "PS_A" if jo % 2 == 0 else "PS_B"
        for ch in range(4):
            psf, pkf = pst[:, 512 * ch:512 * (ch + 1)], (pstn, ch)

            def smm(eng, jo=jo, ch=ch, psf=psf):
                last = None
                for pc in range(2):
                    last = eng.matmul(psf, gpd_tok(pc)[:, 128 * jo:128 * (jo + 1)],
                                      BB(4 * pc, 4)[:, 512 * ch:512 * (ch + 1)],
                                      start=(pc == 0), stop=(pc == 1))
                return last
            g.step("pe", smm,
                   reads=[("QCH", 8 + u) for u in range(8)] +
                   keyspan("BIGB", 0, 8),
                   writes=[pkf], tag=f"d_sc{jo}_{ch}")
        xbuf, xk = [(c.PANA[:, :], ("PANA",)),
                    (c.PANB[:, :], ("PANB",)),
                    (c.XTI[:, 0:S], ("XTI", 0)),
                    (c.XTI[:, S:2 * S], ("XTI", 1))][jo % 4]
        g.step("sp", lambda eng, jo=jo, xbuf=xbuf: [
            eng.dma_start(out=xbuf, in_=xsrc[128 * jo:128 * (jo + 1), :])],
            reads=[("xm", jo)] + [("xm2", jo, s_) for s_ in range(4)] + [xk],
            writes=[xk], dma="ld", n_dma=1, tag=f"d_fx{jo}")
        g.step("pool", lambda eng, jo=jo, xbuf=xbuf: [
            eng.dma_start(out=xbuf, in_=yln[128 * jo:128 * (jo + 1), :],
                          accum_op=ALU.add)],
            reads=[("yd", jo), xk], writes=[xk], dma="wt", n_dma=1,
            tag=f"d_fy{jo}")
        psout = c.PS_A if jo % 2 == 0 else c.PS_B
        g.step("dve" if jo % 2 == 0 else "dve",
               lambda eng, xbuf=xbuf, psout=psout: eng.tensor_add(
                   xbuf, xbuf, psout[:, :]),
               reads=[xk] + keyspan("PS_A" if jo % 2 == 0 else "PS_B", 0, 4),
               writes=[xk], tag=f"d_fc{jo}")
        g.step("sp", lambda eng, jo=jo, xbuf=xbuf: [
            eng.dma_start(out=outT[128 * jo:128 * (jo + 1), :], in_=xbuf)],
            reads=[xk], writes=[("outT", jo)], dma="st", n_dma=1,
            tag=f"d_fo{jo}")


import ml_dtypes
from concourse.bass_utils import run_bass_kernel_spmd

FUSED = True
BF = ml_dtypes.bfloat16


def _colsT(a, n):
    return np.ascontiguousarray(np.asarray(a, np.float32).reshape(n, 128).T)


def _host_stage(inputs):
    inp = {k: np.asarray(v) for k, v in inputs.items()}
    sh = {}
    sh["l0_w_qkv"] = inp["L0_attn_c_attn_w"]
    sh["l0_b_qkv"] = _colsT(inp["L0_attn_c_attn_b"], 48)
    sh["l0_w_proj"] = inp["L0_attn_c_proj_w"]
    sh["l0_b_proj"] = _colsT(inp["L0_attn_c_proj_b"], 16)
    sh["l0_cam_gate"] = np.ascontiguousarray(inp["L0_attn_cam_gate"][:, None])
    sh["l0_cam_w"] = np.ascontiguousarray(
        np.stack([inp["L0_attn_cam_w0"][0], inp["L0_attn_cam_w1"][0]])[None, :])
    sh["l0_wgT"] = np.ascontiguousarray(inp["L0_ffn_wq_gate"].T)
    sh["l0_wuT"] = np.ascontiguousarray(inp["L0_ffn_wq_up"].T)
    sh["l0_wdT"] = np.ascontiguousarray(inp["L0_ffn_wq_down"].T)
    scol = np.zeros((128, 104), dtype=np.float32)
    scol[:, 0:44] = _colsT(inp["L0_ffn_s_gate"], 44)
    scol[:, 44:88] = _colsT(inp["L0_ffn_s_up"], 44)
    scol[:, 88:104] = _colsT(inp["L0_ffn_s_down"], 16)
    sh["l0_scol"] = scol
    sh["l1_wq"] = inp["L1_attn_wq"]
    sh["l1_wk"] = inp["L1_attn_wk"]
    sh["l1_wv"] = inp["L1_attn_wv"]
    sh["l1_wo"] = inp["L1_attn_wo"]
    half = 64
    invf = 1.0 / (100000.0 ** (np.arange(half, dtype=np.float64) / half))
    ang = np.arange(2048, dtype=np.float64)[:, None] * invf[None, :]
    cosT = np.cos(ang).T
    sinT = np.sin(ang).T
    sh["l1_c2T"] = np.concatenate([cosT, cosT], 0).astype(np.float32)
    sh["l1_s2T"] = np.concatenate([sinT, -sinT], 0).astype(np.float32)
    perm = np.zeros((128, 128), dtype=np.float32)
    for mc in range(128):
        perm[(mc + 64) % 128, mc] = 1.0
    sh["l1_perm64"] = perm.astype(BF)
    sh["l1_ones1f"] = np.ones((1, 128), dtype=np.float32)
    sh["l2_wq"] = inp["L1_ffn_wq"]
    sh["l2_wk"] = inp["L1_ffn_wk"]
    sh["l2_wv"] = inp["L1_ffn_wv"]
    sh["l2_wo"] = inp["L1_ffn_wo"]

    sh["l2_wr"] = _colsT(inp["L1_ffn_w_router"], 16)
    sh["l2_bq"] = _colsT(inp["L1_ffn_bq"], 16)
    sh["l2_bkv"] = np.concatenate(
        [_colsT(inp["L1_ffn_bk"], 8), _colsT(inp["L1_ffn_bv"], 8)], 1)
    sh["l2_ln1r"] = np.ascontiguousarray(
        np.asarray(inp["L1_ffn_ln1_w"], np.float32)[None, :])
    sh["l2_ln2c"] = _colsT(inp["L1_ffn_ln2_w"], 16)
    invf2 = 1.0 / (10000.0 ** (np.arange(0, 128, 2, dtype=np.float64) / 128.0))
    ang2 = np.arange(2048, dtype=np.float64)[:, None] * invf2[None, :]
    sh["l2_ctabd"] = np.concatenate(
        [np.cos(ang2), np.cos(ang2)], 1).astype(np.float32)
    sh["l2_stabd"] = np.concatenate(
        [-np.sin(ang2), np.sin(ang2)], 1).astype(np.float32)
    sh["l2_iota256"] = np.arange(256, dtype=np.float32)[None, :]
    sh["l2_idn"] = np.eye(128, dtype=BF)
    sh["l2_sut"] = np.triu(np.ones((128, 128)), 1).astype(BF)
    sh["l2_ons"] = np.ones((128, 128), dtype=BF)
    sh["ones_col"] = np.ones((128, 1), dtype=BF)
    sh["eps_col"] = np.tile(np.array([[1e-5, 1e-6]], np.float32), (128, 1))
    x = np.asarray(inp["L0_attn_hidden_states"], np.float32)
    xTs = [np.ascontiguousarray(x[r].T) for r in range(8)]
    wgs = [np.ascontiguousarray(inp["L1_ffn_w_gate"][:, 1024 * r:1024 * (r + 1)])
           for r in range(8)]
    wus = [np.ascontiguousarray(inp["L1_ffn_w_up"][:, 1024 * r:1024 * (r + 1)])
           for r in range(8)]
    wds = [np.ascontiguousarray(inp["L1_ffn_w_down"][1024 * r:1024 * (r + 1), :])
           for r in range(8)]
    sh["_wgs"], sh["_wus"], sh["_wds"] = wgs, wus, wds
    tt = np.arange(2048)[:, None]
    masks = [np.ascontiguousarray(
        (tt <= (256 * r + np.arange(256)[None, :])).astype(BF)) for r in range(8)]
    return sh, xTs, masks


_EI_SHAPES = {
    "l0_w_qkv": ([2048, 6144], F32), "l0_b_qkv": ([128, 48], F32),
    "l0_w_proj": ([2048, 2048], F32), "l0_b_proj": ([128, 16], F32),
    "l0_cam_gate": ([128, 1], F32), "l0_cam_w": ([1, 2], F32),
    "l0_wgT": ([2048, 5632], I32), "l0_wuT": ([2048, 5632], I32),
    "l0_wdT": ([5632, 2048], I32), "l0_scol": ([128, 104], F32),
    "l1_wq": ([2048, 2048], F32), "l1_wk": ([2048, 512], F32),
    "l1_wv": ([2048, 512], F32), "l1_wo": ([2048, 2048], F32),
    "l1_c2T": ([128, 2048], F32), "l1_s2T": ([128, 2048], F32),
    "l1_perm64": ([128, 128], BF16), "l1_ones1f": ([1, 128], F32),
    "l2_wq": ([2048, 2048], F32), "l2_wk": ([2048, 1024], F32),
    "l2_wv": ([2048, 1024], F32), "l2_wo": ([2048, 2048], F32),
    "l2_wgs": ([2048, 1024], F32), "l2_wus": ([2048, 1024], F32),
    "l2_wds": ([1024, 2048], F32), "l2_wr": ([128, 16], F32),
    "l2_bq": ([128, 16], F32), "l2_bkv": ([128, 16], F32),
    "l2_ln1r": ([1, 2048], F32), "l2_ln2c": ([128, 16], F32),
    "l2_ctabd": ([2048, 128], F32), "l2_stabd": ([2048, 128], F32),
    "l2_iota256": ([1, 256], F32), "l2_idn": ([128, 128], BF16),
    "l2_sut": ([128, 128], BF16), "l2_ons": ([128, 128], BF16),
    "l2_mask": ([2048, 256], BF16),
    "ones_col": ([128, 1], BF16), "eps_col": ([128, 2], F32),
}


def _declare(nc, t, names, xm_kind, out=False):
    from contextlib import ExitStack
    t["xT"] = nc.dram_tensor("xT", [2048, 2048], F32, kind="ExternalInput").ap()
    for n in names:
        shp, dt = _EI_SHAPES[n]
        t[n] = nc.dram_tensor(n, shp, dt, kind="ExternalInput").ap()
    t["xm"] = nc.dram_tensor("xm", [2048, 2048], F32, kind=xm_kind).ap()
    if out:
        t["outT"] = nc.dram_tensor("outT", [2048, 2048], F32,
                                   kind="ExternalOutput").ap()
    t["rows_d"] = nc.dram_tensor("rows_d", [8, 2048], F32).ap()
    return t


_L0A = ["l0_w_qkv", "l0_b_qkv", "l0_w_proj", "l0_b_proj", "l0_cam_gate",
        "l0_cam_w", "ones_col", "eps_col"]
_L0F = ["l0_wgT", "l0_wuT", "l0_wdT", "l0_scol", "ones_col", "eps_col"]
_L1A = ["l1_wq", "l1_wk", "l1_wv", "l1_wo", "l1_c2T", "l1_s2T", "l1_perm64",
        "l1_ones1f", "ones_col", "eps_col"]
_L1F = ["l2_wq", "l2_wk", "l2_wv", "l2_wo", "l2_wgs", "l2_wus", "l2_wds",
        "l2_wr", "l2_bq", "l2_bkv", "l2_ln1r", "l2_ln2c", "l2_ctabd",
        "l2_stabd", "l2_iota256", "l2_idn", "l2_sut", "l2_ons", "l2_mask",
        "l1_perm64", "l1_ones1f", "ones_col", "eps_col"]


def _add_d_scratch(nc, t):
    t["h2in"] = nc.dram_tensor("h2in", [2048, 256], BF16).ap()
    t["h2out"] = nc.dram_tensor("h2out", [16384, 256], BF16).ap()
    t["rsin"] = nc.dram_tensor("rsin", [16384, 256], BF16).ap()
    t["rsout"] = nc.dram_tensor("rsout", [2048, 256], BF16).ap()
    t["yln"] = nc.dram_tensor("yln", [2048, 2048], BF16).ap()
    t["psel_d"] = nc.dram_tensor("psel_d", [2048, 256], BF16).ap()
    t["kin"] = nc.dram_tensor("kin", [1024, 256], BF16).ap()
    t["kout"] = nc.dram_tensor("kout", [8192, 256], BF16).ap()
    t["vin"] = nc.dram_tensor("vin", [256, 1024], BF16).ap()
    t["vout"] = nc.dram_tensor("vout", [2048, 1024], BF16).ap()


def _build_launch(stage):
    from contextlib import ExitStack
    nc = bass.Bass("TRN2", target_bir_lowering=False)
    t = {}
    es = ExitStack()
    with es:
        if stage == "a":
            _declare(nc, t, _L0A, "ExternalOutput")
            t["qT_d"] = nc.dram_tensor("qT_d", [2048, 2048], BF16).ap()
            t["vT_d"] = nc.dram_tensor("vT_d", [2048, 2048], BF16).ap()
            c = alloc_common(nc, es)
            build_preload(c, t)
            build_ln(c, t["xT"], t["rows_d"], tag="ln1")
            build_l0_attn(c, t, xsrc=t["xT"])
            finalize(c, [("xm", j) for j in range(16)] +
                     [("xm2", j, s) for j in range(16) for s in range(4)])
        elif stage == "b":
            _declare(nc, t, _L0F, "ExternalOutput")
            t["dpart"] = nc.dram_tensor("dpart", [2048, 2048], F32).ap()
            c = alloc_common(nc, es)
            build_preload(c, t)
            build_ln(c, t["xT"], t["rows_d"], tag="ln2")
            build_l0_ffn(c, t, xsrc=t["xT"])
            finalize(c, [("xm", j) for j in range(16)])
        elif stage == "c":
            _declare(nc, t, _L1A, "ExternalOutput")
            t["qT_d"] = nc.dram_tensor("qT_d", [2048, 2048], BF16).ap()
            c = alloc_common(nc, es)
            build_preload(c, t)
            c.g.step("pool", lambda eng: [
                eng.dma_start(out=c.PERM[:, :], in_=t["l1_perm64"][:, :]),
                eng.dma_start(out=c.ONES1F[:, :], in_=t["l1_ones1f"][:, :])],
                writes=[("perm64",), ("ones1f",)], dma="wt", n_dma=2, tag="pre2")
            build_ln(c, t["xT"], t["rows_d"], tag="ln3")
            build_l1_attn(c, t, xsrc=t["xT"])
            finalize(c, [("xm", j) for j in range(16)] +
                     [("xm2", j, s) for j in range(16) for s in range(4)])
        else:
            _declare(nc, t, _L1F, "Internal", out=True)
            _add_d_scratch(nc, t)
            t["xm"] = t["xT"]
            c = alloc_common(nc, es)
            build_preload(c, t)
            c.g.step("pool", lambda eng: [
                eng.dma_start(out=c.PERM[:, :], in_=t["l1_perm64"][:, :]),
                eng.dma_start(out=c.ONES1F[:, :], in_=t["l1_ones1f"][:, :])],
                writes=[("perm64",), ("ones1f",)], dma="wt", n_dma=2, tag="pre2")
            build_ln(c, t["xT"], t["rows_d"], yd=t["yln"], tag="ln4")
            build_l1_ffn(c, t, xsrc=t["xT"])
            finalize(c, [("outT", j) for j in range(16)])
        c.g.emit(nc, es)
    return nc


def _build_fused():
    from contextlib import ExitStack
    nc = bass.Bass("TRN2", target_bir_lowering=False)
    t = {}
    names = sorted(set(_L0A + _L0F + _L1A + _L1F))
    es = ExitStack()
    with es:
        _declare(nc, t, names, "Internal", out=True)
        t["qT_d"] = nc.dram_tensor("qT_d", [2048, 2048], BF16).ap()
        t["vT_d"] = nc.dram_tensor("vT_d", [2048, 2048], BF16).ap()
        t["dpart"] = nc.dram_tensor("dpart", [2048, 2048], F32).ap()
        _add_d_scratch(nc, t)
        c = alloc_common(nc, es)
        build_preload(c, t)
        c.g.step("pool", lambda eng: [
            eng.dma_start(out=c.PERM[:, :], in_=t["l1_perm64"][:, :]),
            eng.dma_start(out=c.ONES1F[:, :], in_=t["l1_ones1f"][:, :])],
            writes=[("perm64",), ("ones1f",)], dma="wt", n_dma=2, tag="pre2")
        build_ln(c, t["xT"], t["rows_d"], tag="ln1")
        build_l0_attn(c, t, xsrc=t["xT"])
        build_ln(c, t["xm"], t["rows_d"], tag="ln2")
        build_l0_ffn(c, t, xsrc=t["xm"])
        build_ln(c, t["xm"], t["rows_d"], tag="ln3")
        build_l1_attn(c, t, xsrc=t["xm"])
        build_ln(c, t["xm"], t["rows_d"], yd=t["yln"], tag="ln4")
        build_l1_ffn(c, t, xsrc=t["xm"])
        finalize(c, [("outT", j) for j in range(16)])
        c.g.emit(nc, es)
    return nc


def kernel(**inputs):
    sh, xTs, masks = _host_stage(inputs)
    cores = list(range(8))
    if FUSED:
        nc = _build_fused()
        names = sorted(set(_L0A + _L0F + _L1A + _L1F))
        in_maps = []
        percore = {"l2_mask": None, "l2_wgs": None, "l2_wus": None,
                   "l2_wds": None}
        for r in range(8):
            m = {n: sh[n] for n in names if n not in percore and
                 not n.startswith("_")}
            m["l2_mask"] = masks[r]
            m["l2_wgs"] = sh["_wgs"][r]
            m["l2_wus"] = sh["_wus"][r]
            m["l2_wds"] = sh["_wds"][r]
            m["xT"] = xTs[r]
            in_maps.append(m)
        res = run_bass_kernel_spmd(nc, in_maps, core_ids=cores)
        outs = [res.results[r]["outT"].T for r in range(8)]
    else:
        cur = xTs
        for stage, names in (("a", _L0A), ("b", _L0F), ("c", _L1A)):
            nc = _build_launch(stage)
            in_maps = [dict({n: sh[n] for n in names}, xT=cur[r])
                       for r in range(8)]
            res = run_bass_kernel_spmd(nc, in_maps, core_ids=cores)
            cur = [np.ascontiguousarray(res.results[r]["xm"]) for r in range(8)]
        nc = _build_launch("d")
        in_maps = []
        for r in range(8):
            m = {n: sh[n] for n in _L1F if n not in
                 ("l2_mask", "l2_wgs", "l2_wus", "l2_wds")}
            m["l2_mask"] = masks[r]
            m["l2_wgs"] = sh["_wgs"][r]
            m["l2_wus"] = sh["_wus"][r]
            m["l2_wds"] = sh["_wds"][r]
            m["xT"] = cur[r]
            in_maps.append(m)
        res = run_bass_kernel_spmd(nc, in_maps, core_ids=cores)
        outs = [res.results[r]["outT"].T for r in range(8)]
    return np.ascontiguousarray(np.stack(outs, axis=0)).astype(np.float32)


# version 19
# speedup vs baseline: 1.0125x; 1.0125x over previous
import numpy as np
from contextlib import ExitStack

import concourse.bass as bass
import concourse.mybir as mybir

F32 = mybir.dt.float32
BF16 = mybir.dt.bfloat16
I32 = mybir.dt.int32
AF = mybir.ActivationFunctionType
ALU = mybir.AluOpType
AX = mybir.AxisListType

D = 2048
S = 2048
NT = D // 128
SC = 512
NSC = S // SC
SCALE = 1.0 / float(np.sqrt(128.0))

ENGINES = ("sp", "pool", "pe", "act", "dve")
DMA_UNIT = {"cc": 1}


class Step:
    __slots__ = ("idx", "engine", "fn", "reads", "writes", "kind", "stream",
                 "dma_count_after", "dma_count_before", "dma_slot",
                 "inc_needed", "waits", "milestone", "tag", "_deps")

    def __init__(self, idx, engine, fn, reads, writes, kind, stream, tag):
        self.idx, self.engine, self.fn = idx, engine, fn
        self.reads, self.writes = list(reads), list(writes)
        self.kind, self.stream, self.tag = kind, stream, tag
        self.dma_count_after = 0
        self.inc_needed = False
        self.waits = []
        self.milestone = 0


NSLOT = 6


class Sched:
    def __init__(self):
        self.steps = []
        self.last_write = {}
        self.readers = {}
        self.dma_counts = {}
        self.stream_seq = {}

    def step(self, engine, fn, reads=(), writes=(), dma=None, n_dma=0, tag=""):
        st = Step(len(self.steps), engine, fn, reads, writes,
                  "dma" if dma else "compute", dma, tag)
        if dma:
            seq = self.stream_seq.get(dma, 0)
            self.stream_seq[dma] = seq + 1
            slot = seq % NSLOT
            key = (dma, slot)
            st.dma_slot = key
            st.dma_count_before = self.dma_counts.get(key, 0)
            c = st.dma_count_before + n_dma
            self.dma_counts[key] = c
            st.dma_count_after = c
        deps = {}
        for b in st.reads:
            w = self.last_write.get(b)
            if w is not None:
                deps[w.idx] = w
        for b in st.writes:
            w = self.last_write.get(b)
            if w is not None:
                deps[w.idx] = w
            for r in self.readers.get(b, ()):
                deps[r.idx] = r
        st._deps = [deps[i] for i in sorted(deps)]
        for b in st.reads:
            self.readers.setdefault(b, []).append(st)
        for b in st.writes:
            self.last_write[b] = st
            self.readers[b] = []
        self.steps.append(st)
        return st

    def emit(self, nc, es):
        eng_sem = {e: es.enter_context(nc.semaphore(f"sg_{e}")) for e in ENGINES}
        dma_sems = {k: es.enter_context(nc.semaphore(f"sd_{k[0]}{k[1]}"))
                    for k in self.dma_counts}
        for st in self.steps:
            raw = []
            if st.kind == "dma" and st.dma_count_before > 0:
                unit = DMA_UNIT.get(st.stream, 16)
                raw.append(("dma", st.dma_slot, unit * st.dma_count_before))
            for d in st._deps:
                if d.kind == "dma":
                    unit = DMA_UNIT.get(d.stream, 16)
                    raw.append(("dma", d.dma_slot, unit * d.dma_count_after))
                elif d.engine == st.engine == "pe":
                    continue
                else:
                    d.inc_needed = True
                    raw.append(("pend", d, 0))
            st.waits = raw
        inc_count = {e: 0 for e in ENGINES}
        for st in self.steps:
            if st.inc_needed and st.kind == "compute":
                inc_count[st.engine] += 1
            st.milestone = inc_count[st.engine]
        for st in self.steps:
            best = {}
            for kind, key, val in st.waits:
                if kind == "pend":
                    if key.kind == "dma":
                        continue
                    kind, key, val = "eng", key.engine, key.milestone
                k = (kind, key)
                if val > best.get(k, 0):
                    best[k] = val
            st.waits = [(k[0], k[1], v) for k, v in best.items()]

        per_engine = {e: [] for e in ENGINES}
        for st in self.steps:
            per_engine[st.engine].append(st)

        block = es.enter_context(nc.Block())

        def make_section(ename):
            hw = {}

            def body(eng):
                for st in per_engine[ename]:
                    for kind, key, val in st.waits:
                        k = (kind, key)
                        if val == 0 or hw.get(k, 0) >= val:
                            continue
                        hw[k] = val
                        sem = eng_sem[key] if kind == "eng" else dma_sems[key]
                        eng.wait_ge(sem, val)
                    last = st.fn(eng)
                    if st.kind == "dma":
                        assert isinstance(last, (list, tuple)) and last, \
                            f"dma step {st.tag} must return dma instrs"
                        unit = DMA_UNIT.get(st.stream, 16)
                        for ins in last:
                            ins.then_inc(dma_sems[st.dma_slot], unit)
                    elif st.inc_needed:
                        assert last is not None, f"step {st.tag} needs last instr"
                        last.then_inc(eng_sem[st.engine], 1)
            return body

        for ename, attr in (("sp", "sync"), ("pool", "gpsimd"), ("pe", "tensor"),
                            ("act", "scalar"), ("dve", "vector")):
            if per_engine[ename]:
                getattr(block, attr)(make_section(ename))
        return eng_sem, dma_sems


def keyspan(name, c0, n):
    return [(name, c0 + k) for k in range(n)]


class Ctx:
    pass


def alloc_common(nc, es):
    c = Ctx()
    c.nc = nc
    c.g = Sched()
    sbuf = lambda n, sh, dt: es.enter_context(nc.sbuf_tensor(n, sh, dt))
    psum = lambda n, sh, dt: es.enter_context(nc.psum_tensor(n, sh, dt))
    c.BIGA = sbuf("BIGA", [128, NT * S], BF16)
    c.BIGB = sbuf("BIGB", [128, NT * S], BF16)
    c.QCH = sbuf("QCH", [128, NT * SC], BF16)
    c.EBUF = sbuf("EBUF", [128, NT * SC], BF16)
    c.WBUF = sbuf("WBUF", [128, 2 * S], BF16)
    c.XTI = sbuf("XTI", [128, 2 * S], F32)
    c.PANA = sbuf("PANA", [128, S], F32)
    c.PANB = sbuf("PANB", [128, S], F32)
    c.CAM = sbuf("CAM", [128, SC], F32)
    c.BCOL = sbuf("BCOL", [128, 64], F32)
    c.SCOL = sbuf("SCOL", [128, 144], F32)
    c.PERM = sbuf("PERM", [128, 128], BF16)
    c.IDN = sbuf("IDN", [128, 128], BF16)
    c.SUT = sbuf("SUT", [128, 128], BF16)
    c.ONS = sbuf("ONS", [128, 128], BF16)
    c.MCOL = sbuf("MCOL", [128, 16], BF16)
    c.QBF = sbuf("QBF", [128, 512], BF16)
    c.BQKV = sbuf("BQKV", [128, 48], F32)
    c.ONES1F = sbuf("ONES1F", [1, 128], F32)
    c.ONE = sbuf("ONE", [128, 8], BF16)
    c.PS_A = psum("PS_A", [128, 2048], F32)
    c.PS_B = psum("PS_B", [128, 2048], F32)
    return c


def biga(c, i):
    return c.BIGA[:, i * S:(i + 1) * S]


def bigb(c, i):
    return c.BIGB[:, i * S:(i + 1) * S]


def build_preload(c, t):
    g = c.g

    def fn(eng):
        ins = [
            eng.dma_start(out=c.ONE[:, 0:1], in_=t["ones_col"][:, :]),
            eng.dma_start(out=c.BCOL[:, 30:32], in_=t["eps_col"][:, :]),
        ]
        return ins
    g.step("sp", fn, writes=[("ONE",), ("BCOL", 30)], dma="ld",
           n_dma=2, tag="pre")


def build_ln(c, xsrc, rows_d, yd=None, eps=1e-5, tag="ln"):
    nc, g = c.nc, c.g
    for i in range(NT):
        xb = biga(c, i)
        kxb = keyspan("BIGA", 4 * i, 4)
        sl = i % 2
        xq = c.EBUF[:, sl * S:(sl + 1) * S]
        kxq = keyspan("EBUF", 4 * sl, 4)
        g.step("pool", lambda eng, i=i, xb=xb: [
            eng.dma_start(out=xb, in_=xsrc[128 * i:128 * (i + 1), :])],
            reads=[("xm", i)] + [("xm2", i, s_) for s_ in range(4)],
            writes=kxb, dma="wt", n_dma=1, tag=f"{tag}_lx{i}")
        g.step("act", lambda eng, xb=xb, xq=xq: eng.activation(xq, xb, AF.Square),
               reads=kxb, writes=kxq, tag=f"{tag}_sq{i}")

        def mm(eng, i=i, xb=xb, xq=xq):
            last = None
            for ch in range(4):
                last = eng.matmul(c.PS_B[0:1, 512 * ch:512 * ch + 512],
                                  c.ONE[:, 0:1], xb[:, 512 * ch:512 * ch + 512],
                                  start=(i == 0), stop=(i == NT - 1))
                last = eng.matmul(c.PS_A[0:1, 512 * ch:512 * ch + 512],
                                  c.ONE[:, 0:1], xq[:, 512 * ch:512 * ch + 512],
                                  start=(i == 0), stop=(i == NT - 1))
            return last
        g.step("pe", mm, reads=kxb + kxq + [("ONE",)],
               writes=keyspan("PS_B", 0, 4) + keyspan("PS_A", 0, 4),
               tag=f"{tag}_mm{i}")

    mean_r = c.PANA[0:1, :]
    a_r = c.PANB[0:1, :]
    b_r = c.XTI[0:1, 0:2048]
    g.step("dve", lambda eng: eng.tensor_scalar(
        mean_r, c.PS_B[0:1, :], 1.0 / D, None, ALU.mult),
        reads=keyspan("PS_B", 0, 4), writes=[("PANA",)], tag=f"{tag}_mean")
    g.step("dve", lambda eng: eng.tensor_mul(a_r, mean_r, mean_r),
           reads=[("PANA",)], writes=[("PANB",)], tag=f"{tag}_m2")
    g.step("dve", lambda eng: eng.scalar_tensor_tensor(
        a_r, c.PS_A[0:1, :], 1.0 / D, a_r, ALU.mult, ALU.subtract),
        reads=keyspan("PS_A", 0, 4) + [("PANB",)], writes=[("PANB",)],
        tag=f"{tag}_var")
    epscol = 30 if eps == 1e-5 else 31
    g.step("act", lambda eng: eng.activation(a_r, a_r, AF.Ln,
                                             bias=c.BCOL[0:1, epscol:epscol + 1]),
           reads=[("PANB",), ("BCOL", 30)], writes=[("PANB",)], tag=f"{tag}_lnv")
    g.step("act", lambda eng: eng.activation(a_r, a_r, AF.Exp, scale=-0.5),
           reads=[("PANB",)], writes=[("PANB",)], tag=f"{tag}_rstd")
    g.step("dve", lambda eng: eng.scalar_tensor_tensor(
        b_r, mean_r, -1.0, a_r, ALU.mult, ALU.mult),
        reads=[("PANA",), ("PANB",)], writes=[("XTI", 0)], tag=f"{tag}_bb")
    g.step("sp", lambda eng: [
        eng.dma_start(out=rows_d[0:1, :], in_=a_r),
        eng.dma_start(out=rows_d[1:2, :], in_=b_r)],
        reads=[("PANB",), ("XTI", 0)], writes=[("rows_d", 0), ("rows_d", 1)],
        dma="st", n_dma=2, tag=f"{tag}_rst")
    pan_a = c.PANA[:, :]
    pan_b = c.PANB[:, :]
    g.step("sp", lambda eng: [
        eng.dma_start(out=pan_a, in_=rows_d[0:1, :].partition_broadcast(128)),
        eng.dma_start(out=pan_b, in_=rows_d[1:2, :].partition_broadcast(128))],
        reads=[("rows_d", 0), ("rows_d", 1), ("PANA",), ("PANB",)],
        writes=[("PANA",), ("PANB",)], dma="ld", n_dma=2, tag=f"{tag}_bc")
    for i in range(NT):
        xb = biga(c, i)
        kxb = keyspan("BIGA", 4 * i, 4)
        apeng = "dve" if i % 2 == 0 else "pool"
        g.step(apeng, lambda eng, xb=xb: eng.tensor_tensor(
            xb, xb, pan_a, ALU.mult),
            reads=kxb + [("PANA",)], writes=kxb, tag=f"{tag}_ap1{i}")
        g.step(apeng, lambda eng, xb=xb: eng.tensor_tensor(
            xb, xb, pan_b, ALU.add),
            reads=kxb + [("PANB",)], writes=kxb, tag=f"{tag}_ap2{i}")
        if yd is not None:
            g.step("sp", lambda eng, i=i, xb=xb: [
                eng.dma_start(out=yd[128 * i:128 * (i + 1), :], in_=xb)],
                reads=kxb, writes=[("yd", i)], dma="st", n_dma=1,
                tag=f"{tag}_sy{i}")


def build_l0_attn(c, t, xsrc):
    nc, g = c.nc, c.g
    w = t["l0_w_qkv"]
    xm = t["xm"]
    qT_d, vT_d = t["qT_d"], t["vT_d"]

    g.step("pool", lambda eng: [
        eng.dma_start(out=c.BQKV[:, :], in_=t["l0_b_qkv"][:, :]),
        eng.dma_start(out=c.BCOL[:, 2:18], in_=t["l0_b_proj"][:, :])],
        writes=[("BQKV",), ("BCOL", 2)], dma="wt", n_dma=2, tag="a_bld")
    g.step("pool", lambda eng: [
        eng.dma_start(out=c.BCOL[:, 0:1], in_=t["l0_cam_gate"][:, :]),
        eng.dma_start(out=c.PANB[64:65, 0:2], in_=t["l0_cam_w"][0:1, :])],
        writes=[("BCOL", 0), ("PANB",)], dma="wt", n_dma=2, tag="a_cld")
    g.step("dve", lambda eng: eng.scalar_tensor_tensor(
        c.PANB[64:65, 2:3], c.PANB[64:65, 1:2], 0.5, c.PANB[64:65, 0:1],
        ALU.mult, ALU.add), reads=[("PANB",)], writes=[("PANB",)], tag="a_lw0")
    g.step("act", lambda eng: eng.activation(
        c.PANB[64:65, 2:3], c.PANB[64:65, 2:3], AF.Sigmoid),
        reads=[("PANB",)], writes=[("PANB",)], tag="a_lw1")
    g.step("sp", lambda eng: [
        eng.dma_start(out=t["rows_d"][2:3, 0:1], in_=c.PANB[64:65, 2:3])],
        reads=[("PANB",)], writes=[("rows_d", 2)], dma="st", n_dma=1, tag="a_lw2")
    g.step("sp", lambda eng: [
        eng.dma_start(out=c.BCOL[:, 1:2],
                      in_=t["rows_d"][2:3, 0:1].partition_broadcast(128))],
        reads=[("rows_d", 2)], writes=[("BCOL", 1)], dma="ld", n_dma=1, tag="a_lw3")

    wslots = [(c.WBUF[:, 0:S], keyspan("WBUF", 0, 4)),
              (c.WBUF[:, S:2 * S], keyspan("WBUF", 4, 4)),
              (c.EBUF[:, 0:S], keyspan("EBUF", 0, 4)),
              (c.EBUF[:, S:2 * S], keyspan("EBUF", 4, 4)),
              (c.EBUF[:, 2 * S:3 * S], keyspan("EBUF", 8, 4)),
              (c.EBUF[:, 3 * S:4 * S], keyspan("EBUF", 12, 4))]
    for j in range(48):
        wtile, kw = wslots[j % 6]
        src = w.rearrange("(a p) (j f) -> j p a f", p=128, f=128)[j]
        wt3 = wtile.rearrange("p (a f) -> p a f", f=128)
        g.step("pool", lambda eng, src=src, wt3=wt3: [
            eng.dma_start(out=wt3, in_=src)],
            reads=[], writes=kw, dma="wt", n_dma=1, tag=f"a_wl{j}")
        ps = c.PS_A if j % 2 == 0 else c.PS_B
        pskeys = keyspan("PS_A" if j % 2 == 0 else "PS_B", 0, 4)

        def qkvmm(eng, j=j, ps=ps, wtile=wtile):
            last = None
            for d in range(NT):
                lhsT = wtile[:, d * 128:(d + 1) * 128]
                for ch in range(4):
                    last = eng.matmul(
                        ps[:, 512 * ch:512 * ch + 512], lhsT,
                        biga(c, d)[:, 512 * ch:512 * ch + 512],
                        start=(d == 0), stop=False)
            return last
        g.step("pe", qkvmm,
               reads=kw + keyspan("BIGA", 0, 64),
               writes=pskeys, tag=f"a_mm{j}")
        if 16 <= j < 32:
            dst = bigb(c, j - 16)
            dkeys = keyspan("BIGB", 4 * (j - 16), 4)
            g.step("act", lambda eng, ps=ps, dst=dst, j=j: eng.activation(
                dst, ps[:, :], AF.Identity, bias=c.BQKV[:, j:j + 1]),
                reads=pskeys + [("BQKV",)], writes=dkeys, tag=f"a_evK{j}")
        else:
            ssl = j % 2
            stg = c.QCH[:, ssl * S:(ssl + 1) * S]
            skeys = keyspan("QCH", 4 * ssl, 4)
            g.step("act", lambda eng, ps=ps, stg=stg, j=j: eng.activation(
                stg, ps[:, :], AF.Identity, bias=c.BQKV[:, j:j + 1]),
                reads=pskeys + [("BQKV",)], writes=skeys, tag=f"a_ev{j}")
            dd = qT_d if j < 16 else vT_d
            r = j if j < 16 else j - 32
            g.step("sp", lambda eng, stg=stg, dd=dd, r=r: [
                eng.dma_start(out=dd[128 * r:128 * (r + 1), :], in_=stg)],
                reads=skeys, writes=[("qT_d" if j < 16 else "vT_d", r)],
                dma="st", n_dma=1, tag=f"a_sp{j}")

    for tt in range(NT):
        g.step("sp", lambda eng, tt=tt: [
            eng.dma_start(out=biga(c, tt), in_=vT_d[0:2048, 128 * tt:128 * (tt + 1)],
                          transpose=True)],
            reads=[("vT_d", r) for r in range(16)] + keyspan("BIGA", 4 * tt, 4),
            writes=keyspan("BIGA", 4 * tt, 4), dma="tp", n_dma=1, tag=f"a_vt{tt}")

    for sc in range(NSC):
        s0 = SC * sc
        qsrc = qT_d.rearrange("(a p) s -> p a s", p=128)[:, :, s0:s0 + SC]
        qdst = c.QCH[:, :].rearrange("p (a s) -> p a s", s=512)
        g.step("sp", lambda eng, qsrc=qsrc, qdst=qdst: [
            eng.dma_start(out=qdst, in_=qsrc)],
            reads=[("qT_d", r) for r in range(16)] + keyspan("QCH", 0, 16),
            writes=keyspan("QCH", 0, 16), dma="ld", n_dma=1, tag=f"a_ql{sc}")
        qch = lambda e: c.QCH[:, e * SC:(e + 1) * SC]

        for tg in range(4):
            pst = c.PS_A if tg % 2 == 0 else c.PS_B
            pstn = "PS_A" if tg % 2 == 0 else "PS_B"

            def bsmm(eng, tg=tg, pst=pst):
                last = None
                for k4 in range(4):
                    tt = 4 * tg + k4
                    last = eng.matmul(pst[:, 512 * k4:512 * k4 + 512],
                                      bigb(c, 0)[:, 128 * tt:128 * (tt + 1)],
                                      qch(0), start=True, stop=True)
                return last
            g.step("pe", bsmm,
                   reads=keyspan("BIGB", 0, 4) + keyspan("QCH", 0, 1),
                   writes=keyspan(pstn, 0, 4), tag=f"a_bs{sc}_{tg}")
            ebt = c.EBUF[:, 4 * tg * SC:(4 * tg + 4) * SC]
            g.step("act", lambda eng, ebt=ebt, pst=pst: eng.activation(
                ebt, pst[:, :], AF.Exp, scale=SCALE),
                reads=keyspan(pstn, 0, 4),
                writes=keyspan("EBUF", 4 * tg, 4), tag=f"a_be{sc}_{tg}")
        def bdmm(eng):
            last = None
            for tt in range(NT):
                last = eng.matmul(c.PS_B[0:1, 0:512], c.ONE[:, 0:1],
                                  c.EBUF[:, tt * SC:(tt + 1) * SC],
                                  start=(tt == 0), stop=(tt == NT - 1))
            return last
        g.step("pe", bdmm, reads=keyspan("EBUF", 0, 16) + [("ONE",)],
               writes=[("PS_B", 0)], tag=f"a_bd{sc}")
        g.step("dve", lambda eng: eng.reciprocal(c.PANB[0:1, 0:512], c.PS_B[0:1, 0:512]),
               reads=[("PS_B", 0)], writes=[("PANB",)], tag=f"a_br{sc}")
        g.step("sp", lambda eng: [
            eng.dma_start(out=t["rows_d"][3:4, 0:512], in_=c.PANB[0:1, 0:512])],
            reads=[("PANB",)], writes=[("rows_d", 3)], dma="st", n_dma=1,
            tag=f"a_brs{sc}")
        g.step("sp", lambda eng: [
            eng.dma_start(out=c.PANA[:, 0:512],
                          in_=t["rows_d"][3:4, 0:512].partition_broadcast(128))],
            reads=[("rows_d", 3), ("PANA",)], writes=[("PANA",)], dma="ld",
            n_dma=1, tag=f"a_brb{sc}")
        def camm(eng):
            last = None
            for tt in range(NT):
                last = eng.matmul(c.PS_B[:, 512:1024], biga(c, tt)[:, 0:128],
                                  c.EBUF[:, tt * SC:(tt + 1) * SC],
                                  start=(tt == 0), stop=(tt == NT - 1))
            return last
        g.step("pe", camm, reads=keyspan("BIGA", 0, 64) + keyspan("EBUF", 0, 16),
               writes=[("PS_B", 1)], tag=f"a_cm{sc}")
        g.step("dve", lambda eng: eng.scalar_tensor_tensor(
            c.CAM[:, :], c.PS_B[:, 512:1024], c.BCOL[:, 0:1], c.PANA[:, 0:512],
            ALU.mult, ALU.mult),
            reads=[("PS_B", 1), ("BCOL", 0), ("PANA",)], writes=[("CAM",)],
            tag=f"a_cn{sc}")
        g.step("act", lambda eng: eng.activation(c.CAM[:, :], c.CAM[:, :], AF.Tanh),
               reads=[("CAM",)], writes=[("CAM",)], tag=f"a_ct{sc}")
        g.step("dve", lambda eng: eng.tensor_scalar(
            c.WBUF[:, 0:512], c.CAM[:, :], c.BCOL[:, 1:2], None, ALU.mult),
            reads=[("CAM",), ("BCOL", 1)], writes=[("WBUF", 0)], tag=f"a_cl{sc}")

        for tg in range(4):
            pst = c.PS_A if tg % 2 == 0 else c.PS_B
            pstn = "PS_A" if tg % 2 == 0 else "PS_B"

            def fsmm(eng, tg=tg, pst=pst):
                last = None
                for k4 in range(4):
                    tt = 4 * tg + k4
                    psc = pst[:, 512 * k4:512 * k4 + 512]
                    for e in range(NT):
                        last = eng.matmul(psc, bigb(c, e)[:, 128 * tt:(tt + 1) * 128],
                                          qch(e), start=(e == 0), stop=False)
                    last = eng.matmul(psc, bigb(c, 0)[:, 128 * tt:(tt + 1) * 128],
                                      c.WBUF[:, 0:512], start=False, stop=True)
                return last
            g.step("pe", fsmm,
                   reads=keyspan("BIGB", 0, 64) + keyspan("QCH", 0, 16) + [("WBUF", 0)],
                   writes=keyspan(pstn, 0, 4), tag=f"a_fs{sc}_{tg}")
            ebt = c.EBUF[:, 4 * tg * SC:(4 * tg + 4) * SC]
            g.step("act", lambda eng, ebt=ebt, pst=pst: eng.activation(
                ebt, pst[:, :], AF.Exp, scale=SCALE),
                reads=keyspan(pstn, 0, 4),
                writes=keyspan("EBUF", 4 * tg, 4), tag=f"a_fe{sc}_{tg}")
        g.step("pe", bdmm, reads=keyspan("EBUF", 0, 16) + [("ONE",)],
               writes=[("PS_B", 0)], tag=f"a_fd{sc}")
        g.step("dve", lambda eng: eng.reciprocal(c.PANB[0:1, 0:512], c.PS_B[0:1, 0:512]),
               reads=[("PS_B", 0)], writes=[("PANB",)], tag=f"a_fr{sc}")
        g.step("sp", lambda eng: [
            eng.dma_start(out=t["rows_d"][3:4, 0:512], in_=c.PANB[0:1, 0:512])],
            reads=[("PANB",)], writes=[("rows_d", 3)], dma="st", n_dma=1,
            tag=f"a_frs{sc}")
        g.step("sp", lambda eng: [
            eng.dma_start(out=c.PANA[:, 0:512],
                          in_=t["rows_d"][3:4, 0:512].partition_broadcast(128))],
            reads=[("rows_d", 3), ("PANA",)], writes=[("PANA",)], dma="ld",
            n_dma=1, tag=f"a_frb{sc}")

        for e in range(NT):
            bank = e % 2
            psc, pk = (c.PS_B[:, 1024 + 512 * bank:1536 + 512 * bank], ("PS_B", 2 + bank))

            def avmm(eng, e=e, psc=psc):
                last = None
                for tt in range(NT):
                    last = eng.matmul(psc, biga(c, tt)[:, 128 * e:(e + 1) * 128],
                                      c.EBUF[:, tt * SC:(tt + 1) * SC],
                                      start=(tt == 0), stop=(tt == NT - 1))
                return last
            g.step("pe", avmm, reads=keyspan("BIGA", 0, 64) + keyspan("EBUF", 0, 16),
                   writes=[pk], tag=f"a_av{sc}_{e}")
            ctx_t = c.QCH[:, e * SC:(e + 1) * SC]
            g.step("dve", lambda eng, psc=psc, ctx_t=ctx_t: eng.tensor_mul(
                ctx_t, psc, c.PANA[:, 0:512]),
                reads=[pk, ("PANA",)], writes=[("QCH", e)], tag=f"a_an{sc}_{e}")

        pslots = [(c.WBUF[:, S:2 * S], keyspan("WBUF", 4, 4)),
                  (c.EBUF[:, 0:S], keyspan("EBUF", 0, 4)),
                  (c.EBUF[:, S:2 * S], keyspan("EBUF", 4, 4)),
                  (c.EBUF[:, 2 * S:3 * S], keyspan("EBUF", 8, 4))]
        for jo in range(NT):
            wtile, kw = pslots[jo % 4]
            src = t["l0_w_proj"].rearrange("(a p) (j f) -> j p a f", p=128, f=128)[jo]
            wt3 = wtile.rearrange("p (a f) -> p a f", f=128)
            g.step("pool", lambda eng, src=src, wt3=wt3: [
                eng.dma_start(out=wt3, in_=src)],
                writes=kw, dma="wt", n_dma=1, tag=f"a_pw{sc}_{jo}")
            psc, pk = (c.PS_A[:, 512 * (jo % 2):512 * (jo % 2) + 512], ("PS_A", jo % 2))

            def pmm(eng, wtile=wtile, psc=psc, jo=jo):
                last = None
                for e in range(NT):
                    last = eng.matmul(psc, wtile[:, e * 128:(e + 1) * 128],
                                      c.QCH[:, e * SC:(e + 1) * SC],
                                      start=(e == 0), stop=(e == NT - 1))
                return last
            g.step("pe", pmm, reads=kw + keyspan("QCH", 0, 16) + [("BROW",), ("ONER",)],
                   writes=[pk], tag=f"a_pm{sc}_{jo}")
            xsl = jo % 2
            xt = c.XTI[:, xsl * S + 0: xsl * S + 512]
            g.step("sp", lambda eng, jo=jo, xt=xt, s0=s0: [
                eng.dma_start(out=xt, in_=xsrc[128 * jo:128 * (jo + 1), s0:s0 + SC])],
                reads=[("xm2", jo, sc)], writes=[("XTI", xsl)], dma="ld", n_dma=1,
                tag=f"a_rx{sc}_{jo}")
            g.step("dve", lambda eng, xt=xt, psc=psc, jo=jo: eng.scalar_tensor_tensor(
                xt, psc, c.BCOL[:, 2 + jo:3 + jo], xt, ALU.add, ALU.add),
                reads=[pk, ("XTI", xsl), ("BCOL", 2)], writes=[("XTI", xsl)],
                tag=f"a_ra{sc}_{jo}")
            g.step("sp", lambda eng, jo=jo, xt=xt, s0=s0: [
                eng.dma_start(out=xm[128 * jo:128 * (jo + 1), s0:s0 + SC], in_=xt)],
                reads=[("XTI", xsl)], writes=[("xm2", jo, sc), ("xm", jo)],
                dma="st", n_dma=1, tag=f"a_rs{sc}_{jo}")


def finalize(c, outs_dram_keys):
    g = c.g
    g.step("pool", lambda eng: None, reads=list(outs_dram_keys), tag="final")


HCH = [(0, 16), (16, 32), (32, 44)]


def build_l0_ffn(c, t, xsrc):
    nc, g = c.nc, c.g
    xm, dpart = t["xm"], t["dpart"]

    g.step("pool", lambda eng: [
        eng.dma_start(out=c.SCOL[:, 0:104], in_=t["l0_scol"][:, :])],
        writes=[("SCOL",)], dma="wt", n_dma=1, tag="b_scol")

    def load_int_block(wname, wsrc, hh, nh_cols, xsl, esl, tagn):
        src3 = wsrc.rearrange("(a p) (j f) -> j p a f", p=128, f=128)[hh]
        ebuf = c.EBUF[:, esl * S:(esl + 1) * S]
        eb3 = ebuf.rearrange("p (a f) -> p a f", f=128)
        ek = keyspan("EBUF", 4 * esl, 4)
        g.step("pool", lambda eng: [eng.dma_start(out=eb3, in_=src3)],
               reads=[], writes=ek, dma="wt", n_dma=1, tag=f"b_wl{tagn}")
        g.step("dve", lambda eng: eng.tensor_scalar(
            ebuf, ebuf, -128.0, None, ALU.add),
            reads=ek, writes=ek, tag=f"b_wc{tagn}")
        return ebuf

    for hci, (h0, h1) in enumerate(HCH):
        nh = h1 - h0
        for hh in range(h0, h1):
            hl = hh - h0
            xsl = hh % 2
            wg = load_int_block("g", t["l0_wgT"], hh, 128, xsl, (hh % 2) * 2 + 0,
                                f"g{hh}")
            wu = load_int_block("u", t["l0_wuT"], hh, 128, xsl, (hh % 2) * 2 + 1,
                                f"u{hh}")
            kwg = keyspan("EBUF", 4 * ((hh % 2) * 2 + 0), 4)
            kwu = keyspan("EBUF", 4 * ((hh % 2) * 2 + 1), 4)

            def gmm(eng, wg=wg):
                last = None
                for d in range(NT):
                    lhsT = wg[:, d * 128:(d + 1) * 128]
                    for ch in range(4):
                        last = eng.matmul(c.PS_A[:, 512 * ch:512 * ch + 512],
                                          lhsT,
                                          biga(c, d)[:, 512 * ch:512 * ch + 512],
                                          start=(d == 0), stop=(d == NT - 1))
                return last
            g.step("pe", gmm, reads=kwg + keyspan("BIGA", 0, 64),
                   writes=keyspan("PS_A", 0, 4), tag=f"b_gm{hh}")

            def umm(eng, wu=wu):
                last = None
                for d in range(NT):
                    lhsT = wu[:, d * 128:(d + 1) * 128]
                    for ch in range(4):
                        last = eng.matmul(c.PS_B[:, 512 * ch:512 * ch + 512],
                                          lhsT,
                                          biga(c, d)[:, 512 * ch:512 * ch + 512],
                                          start=(d == 0), stop=(d == NT - 1))
                return last
            g.step("pe", umm, reads=kwu + keyspan("BIGA", 0, 64),
                   writes=keyspan("PS_B", 0, 4), tag=f"b_um{hh}")

            for ch in range(4):
                psg = c.PS_A[:, 512 * ch:512 * ch + 512]
                psu = c.PS_B[:, 512 * ch:512 * ch + 512]
                g.step("act", lambda eng, psg=psg, hh=hh: eng.activation(
                    c.CAM[:, :], psg, AF.Silu, scale=c.SCOL[:, hh:hh + 1]),
                    reads=[("PS_A", ch), ("SCOL",)], writes=[("CAM",)],
                    tag=f"b_si{hh}_{ch}")
                atile = bigb(c, hl)[:, 512 * ch:512 * ch + 512]
                g.step("dve", lambda eng, psu=psu, atile=atile, hh=hh:
                       eng.scalar_tensor_tensor(
                           atile, psu, c.SCOL[:, 44 + hh:45 + hh], c.CAM[:, :],
                           ALU.mult, ALU.mult),
                       reads=[("PS_B", ch), ("CAM",), ("SCOL",)],
                       writes=[("BIGB", 4 * hl + ch)], tag=f"b_ac{hh}_{ch}")

        for jo in range(NT):
            src3 = t["l0_wdT"].rearrange("(a p) (j f) -> j p a f", p=128, f=128)
            blk = src3[jo][:, h0:h1, :]
            esl = jo % 2
            wd = c.EBUF[:, esl * S:esl * S + nh * 128]
            wd3 = wd.rearrange("p (a f) -> p a f", f=128)
            ek = keyspan("EBUF", 4 * esl, 4)
            g.step("pool", lambda eng, wd3=wd3, blk=blk: [
                eng.dma_start(out=wd3, in_=blk)],
                reads=[], writes=ek, dma="wt", n_dma=1,
                tag=f"b_dl{hci}_{jo}")
            g.step("dve", lambda eng, wd=wd: eng.tensor_scalar(
                wd, wd, -128.0, None, ALU.add),
                reads=ek, writes=ek, tag=f"b_dc{hci}_{jo}")

            ps = c.PS_A if jo % 2 == 0 else c.PS_B
            pskeys = keyspan("PS_A" if jo % 2 == 0 else "PS_B", 0, 4)

            def dmm(eng, wd=wd, ps=ps, nh=nh, h0=h0):
                last = None
                for a in range(nh):
                    lhsT = wd[:, a * 128:(a + 1) * 128]
                    for ch in range(4):
                        last = eng.matmul(ps[:, 512 * ch:512 * ch + 512],
                                          lhsT,
                                          bigb(c, a)[:, 512 * ch:512 * ch + 512],
                                          start=(a == 0), stop=(a == nh - 1))
                return last
            g.step("pe", dmm,
                   reads=keyspan("EBUF", 4 * esl, 4) + keyspan("BIGB", 0, 4 * nh),
                   writes=pskeys, tag=f"b_dm{hci}_{jo}")

            if hci == 0:
                g.step("act", lambda eng, ps=ps: eng.copy(c.PANA[:, :], ps[:, :]),
                       reads=pskeys, writes=[("PANA",)], tag=f"b_e0{hci}_{jo}")
                g.step("sp", lambda eng, jo=jo: [
                    eng.dma_start(out=dpart[128 * jo:128 * (jo + 1), :], in_=c.PANA[:, :])],
                    reads=[("PANA",)], writes=[("dpart", jo)], dma="st", n_dma=1,
                    tag=f"b_s0{hci}_{jo}")
            elif hci == 1:
                g.step("act", lambda eng, ps=ps: eng.copy(c.PANA[:, :], ps[:, :]),
                       reads=pskeys, writes=[("PANA",)], tag=f"b_e1{hci}_{jo}")
                g.step("pool", lambda eng, jo=jo: [
                    eng.dma_start(out=c.PANA[:, :],
                                  in_=dpart[128 * jo:128 * (jo + 1), :],
                                  accum_op=ALU.add)],
                    reads=[("dpart", jo), ("PANA",)], writes=[("PANA",)],
                    dma="wt", n_dma=1, tag=f"b_l1{hci}_{jo}")
                g.step("sp", lambda eng, jo=jo: [
                    eng.dma_start(out=dpart[128 * jo:128 * (jo + 1), :], in_=c.PANA[:, :])],
                    reads=[("PANA",)], writes=[("dpart", jo)], dma="st", n_dma=1,
                    tag=f"b_s1{hci}_{jo}")
            else:
                g.step("act", lambda eng, ps=ps: eng.copy(c.PANA[:, :], ps[:, :]),
                       reads=pskeys, writes=[("PANA",)], tag=f"b_e2{hci}_{jo}")
                g.step("pool", lambda eng, jo=jo: [
                    eng.dma_start(out=c.PANA[:, :],
                                  in_=dpart[128 * jo:128 * (jo + 1), :],
                                  accum_op=ALU.add)],
                    reads=[("dpart", jo), ("PANA",)], writes=[("PANA",)],
                    dma="wt", n_dma=1, tag=f"b_l2{hci}_{jo}")
                g.step("sp", lambda eng, jo=jo: [
                    eng.dma_start(out=c.PANB[:, :], in_=xsrc[128 * jo:128 * (jo + 1), :])],
                    reads=[("xm", jo)] + [("xm2", jo, s_) for s_ in range(4)] +
                    [("PANB",)], writes=[("PANB",)],
                    dma="ld", n_dma=1, tag=f"b_lx{hci}_{jo}")
                g.step("dve", lambda eng, jo=jo: eng.scalar_tensor_tensor(
                    c.PANB[:, :], c.PANA[:, :], c.SCOL[:, 88 + jo:89 + jo],
                    c.PANB[:, :], ALU.mult, ALU.add),
                    reads=[("PANA",), ("PANB",), ("SCOL",)], writes=[("PANB",)],
                    tag=f"b_f{hci}_{jo}")
                g.step("sp", lambda eng, jo=jo: [
                    eng.dma_start(out=xm[128 * jo:128 * (jo + 1), :], in_=c.PANB[:, :])],
                    reads=[("PANB",)], writes=[("xm", jo)] +
                    [("xm2", jo, sc_) for sc_ in range(4)],
                    dma="st", n_dma=1, tag=f"b_sx{hci}_{jo}")


def _rope_norm(c, g, t, j, psq_name, dst, dkeys, tagp):
    ps = c.PS_A
    qbf = c.EBUF[:, 0:S]
    g.step("act", lambda eng: eng.copy(qbf, ps[:, :]),
           reads=keyspan("PS_A", 0, 4), writes=keyspan("EBUF", 0, 4),
           tag=f"{tagp}_qb{j}")

    def permm(eng):
        last = None
        for ch in range(4):
            last = eng.matmul(c.PS_B[:, 512 * ch:512 * ch + 512],
                              c.PERM[:, :], qbf[:, 512 * ch:512 * ch + 512],
                              start=True, stop=True)
        return last
    g.step("pe", permm, reads=keyspan("EBUF", 0, 4) + [("perm64",)],
           writes=keyspan("PS_B", 0, 4), tag=f"{tagp}_pm{j}")
    x0 = c.XTI[:, 0:S]
    x1 = c.XTI[:, S:2 * S]
    g.step("dve", lambda eng: eng.tensor_mul(x0, c.PS_B[:, :], c.PANB[:, :]),
           reads=keyspan("PS_B", 0, 4) + [("PANB",)], writes=[("XTI", 0)],
           tag=f"{tagp}_t1{j}")
    g.step("dve", lambda eng: eng.tensor_mul(x1, qbf, c.PANA[:, :]),
           reads=keyspan("EBUF", 0, 4) + [("PANA",)], writes=[("XTI", 1)],
           tag=f"{tagp}_t2{j}")
    g.step("dve", lambda eng: eng.tensor_add(x0, x0, x1),
           reads=[("XTI", 0), ("XTI", 1)], writes=[("XTI", 0)],
           tag=f"{tagp}_rp{j}")
    sq = c.EBUF[:, S:2 * S]
    g.step("act", lambda eng: eng.activation(sq, x0, AF.Square),
           reads=[("XTI", 0)], writes=keyspan("EBUF", 4, 4), tag=f"{tagp}_sq{j}")

    def ssqmm(eng):
        last = None
        for ch in range(4):
            last = eng.matmul(c.PS_B[0:1, 512 * ch:512 * ch + 512],
                              c.ONE[:, 0:1], sq[:, 512 * ch:512 * ch + 512],
                              start=True, stop=True)
        return last
    g.step("pe", ssqmm, reads=keyspan("EBUF", 4, 4) + [("ONE",)],
           writes=keyspan("PS_B", 0, 4), tag=f"{tagp}_sm{j}")
    row = x1[0:1, 0:S]
    g.step("act", lambda eng: eng.activation(
        row, c.PS_B[0:1, :], AF.Ln, scale=1.0 / 128.0,
        bias=c.BCOL[0:1, 31:32]),
        reads=keyspan("PS_B", 0, 4) + [("BCOL", 30)], writes=[("XTI", 1)],
        tag=f"{tagp}_ln{j}")
    g.step("act", lambda eng: eng.activation(row, row, AF.Exp, scale=-0.5),
           reads=[("XTI", 1)], writes=[("XTI", 1)], tag=f"{tagp}_ex{j}")

    g.step("sp", lambda eng: [
        eng.dma_start(out=t["rows_d"][6:7, :], in_=row)],
        reads=[("XTI", 1)], writes=[("rows_d", 6)], dma="st", n_dma=1,
        tag=f"{tagp}_rs{j}")
    g.step("sp", lambda eng: [
        eng.dma_start(out=x1, in_=t["rows_d"][6:7, :].partition_broadcast(128))],
        reads=[("rows_d", 6)], writes=[("XTI", 1)], dma="ld", n_dma=1,
        tag=f"{tagp}_bc{j}")
    g.step("dve", lambda eng: eng.tensor_mul(dst, x0, x1),
           reads=[("XTI", 0), ("XTI", 1)], writes=dkeys,
           tag=f"{tagp}_no{j}")


def build_l1_attn(c, t, xsrc):
    nc, g = c.nc, c.g
    xm, qT_d = t["xm"], t["qT_d"]
    g.step("sp", lambda eng: [
        eng.dma_start(out=c.PANA[:, :], in_=t["l1_c2T"][:, :]),
        eng.dma_start(out=c.PANB[:, :], in_=t["l1_s2T"][:, :])],
        reads=[("PANA",), ("PANB",)], writes=[("PANA",), ("PANB",)],
        dma="ld", n_dma=2, tag="c_rp")
    g.step("pool", lambda eng: [
        eng.dma_start(out=c.PERM[:, :], in_=t["l1_perm64"][:, :]),
        eng.dma_start(out=c.ONES1F[:, :], in_=t["l1_ones1f"][:, :])],
        writes=[("perm64",), ("ones1f",)], dma="wt", n_dma=2,
        tag="c_pml")

    cslots = [(c.WBUF[:, 0:S], keyspan("WBUF", 0, 4)),
              (c.WBUF[:, S:2 * S], keyspan("WBUF", 4, 4)),
              (c.EBUF[:, 3 * S:4 * S], keyspan("EBUF", 12, 4))]
    for j in range(20):
        wtile, kw = cslots[j % 3]
        if j < 16:
            wsrc = t["l1_wq"].rearrange("(a p) (j f) -> j p a f", p=128, f=128)[j]
        else:
            wsrc = t["l1_wk"].rearrange("(a p) (j f) -> j p a f", p=128, f=128)[j - 16]
        wt3 = wtile.rearrange("p (a f) -> p a f", f=128)
        g.step("pool", lambda eng, wsrc=wsrc, wt3=wt3: [
            eng.dma_start(out=wt3, in_=wsrc)],
            writes=kw, dma="wt", n_dma=1, tag=f"c_wl{j}")

        def qmm(eng, wtile=wtile):
            last = None
            for d in range(NT):
                lhsT = wtile[:, d * 128:(d + 1) * 128]
                for ch in range(4):
                    last = eng.matmul(c.PS_A[:, 512 * ch:512 * ch + 512], lhsT,
                                      biga(c, d)[:, 512 * ch:512 * ch + 512],
                                      start=(d == 0), stop=(d == NT - 1))
            return last
        g.step("pe", qmm, reads=kw + keyspan("BIGA", 0, 64),
               writes=keyspan("PS_A", 0, 4), tag=f"c_qm{j}")
        if j < 16:
            dst = c.EBUF[:, 2 * S:3 * S]
            dkeys = keyspan("EBUF", 8, 4)
            _rope_norm(c, g, t, j, "PS_A", dst, dkeys, "c_q")
            g.step("sp", lambda eng, j=j, dst=dst: [
                eng.dma_start(out=qT_d[128 * j:128 * (j + 1), :], in_=dst)],
                reads=dkeys, writes=[("qT_d", j)], dma="st", n_dma=1,
                tag=f"c_qs{j}")
        else:
            gkv = j - 16
            dst = c.BIGB[:, gkv * S:(gkv + 1) * S]
            dkeys = keyspan("BIGB", 4 * gkv, 4)
            _rope_norm(c, g, t, j, "PS_A", dst, dkeys, "c_k")

    wsl = 0
    wtile = c.WBUF[:, 0:512]
    kwv = keyspan("WBUF", 0, 1)
    wv3 = c.QCH[:, :].rearrange("p (a f) -> p a f", f=512)
    g.step("pool", lambda eng: [
        eng.dma_start(out=wv3, in_=t["l1_wv"].rearrange("(a p) f -> p a f", p=128))],
        reads=keyspan("QCH", 0, 16), writes=keyspan("QCH", 0, 16),
        dma="wt", n_dma=1, tag="c_wv")
    for sb in range(16):
        bank = sb % 2
        psv = c.PS_B[:, 512 * bank:512 * bank + 512]
        pkv = ("PS_B", bank)

        def vmm(eng, sb=sb, psv=psv):
            last = None
            for d in range(NT):
                last = eng.matmul(psv, biga(c, d)[:, 128 * sb:128 * (sb + 1)],
                                  c.QCH[:, d * 512:(d + 1) * 512],
                                  start=(d == 0), stop=(d == NT - 1))
            return last
        g.step("pe", vmm, reads=keyspan("BIGA", 0, 64) + keyspan("QCH", 0, 16),
               writes=[pkv], tag=f"c_vm{sb}")
        vdst = c.BIGB[:, (16 + sb) * 512:(17 + sb) * 512]
        g.step("act", lambda eng, psv=psv, vdst=vdst: eng.copy(vdst, psv),
               reads=[pkv], writes=[("BIGB", 16 + sb)], tag=f"c_ve{sb}")

    for jo in range(NT):
        wsrc = t["l1_wo"].rearrange("(a p) (j f) -> j p a f", p=128, f=128)[jo]
        dst3 = biga(c, jo).rearrange("p (a f) -> p a f", f=128)
        g.step("pool", lambda eng, wsrc=wsrc, dst3=dst3: [
            eng.dma_start(out=dst3, in_=wsrc)],
            reads=keyspan("BIGA", 4 * jo, 4), writes=keyspan("BIGA", 4 * jo, 4),
            dma="wt", n_dma=1, tag=f"c_wo{jo}")

    for sc in range(NSC):
        s0 = SC * sc
        nvalid = 4 * (sc + 1)
        qsrc = qT_d.rearrange("(a p) s -> p a s", p=128)[:, :, s0:s0 + SC]
        qdst = c.QCH[:, :].rearrange("p (a s) -> p a s", s=512)
        g.step("sp", lambda eng, qsrc=qsrc, qdst=qdst: [
            eng.dma_start(out=qdst, in_=qsrc)],
            reads=[("qT_d", r) for r in range(16)] + keyspan("QCH", 0, 16),
            writes=keyspan("QCH", 0, 16), dma="ld", n_dma=1, tag=f"c_ql{sc}")
        for h in range(16):
            gkv = h // 4
            qch_h = c.QCH[:, h * SC:(h + 1) * SC]
            for T in range(nvalid):
                bank = T % 4
                psc, pk = c.PS_A[:, 512 * bank:512 * bank + 512], ("PS_A", bank)
                lhsT = c.BIGB[:, gkv * S + 128 * T: gkv * S + 128 * (T + 1)]
                g.step("pe", lambda eng, lhsT=lhsT, psc=psc, qch_h=qch_h:
                       eng.matmul(psc, lhsT, qch_h, start=True, stop=True),
                       reads=keyspan("BIGB", 4 * gkv, 4) + [("QCH", h)],
                       writes=[pk], tag=f"c_s{sc}_{h}_{T}")
                ebt = c.EBUF[:, T * SC:(T + 1) * SC]
                g.step("act", lambda eng, psc=psc, ebt=ebt: eng.activation(
                    ebt, psc, AF.Exp, scale=SCALE),
                    reads=[pk], writes=[("EBUF", T)], tag=f"c_e{sc}_{h}_{T}")
                if T >= 4 * sc:
                    base = -128 * (T - 4 * sc)
                    g.step("pool", lambda eng, ebt=ebt, base=base:
                           eng.affine_select(ebt, ebt, [[1, 512]], ALU.is_ge,
                                             0.0, base=base,
                                             channel_multiplier=-1),
                           reads=[("EBUF", T)], writes=[("EBUF", T)],
                           tag=f"c_m{sc}_{h}_{T}")
            avb = h % 2
            def avmm(eng, gkv=gkv, nvalid=nvalid, avb=avb):
                last = None
                for T in range(nvalid):
                    lhsT = c.BIGB[:, (16 + T) * 512 + 128 * gkv:
                                  (16 + T) * 512 + 128 * (gkv + 1)]
                    last = eng.matmul(c.PS_B[:, 512 * avb:512 * avb + 512], lhsT,
                                      c.EBUF[:, T * SC:(T + 1) * SC],
                                      start=(T == 0), stop=(T == nvalid - 1))
                return last
            g.step("pe", avmm,
                   reads=[("BIGB", 16 + T) for T in range(nvalid)] +
                   keyspan("EBUF", 0, nvalid),
                   writes=[("PS_B", avb)], tag=f"c_av{sc}_{h}")

            def dnmm(eng, nvalid=nvalid):
                last = None
                for T in range(nvalid):
                    last = eng.matmul(c.PS_B[0:1, 1024:1536], c.ONE[:, 0:1],
                                      c.EBUF[:, T * SC:(T + 1) * SC],
                                      start=(T == 0), stop=(T == nvalid - 1))
                return last
            g.step("pe", dnmm, reads=keyspan("EBUF", 0, nvalid) + [("ONE",)],
                   writes=[("PS_B", 2)], tag=f"c_d{sc}_{h}")
            rrow = c.XTI[0:1, 0:512]
            g.step("dve", lambda eng, rrow=rrow: eng.reciprocal(
                rrow, c.PS_B[0:1, 1024:1536]),
                reads=[("PS_B", 2)], writes=[("XTI", 0)], tag=f"c_r{sc}_{h}")
            g.step("sp", lambda eng, rrow=rrow: [
                eng.dma_start(out=t["rows_d"][6:7, 0:512], in_=rrow)],
                reads=[("XTI", 0)], writes=[("rows_d", 6)], dma="st", n_dma=1,
                tag=f"c_rs{sc}_{h}")
            camh = c.CAM[:, 256 * avb:256 * avb + 256]
            pan = c.CAM[:, :] if avb == 0 else c.PANB[:, 0:512]
            pank = ("CAM",) if avb == 0 else ("PANB",)
            g.step("sp", lambda eng, pan=pan: [
                eng.dma_start(out=pan,
                              in_=t["rows_d"][6:7, 0:512].partition_broadcast(128))],
                reads=[("rows_d", 6), pank], writes=[pank], dma="ld", n_dma=1,
                tag=f"c_rb{sc}_{h}")
            g.step("dve", lambda eng, qch_h=qch_h, avb=avb, pan=pan: eng.tensor_mul(
                qch_h, c.PS_B[:, 512 * avb:512 * avb + 512], pan),
                reads=[("PS_B", avb), pank], writes=[("QCH", h)],
                tag=f"c_cn{sc}_{h}")

        for jo in range(NT):
            wtile, kw = biga(c, jo), keyspan("BIGA", 4 * jo, 4)
            psc, pk = c.PS_A[:, 512 * (jo % 2):512 * (jo % 2) + 512], ("PS_A", jo % 2)

            def pmm(eng, wtile=wtile, psc=psc):
                last = None
                for e in range(NT):
                    last = eng.matmul(psc, wtile[:, e * 128:(e + 1) * 128],
                                      c.QCH[:, e * SC:(e + 1) * SC],
                                      start=(e == 0), stop=(e == NT - 1))
                return last
            g.step("pe", pmm, reads=kw + keyspan("QCH", 0, 16),
                   writes=[pk], tag=f"c_pm{sc}_{jo}")
            xsl = jo % 2
            xt = c.XTI[:, xsl * S:xsl * S + 512]
            g.step("sp", lambda eng, jo=jo, xt=xt, s0=s0: [
                eng.dma_start(out=xt, in_=xsrc[128 * jo:128 * (jo + 1), s0:s0 + SC])],
                reads=[("xm2", jo, sc)], writes=[("XTI", xsl)], dma="ld", n_dma=1,
                tag=f"c_rx{sc}_{jo}")
            g.step("dve", lambda eng, xt=xt, psc=psc: eng.tensor_add(xt, xt, psc),
                   reads=[pk, ("XTI", xsl)], writes=[("XTI", xsl)],
                   tag=f"c_ra{sc}_{jo}")
            g.step("sp", lambda eng, jo=jo, xt=xt, s0=s0: [
                eng.dma_start(out=xm[128 * jo:128 * (jo + 1), s0:s0 + SC], in_=xt)],
                reads=[("XTI", xsl)], writes=[("xm2", jo, sc), ("xm", jo)],
                dma="st", n_dma=1, tag=f"c_rs{sc}_{jo}")



def _transpose_pair_group(c, g, pe_srcs, ps, pskeys, rkeys, tag):
    def fn(eng):
        last = None
        n = len(pe_srcs)
        for idx, (lhsT, off) in enumerate(pe_srcs):
            last = eng.matmul(ps[:, off:off + 128], lhsT, c.IDN[:, :],
                              start=(idx == 0), stop=(idx == n - 1))
        return last
    g.step("pe", fn, reads=rkeys + [("IDN",)], writes=pskeys, tag=tag)




def _dbg_dump(c, t, name, src_ap, rkeys, idx, cast_f32=True):
    if name not in t:
        return
    g = c.g
    g.step("sp", lambda eng: [eng.dma_start(out=t[name], in_=src_ap)],
           reads=rkeys, writes=[("dbg", idx)], dma="st", n_dma=1,
           tag=f"dbg_{name}")

def build_l1_ffn(c, t, xsrc):
    nc, g = c.nc, c.g
    yln, xm, outT = t["yln"], t["xm"], t["outT"]
    rows_d = t["rows_d"]
    EPS2 = ("BCOL", 30)

    def U(arena, u, n=1):
        return arena[:, 512 * u:512 * (u + n)]

    QCHU = lambda u, n=1: U(c.QCH, u, n)
    EBU = lambda u, n=1: U(c.EBUF, u, n)

    def BB(u, n=1):
        return c.BIGB[:, 512 * u:512 * (u + n)]

    def psel_f32(T):
        return BB(T).bitcast(F32)

    g.step("pool", lambda eng: [
        eng.dma_start(out=c.IDN[:, :], in_=t["l2_idn"][:, :]),
        eng.dma_start(out=c.SUT[:, :], in_=t["l2_sut"][:, :]),
        eng.dma_start(out=c.ONS[:, :], in_=t["l2_ons"][:, :]),
        eng.dma_start(out=c.SCOL[:, 0:16], in_=t["l2_wr"][:, :]),
        eng.dma_start(out=c.SCOL[:, 16:32], in_=t["l2_bq"][:, :]),
        eng.dma_start(out=c.SCOL[:, 32:48], in_=t["l2_bkv"][:, :]),
        eng.dma_start(out=c.SCOL[:, 48:64], in_=t["l2_ln2c"][:, :])],
        writes=[("IDN",), ("SUT",), ("ONS",), ("SCOL",)], dma="wt", n_dma=7,
        tag="d_cl")
    g.step("dve", lambda eng: eng.tensor_copy(c.MCOL[:, 0:16], c.SCOL[:, 0:16]),
           reads=[("SCOL",)], writes=[("MCOL",)], tag="d_wrb")
    g.step("sp", lambda eng: [
        eng.dma_start(out=c.CAM[:, 256:512],
                      in_=t["l2_iota256"][0:1, :].partition_broadcast(128))],
        reads=[("CAM",)], writes=[("CAM",)], dma="ld", n_dma=1, tag="d_io")

    def rowmm(eng):
        last = None
        for d in range(NT):
            for ch in range(4):
                last = eng.matmul(c.PS_B[0:1, 512 * ch:512 * ch + 512],
                                  c.MCOL[:, d:d + 1],
                                  biga(c, d)[:, 512 * ch:512 * ch + 512],
                                  start=(d == 0), stop=(d == NT - 1))
        return last
    g.step("pe", rowmm, reads=keyspan("BIGA", 0, 64) + [("MCOL",)],
           writes=keyspan("PS_B", 0, 4), tag="d_srm")
    srow = c.XTI[0:1, 0:2048]
    g.step("dve", lambda eng: eng.tensor_copy(srow, c.PS_B[0:1, :]),
           reads=keyspan("PS_B", 0, 4), writes=[("XTI", 0)], tag="d_sre")
    g.step("sp", lambda eng: [
        eng.dma_start(out=rows_d[4:5, :], in_=srow)],
        reads=[("XTI", 0)], writes=[("rows_d", 4)], dma="st", n_dma=1, tag="d_srs")
    g.step("sp", lambda eng: [
        eng.dma_start(out=c.PANA[:, :],
                      in_=rows_d[4:5, :].partition_broadcast(128))],
        reads=[("rows_d", 4), ("PANA",)], writes=[("PANA",)], dma="ld",
        n_dma=1, tag="d_srb")

    def colmm(eng):
        last = None
        for d in range(NT):
            for T in range(NT):
                last = eng.matmul(c.PS_A[:, T:T + 1],
                                  biga(c, d)[:, 128 * T:128 * (T + 1)],
                                  c.MCOL[:, d:d + 1],
                                  start=(d == 0 and T == 0),
                                  stop=(d == NT - 1 and T == NT - 1))
        return last
    g.step("pe", colmm, reads=keyspan("BIGA", 0, 64) + [("MCOL",)],
           writes=[("PS_A", 0)], tag="d_scm")
    g.step("dve", lambda eng: eng.tensor_copy(c.SCOL[:, 64:80], c.PS_A[:, 0:16]),
           reads=[("PS_A", 0)], writes=[("SCOL",)], tag="d_sce")

    trash = c.XTI[:, S:2 * S]
    for T in range(NT):
        g.step("dve", lambda eng, T=T: eng.tensor_scalar(
            trash, c.PANA[:, :], c.SCOL[:, 64 + T:65 + T], 0.0, ALU.is_gt,
            ALU.add, accum_out=c.SCOL[:, 80 + T:81 + T]),
            reads=[("PANA",), ("SCOL",)], writes=[("XTI", 1), ("SCOL",)],
            tag=f"d_rk{T}")
    g.step("dve", lambda eng: eng.tensor_scalar(
        c.SCOL[:, 96:112], c.SCOL[:, 80:96], 256.0, None, ALU.is_lt),
        reads=[("SCOL",)], writes=[("SCOL",)], tag="d_msk")
    g.step("dve", lambda eng: eng.tensor_copy(c.MCOL[:, 0:16], c.SCOL[:, 96:112]),
           reads=[("SCOL",)], writes=[("MCOL",)], tag="d_mbf")

    def cummm(eng):
        last = None
        for T in range(NT):
            for Uu in range(T + 1):
                lhsT = c.SUT[:, :] if Uu == T else c.ONS[:, :]
                last = eng.matmul(c.PS_B[:, T:T + 1], lhsT, c.MCOL[:, Uu:Uu + 1],
                                  start=(T == 0 and Uu == 0),
                                  stop=(T == NT - 1 and Uu == NT - 1))
        return last
    g.step("pe", cummm, reads=[("MCOL",), ("SUT",), ("ONS",)],
           writes=[("PS_B", 0)], tag="d_cum")
    g.step("dve", lambda eng: eng.tensor_copy(c.SCOL[:, 112:128], c.PS_B[:, 0:16]),
           reads=[("PS_B", 0)], writes=[("SCOL",)], tag="d_cue")
    _dbg_dump(c, t, "dbg_scol", c.SCOL[:, :], [("SCOL",)], 0)

    for T in range(NT):
        g.step("dve", lambda eng, T=T: eng.tensor_scalar(
            psel_f32(T), c.CAM[:, 256:512], c.SCOL[:, 112 + T:113 + T],
            c.SCOL[:, 96 + T:97 + T], ALU.is_equal, ALU.mult),
            reads=[("CAM",), ("SCOL",)], writes=[("BIGB", T)], tag=f"d_ps{T}")
        g.step("dve", lambda eng, T=T: eng.tensor_copy(BB(16 + T)[:, 0:256],
                                                       psel_f32(T)),
               reads=[("BIGB", T)], writes=[("BIGB", 16 + T)], tag=f"d_pb{T}")
        g.step("sp", lambda eng, T=T: [
            eng.dma_start(out=t["psel_d"][128 * T:128 * (T + 1), :],
                          in_=BB(16 + T)[:, 0:256])],
            reads=[("BIGB", 16 + T)], writes=[("psel_d", T)], dma="st", n_dma=1,
            tag=f"d_pss{T}")

    ct3 = c.PANB[:, :].rearrange("p (a f) -> p a f", f=128)
    g.step("sp", lambda eng: [
        eng.dma_start(out=ct3,
                      in_=t["l2_ctabd"].rearrange("(a p) f -> p a f", p=128))],
        reads=[("PANB",)], writes=[("PANB",)], dma="ld", n_dma=1, tag="d_ct")
    st3 = c.XTI[:, S:2 * S].rearrange("p (a f) -> p a f", f=128)
    g.step("sp", lambda eng: [
        eng.dma_start(out=st3,
                      in_=t["l2_stabd"].rearrange("(a p) f -> p a f", p=128))],
        reads=[("XTI", 1)], writes=[("XTI", 1)], dma="ld", n_dma=1, tag="d_st")

    def cgmm(eng):
        last = None
        for T in range(NT):
            last = eng.matmul(c.PS_A[:, 0:256], c.PANB[:, 128 * T:128 * (T + 1)],
                              psel_f32(T), start=(T == 0), stop=(T == NT - 1))
        return last
    g.step("pe", cgmm, reads=[("PANB",)] + keyspan("BIGB", 0, 16),
           writes=[("PS_A", 0)], tag="d_cg")

    def sgmm(eng):
        last = None
        for T in range(NT):
            last = eng.matmul(c.PS_A[:, 512:768],
                              c.XTI[:, S + 128 * T:S + 128 * (T + 1)],
                              psel_f32(T), start=(T == 0), stop=(T == NT - 1))
        return last
    g.step("pe", sgmm, reads=[("XTI", 1)] + keyspan("BIGB", 0, 16),
           writes=[("PS_A", 1)], tag="d_sg")
    g.step("dve", lambda eng: eng.tensor_copy(c.CAM[:, 0:256], c.PS_A[:, 0:256]),
           reads=[("PS_A", 0)], writes=[("CAM",)], tag="d_cge")
    g.step("dve", lambda eng: eng.tensor_copy(c.CAM[:, 256:512], c.PS_A[:, 512:768]),
           reads=[("PS_A", 1), ("CAM",)], writes=[("CAM",)], tag="d_sge")

    def ssmm(eng):
        last = None
        for T in range(NT):
            last = eng.matmul(c.PS_B[0:1, 0:256], c.SCOL[:, 64 + T:65 + T],
                              psel_f32(T), start=(T == 0), stop=(T == NT - 1))
        return last
    g.step("pe", ssmm, reads=[("SCOL",)] + keyspan("BIGB", 0, 16),
           writes=[("PS_B", 0)], tag="d_ss")
    gprow = c.XTI[0:1, 0:256]
    g.step("act", lambda eng: eng.activation(gprow, c.PS_B[0:1, 0:256], AF.Sigmoid),
           reads=[("PS_B", 0)], writes=[("XTI", 0)], tag="d_gp")
    g.step("sp", lambda eng: [
        eng.dma_start(out=rows_d[5:6, 0:256], in_=gprow)],
        reads=[("XTI", 0)], writes=[("rows_d", 5)], dma="st", n_dma=1, tag="d_gps")

    for T in range(NT):
        g.step("sp", lambda eng, T=T: [
            eng.dma_start(out=biga(c, T), in_=yln[0:2048, 128 * T:128 * (T + 1)],
                          transpose=True)],
            reads=[("yd", i) for i in range(NT)] + keyspan("BIGA", 4 * T, 4),
            writes=keyspan("BIGA", 4 * T, 4), dma="tp", n_dma=1, tag=f"d_yt{T}")

    for pc in range(2):
        for dch in range(4):
            psx, pkx = c.PS_B[:, 512 * (dch % 2):512 * (dch % 2) + 512], ("PS_B", dch % 2)

            def gmm(eng, pc=pc, dch=dch, psx=psx):
                last = None
                for T in range(NT):
                    last = eng.matmul(psx, BB(16 + T)[:, 128 * pc:128 * (pc + 1)],
                                      biga(c, T)[:, 512 * dch:512 * (dch + 1)],
                                      start=(T == 0), stop=(T == NT - 1))
                return last
            g.step("pe", gmm,
                   reads=keyspan("BIGB", 16, 16) + keyspan("BIGA", 0, 64),
                   writes=[pkx], tag=f"d_gm{pc}_{dch}")
            dst = QCHU(4 * pc + dch)
            g.step("act", lambda eng, psx=psx, dst=dst: eng.copy(dst, psx),
                   reads=[pkx], writes=[("QCH", 4 * pc + dch)], tag=f"d_ge{pc}_{dch}")

    if "dbg_sel" in t:
        g.step("sp", lambda eng: [
            eng.dma_start(out=t["dbg_sel"][0:128, :], in_=QCHU(0, 4)),
            eng.dma_start(out=t["dbg_sel"][128:256, :], in_=QCHU(4, 4))],
            reads=keyspan("QCH", 0, 8), writes=[("dbg", 1)], dma="st", n_dma=2,
            tag="dbg_sel")

    g.step("sp", lambda eng: [
        eng.dma_start(out=c.PANB[:, :],
                      in_=t["l2_ln1r"][0:1, :].partition_broadcast(128))],
        reads=[("PANB",)], writes=[("PANB",)], dma="ld", n_dma=1, tag="d_l1p")
    for pc in range(2):
        stile = QCHU(4 * pc, 4)
        scr = EBU(0, 4)
        g.step("act", lambda eng, stile=stile, scr=scr, pc=pc: eng.activation(
            scr, stile, AF.Square, accum_out=c.SCOL[:, 128 + pc:129 + pc]),
            reads=keyspan("QCH", 4 * pc, 4), writes=keyspan("EBUF", 0, 4) + [("SCOL",)],
            tag=f"d_hs{pc}")
        g.step("act", lambda eng, pc=pc: eng.activation(
            c.SCOL[:, 130 + pc:131 + pc], c.SCOL[:, 128 + pc:129 + pc], AF.Ln,
            scale=1.0 / D, bias=c.BCOL[0:128, 31:32]),
            reads=[("SCOL",), EPS2], writes=[("SCOL",)], tag=f"d_hl{pc}")
        g.step("act", lambda eng, pc=pc: eng.activation(
            c.SCOL[:, 130 + pc:131 + pc], c.SCOL[:, 130 + pc:131 + pc],
            AF.Exp, scale=-0.5),
            reads=[("SCOL",)], writes=[("SCOL",)], tag=f"d_he{pc}")
        tmp = c.XTI[:, S:2 * S]
        g.step("dve", lambda eng, stile=stile, tmp=tmp, pc=pc: eng.tensor_scalar(
            tmp, stile, c.SCOL[:, 130 + pc:131 + pc], None, ALU.mult),
            reads=keyspan("QCH", 4 * pc, 4) + [("SCOL",)], writes=[("XTI", 1)],
            tag=f"d_hm{pc}")
        htile = QCHU(8 + 4 * pc, 4)
        g.step("dve", lambda eng, tmp=tmp, htile=htile: eng.tensor_mul(
            htile, tmp, c.PANB[:, :]),
            reads=[("XTI", 1), ("PANB",)], writes=keyspan("QCH", 8 + 4 * pc, 4),
            tag=f"d_hw{pc}")

    for i2 in range(8):
        srcs = []
        rkeys = []
        for sub in range(2):
            i = 2 * i2 + sub
            for pc in range(2):
                lhsT = QCHU(8 + 4 * pc, 4)[:, 128 * i:128 * (i + 1)]
                srcs.append((lhsT, 256 * sub + 128 * pc))
                rkeys.append(("QCH", 8 + 4 * pc + i // 4))
        ps, pk = c.PS_A[:, 0:512], ("PS_A", 0)
        _transpose_pair_group(c, g, srcs, ps, [pk], list(set(rkeys)), f"d_ht{i2}")
        dst = EBU(i2)
        g.step("dve", lambda eng, ps=ps, dst=dst: eng.tensor_copy(dst, ps),
               reads=[pk], writes=[("EBUF", i2)], tag=f"d_hte{i2}")

    def hT(i):
        return EBU(i // 2)[:, 256 * (i % 2):256 * (i % 2) + 256]

    xti_bf = c.XTI[:, 0:S].bitcast(BF16)
    dslots = [(c.WBUF[:, 0:S], keyspan("WBUF", 0, 4)),
              (c.WBUF[:, S:2 * S], keyspan("WBUF", 4, 4)),
              (xti_bf[:, 0:2048], [("XTI", 0)]),
              (xti_bf[:, 2048:4096], [("XTI", 0)])]
    for j in range(32):
        wtile, kw = dslots[j % 4]
        if j < 16:
            wsrc = t["l2_wq"].rearrange("(a p) (j f) -> j p a f", p=128, f=128)[j]
        elif j < 24:
            wsrc = t["l2_wk"].rearrange("(a p) (j f) -> j p a f", p=128, f=128)[j - 16]
        else:
            wsrc = t["l2_wv"].rearrange("(a p) (j f) -> j p a f", p=128, f=128)[j - 24]
        wt3 = wtile.rearrange("p (a f) -> p a f", f=128)
        g.step("pool", lambda eng, wsrc=wsrc, wt3=wt3: [
            eng.dma_start(out=wt3, in_=wsrc)],
            writes=kw, dma="wt", n_dma=1, tag=f"d_wl{j}")
        bank = j % 4
        psq, pkq = c.PS_A[:, 512 * bank:512 * bank + 256], ("PS_A", bank)

        def qmm(eng, wtile=wtile, psq=psq):
            last = None
            for d in range(NT):
                last = eng.matmul(psq, wtile[:, 128 * d:128 * (d + 1)], hT(d),
                                  start=(d == 0), stop=(d == NT - 1))
            return last
        g.step("pe", qmm, reads=kw + keyspan("EBUF", 0, 8), writes=[pkq],
               tag=f"d_qm{j}")
        if j < 16:
            bcol = c.SCOL[:, 16 + j:17 + j]
        elif j < 24:
            bcol = c.SCOL[:, 32 + (j - 16):33 + (j - 16)]
        else:
            bcol = c.SCOL[:, 40 + (j - 24):41 + (j - 24)]
        qbf = c.QBF[:, 256 * (j % 2):256 * (j % 2) + 256]
        qbfk = ("QBF", j % 2)
        g.step("act", lambda eng, psq=psq, qbf=qbf, bcol=bcol: eng.activation(
            qbf, psq, AF.Identity, bias=bcol),
            reads=[pkq, ("SCOL",)], writes=[qbfk],
            tag=f"d_qb{j}")
        if j < 24:
            psw, pkw = c.PS_B[:, 512 * (bank % 2):512 * (bank % 2) + 256], ("PS_B", bank % 2)
            g.step("pe", lambda eng, qbf=qbf, psw=psw: eng.matmul(
                psw, c.PERM[:, :], qbf, start=True, stop=True),
                reads=[qbfk, ("IDN",), ("perm64",)], writes=[pkw],
                tag=f"d_qp{j}")
            t1 = c.XTI[:, S + 512 * bank:S + 512 * bank + 256]
            g.step("dve", lambda eng, psw=psw, t1=t1: eng.tensor_mul(
                t1, psw, c.CAM[:, 256:512]),
                reads=[pkw, ("CAM",)], writes=[("XTIr", bank)], tag=f"d_q1{j}")
            t2 = c.XTI[:, S + 512 * bank + 256:S + 512 * bank + 512]
            g.step("dve", lambda eng, qbf=qbf, t2=t2: eng.tensor_mul(
                t2, qbf, c.CAM[:, 0:256]),
                reads=[qbfk, ("CAM",)], writes=[("XTIr", bank)],
                tag=f"d_q2{j}")
            if j < 16:
                dst = EBU(8 + j // 2)[:, 256 * (j % 2):256 * (j % 2) + 256]
                dkey = ("EBUF", 8 + j // 2)
            else:
                jk = j - 16
                dst = QCHU(8 + jk // 2)[:, 256 * (jk % 2):256 * (jk % 2) + 256]
                dkey = ("QCH", 8 + jk // 2)
            g.step("dve", lambda eng, t1=t1, t2=t2, dst=dst: eng.tensor_add(
                dst, t1, t2),
                reads=[("XTIr", bank)], writes=[dkey], tag=f"d_qr{j}")
        else:
            jv = j - 24
            dst = QCHU(12 + jv // 2)[:, 256 * (jv % 2):256 * (jv % 2) + 256]
            g.step("dve", lambda eng, qbf=qbf, dst=dst: eng.tensor_copy(dst, qbf),
                   reads=[qbfk], writes=[("QCH", 12 + jv // 2)],
                   tag=f"d_vc{j}")

    if "dbg_q" in t:
        def qT_(h):
            return c.EBUF[:, 512 * (8 + h // 2) + 256 * (h % 2):
                          512 * (8 + h // 2) + 256 * (h % 2) + 256]
        g.step("sp", lambda eng: [
            eng.dma_start(out=t["dbg_q"][128 * h:128 * (h + 1), :], in_=qT_(h))
            for h in range(16)],
            reads=[("EBUF", 8 + u) for u in range(8)], writes=[("dbg", 2)],
            dma="st", n_dma=16, tag="dbg_q")
    build_l1_ffn_tail(c, t, xsrc)


def build_l1_ffn_tail(c, t, xsrc):
    nc, g = c.nc, c.g
    yln, xm, outT, rows_d = t["yln"], t["xm"], t["outT"], t["rows_d"]

    def BB(u, n=1):
        return c.BIGB[:, 512 * u:512 * (u + n)]

    QCHU = lambda u, n=1: c.QCH[:, 512 * u:512 * (u + n)]
    EBU = lambda u, n=1: c.EBUF[:, 512 * u:512 * (u + n)]

    def qT(h):
        return EBU(8 + h // 2)[:, 256 * (h % 2):256 * (h % 2) + 256]

    for pc in range(2):
        for half in range(2):
            srcs, rkeys = [], []
            for sub in range(4):
                jv = 4 * half + sub
                lhsT = QCHU(12 + jv // 2)[:, 256 * (jv % 2) + 128 * pc:
                                          256 * (jv % 2) + 128 * (pc + 1)]
                srcs.append((lhsT, 128 * sub))
                rkeys.append(("QCH", 12 + jv // 2))
            ps, pk = c.PS_A[:, 1024:1536], ("PS_A", 2)
            _transpose_pair_group(c, g, srcs, ps, [pk], list(set(rkeys)),
                                  f"d_vt{pc}_{half}")
            vst = c.QBF[:, 0:512]
            g.step("dve", lambda eng, ps=ps, vst=vst: eng.tensor_copy(vst, ps),
                   reads=[pk], writes=[("QBF", 0), ("QBF", 1)],
                   tag=f"d_ve{pc}_{half}")
            g.step("sp", lambda eng, pc=pc, half=half, vst=vst: [
                eng.dma_start(out=t["vin"][128 * pc:128 * (pc + 1),
                                           512 * half:512 * (half + 1)], in_=vst)],
                reads=[("QBF", 0), ("QBF", 1)], writes=[("vin", pc, half)],
                dma="st", n_dma=1, tag=f"d_vs{pc}_{half}")

    for jk in range(8):
        ksrc = QCHU(8 + jk // 2)[:, 256 * (jk % 2):256 * (jk % 2) + 256]
        g.step("sp", lambda eng, jk=jk, ksrc=ksrc: [
            eng.dma_start(out=t["kin"][128 * jk:128 * (jk + 1), :], in_=ksrc)],
            reads=[("QCH", 8 + jk // 2)], writes=[("kin", jk)], dma="st", n_dma=1,
            tag=f"d_ks{jk}")

    g.step("pool", lambda eng: [eng.collective_compute(
        "AllGather", ALU.bypass, replica_groups=[list(range(8))],
        ins=[t["kin"].opt()],
        outs=[t["kout"].opt()])],
        reads=[("kin", jk) for jk in range(8)], writes=[("kout",)],
        dma="cc", n_dma=1, tag="d_agk")
    g.step("pool", lambda eng: [eng.collective_compute(
        "AllGather", ALU.bypass, replica_groups=[list(range(8))],
        ins=[t["vin"].opt()], outs=[t["vout"].opt()])],
        reads=[("vin", pc, hf) for pc in range(2) for hf in range(2)],
        writes=[("vout",)], dma="cc", n_dma=1, tag="d_agv")

    for g2 in range(8):
        dst3 = BB(4 * g2, 4).rearrange("p (r f) -> p r f", f=256)
        src3 = t["kout"].rearrange("(r g e) f -> g e r f", g=8, e=128)[g2]
        g.step("sp", lambda eng, dst3=dst3, src3=src3: [
            eng.dma_start(out=dst3, in_=src3)],
            reads=[("kout",)] + keyspan("BIGB", 4 * g2, 4),
            writes=keyspan("BIGB", 4 * g2, 4), dma="ld", n_dma=1, tag=f"d_kg{g2}")
    for T in range(NT):
        g.step("sp", lambda eng, T=T: [
            eng.dma_start(out=BB(32 + 2 * T, 2),
                          in_=t["vout"][128 * T:128 * (T + 1), :])],
            reads=[("vout",)] + keyspan("BIGB", 32 + 2 * T, 2),
            writes=keyspan("BIGB", 32 + 2 * T, 2), dma="ld", n_dma=1,
            tag=f"d_vg{T}")
    mview = c.XTI[:, 0:S].bitcast(BF16)
    mview3 = mview.rearrange("p (a f) -> p a f", f=256)
    g.step("sp", lambda eng: [
        eng.dma_start(out=mview3, in_=t["l2_mask"].rearrange(
            "(a p) f -> p a f", p=128))],
        reads=[("XTI", 0)], writes=[("XTI", 0)], dma="ld", n_dma=1, tag="d_mk")

    def mtile(T):
        return mview[:, 256 * T:256 * (T + 1)]

    for h in range(16):
        g2 = h // 2
        for gw in range(2):
            psw_t = c.PS_A if gw == 0 else c.PS_B
            pswk = keyspan("PS_A" if gw == 0 else "PS_B", 0, 4)

            def dsmm(eng, g2=g2, h=h, gw=gw, psw_t=psw_t):
                last = None
                for k8 in range(8):
                    T = 8 * gw + k8
                    last = eng.matmul(psw_t[:, 256 * k8:256 * k8 + 256],
                                      BB(4 * g2, 4)[:, 128 * T:128 * (T + 1)],
                                      qT(h), start=(k8 % 2 == 0),
                                      stop=(k8 % 2 == 1))
                return last
            g.step("pe", dsmm,
                   reads=keyspan("BIGB", 4 * g2, 4) + [("EBUF", 8 + h // 2)],
                   writes=pswk, tag=f"d_as{h}_{gw}")
            ebt = EBU(4 * gw, 4)
            g.step("act", lambda eng, ebt=ebt, psw_t=psw_t: eng.activation(
                ebt, psw_t[:, :], AF.Exp, scale=SCALE),
                reads=pswk, writes=keyspan("EBUF", 4 * gw, 4),
                tag=f"d_ae{h}_{gw}")
            g.step("dve", lambda eng, ebt=ebt, gw=gw: eng.tensor_mul(
                ebt, ebt, mview[:, 2048 * gw:2048 * (gw + 1)]),
                reads=keyspan("EBUF", 4 * gw, 4) + [("XTI", 0)],
                writes=keyspan("EBUF", 4 * gw, 4),
                tag=f"d_am{h}_{gw}")

        def avmm(eng, g2=g2):
            last = None
            for T in range(NT):
                ebt = EBU(T // 2)[:, 256 * (T % 2):256 * (T % 2) + 256]
                lhsT = BB(32 + 2 * T, 2)[:, 128 * g2:128 * (g2 + 1)]
                last = eng.matmul(c.PS_B[:, 1536:1792], lhsT, ebt,
                                  start=(T == 0), stop=(T == NT - 1))
            return last
        g.step("pe", avmm,
               reads=keyspan("EBUF", 0, 8) + keyspan("BIGB", 32, 32),
               writes=[("PS_B", 3)], tag=f"d_aa{h}")

        def dnmm(eng):
            last = None
            for T in range(NT):
                ebt = EBU(T // 2)[:, 256 * (T % 2):256 * (T % 2) + 256]
                last = eng.matmul(c.PS_B[0:1, 1024:1280], c.ONE[:, 0:1], ebt,
                                  start=(T == 0), stop=(T == NT - 1))
            return last
        g.step("pe", dnmm, reads=keyspan("EBUF", 0, 8) + [("ONE",)],
               writes=[("PS_B", 2)], tag=f"d_ad{h}")
        rrow = c.XTI[0:1, S:S + 256]
        g.step("dve", lambda eng, rrow=rrow: eng.reciprocal(
            rrow, c.PS_B[0:1, 1024:1280]),
            reads=[("PS_B", 2)], writes=[("XTI", 1), ("XTIr", 0)], tag=f"d_ar{h}")
        g.step("sp", lambda eng, rrow=rrow: [
            eng.dma_start(out=t["rows_d"][7:8, 0:256], in_=rrow)],
            reads=[("XTI", 1)], writes=[("rows_d", 7)], dma="st", n_dma=1,
            tag=f"d_ars{h}")
        pan = c.CAM[:, 256 * (h % 2):256 * (h % 2) + 256]
        g.step("sp", lambda eng, pan=pan: [
            eng.dma_start(out=pan,
                          in_=t["rows_d"][7:8, 0:256].partition_broadcast(128))],
            reads=[("rows_d", 7), ("CAM",)], writes=[("CAM",)], dma="ld",
            n_dma=1, tag=f"d_ab{h}")
        g.step("dve", lambda eng, h=h, pan=pan: eng.tensor_mul(
            qT(h), c.PS_B[:, 1536:1792], pan),
            reads=[("PS_B", 3), ("CAM",)], writes=[("EBUF", 8 + h // 2)],
            tag=f"d_an{h}")

    if "dbg_ctx" in t:
        g.step("sp", lambda eng: [
            eng.dma_start(out=t["dbg_ctx"][128 * h:128 * (h + 1), :], in_=qT(h))
            for h in range(16)],
            reads=[("EBUF", 8 + u) for u in range(8)], writes=[("dbg", 3)],
            dma="st", n_dma=16, tag="dbg_ctx")

    oslots = [(c.WBUF[:, 0:S], keyspan("WBUF", 0, 4)),
              (c.WBUF[:, S:2 * S], keyspan("WBUF", 4, 4)),
              (BB(0, 4), keyspan("BIGB", 0, 4)),
              (BB(4, 4), keyspan("BIGB", 4, 4))]
    for jo in range(NT):
        wtile, kw = oslots[jo % 4]
        wsrc = t["l2_wo"].rearrange("(a p) (j f) -> j p a f", p=128, f=128)[jo]
        wt3 = wtile.rearrange("p (a f) -> p a f", f=128)
        g.step("pool", lambda eng, wsrc=wsrc, wt3=wt3: [
            eng.dma_start(out=wt3, in_=wsrc)],
            writes=kw, dma="wt", n_dma=1, tag=f"d_ow{jo}")
        bank = jo % 4
        pso, pko = c.PS_A[:, 512 * bank:512 * bank + 256], ("PS_A", bank)

        def omm(eng, wtile=wtile, pso=pso):
            last = None
            for e in range(NT):
                last = eng.matmul(pso, wtile[:, 128 * e:128 * (e + 1)], qT(e),
                                  start=(e == 0), stop=(e == NT - 1))
            return last
        g.step("pe", omm, reads=kw + keyspan("EBUF", 8, 8), writes=[pko],
               tag=f"d_om{jo}")
        adst = QCHU(8 + jo // 2)[:, 256 * (jo % 2):256 * (jo % 2) + 256]
        g.step("act", lambda eng, pso=pso, adst=adst: eng.copy(adst, pso),
               reads=[pko], writes=[("QCH", 8 + jo // 2)], tag=f"d_oe{jo}")

    def attnT(jo):
        return QCHU(8 + jo // 2)[:, 256 * (jo % 2):256 * (jo % 2) + 256]

    if "dbg_attn" in t:
        g.step("sp", lambda eng: [
            eng.dma_start(out=t["dbg_attn"][128 * jo:128 * (jo + 1), :],
                          in_=attnT(jo)) for jo in range(16)],
            reads=[("QCH", 8 + u) for u in range(8)], writes=[("dbg", 4)],
            dma="st", n_dma=16, tag="dbg_attn")

    for jo2 in range(8):
        srcs, rkeys = [], []
        for sub in range(2):
            jo = 2 * jo2 + sub
            for pc in range(2):
                lhsT = QCHU(4 * pc, 4)[:, 128 * jo:128 * (jo + 1)]
                srcs.append((lhsT, 256 * sub + 128 * pc))
                rkeys.append(("QCH", 4 * pc + jo // 4))
        ps, pk = c.PS_B[:, 1024:1536], ("PS_B", 2)
        _transpose_pair_group(c, g, srcs, ps, [pk], list(set(rkeys)),
                              f"d_xt{jo2}")
        for sub in range(2):
            jo = 2 * jo2 + sub
            x1t = c.XTI[:, 256 * jo:256 * (jo + 1)]
            xw = [("XTI", 0)] if jo < 8 else [("XTI", 1), ("XTIr", (jo - 8) // 2)]
            g.step("dve", lambda eng, ps=ps, sub=sub, x1t=x1t, jo=jo:
                   eng.tensor_add(x1t, ps[:, 256 * sub:256 * (sub + 1)],
                                  attnT(jo)),
                   reads=[pk, ("QCH", 8 + jo // 2)],
                   writes=xw,
                   tag=f"d_x1{jo}")

    def x1T(jo):
        return c.XTI[:, 256 * jo:256 * (jo + 1)]

    for jo in range(NT):
        scr = BB(24 + jo // 2)[:, 256 * (jo % 2):256 * (jo % 2) + 256]
        g.step("act", lambda eng, jo=jo, scr=scr: eng.activation(
            scr, x1T(jo), AF.Square),
            reads=[("XTI", 0) if jo < 8 else ("XTI", 1)],
            writes=[("BIGB", 24 + jo // 2)], tag=f"d_2s{jo}")

    def sq2mm(eng):
        last = None
        for jo in range(NT):
            scr = BB(24 + jo // 2)[:, 256 * (jo % 2):256 * (jo % 2) + 256]
            last = eng.matmul(c.PS_B[0:1, 1024:1280], c.ONE[:, 0:1], scr,
                              start=(jo == 0), stop=(jo == NT - 1))
        return last
    g.step("pe", sq2mm, reads=[("BIGB", 24 + u) for u in range(8)] + [("ONE",)],
           writes=[("PS_B", 2)], tag="d_2m")
    r2 = c.CAM[0:1, 256:512]
    g.step("act", lambda eng: eng.activation(
        r2, c.PS_B[0:1, 1024:1280], AF.Ln, scale=1.0 / D,
        bias=c.BCOL[0:1, 31:32]),
        reads=[("PS_B", 2), ("BCOL", 30)], writes=[("CAM",)], tag="d_2l")
    g.step("act", lambda eng: eng.activation(r2, r2, AF.Exp, scale=-0.5),
           reads=[("CAM",)], writes=[("CAM",)], tag="d_2e")
    g.step("sp", lambda eng: [
        eng.dma_start(out=rows_d[7:8, 0:256], in_=r2[0:1, 0:256])],
        reads=[("CAM",)], writes=[("rows_d", 7)], dma="st", n_dma=1, tag="d_2bs")
    g.step("sp", lambda eng: [
        eng.dma_start(out=c.CAM[:, 0:256],
                      in_=rows_d[7:8, 0:256].partition_broadcast(128))],
        reads=[("rows_d", 7), ("CAM",)], writes=[("CAM",)], dma="ld", n_dma=1,
        tag="d_2c")
    for jo in range(NT):
        tmp = c.QBF[:, 0:256]
        g.step("dve", lambda eng, jo=jo, tmp=tmp: eng.tensor_mul(
            tmp, x1T(jo), c.CAM[:, 0:256]),
            reads=[("XTI", 0) if jo < 8 else ("XTI", 1), ("CAM",)],
            writes=[("QBF", 0)], tag=f"d_2t{jo}")
        h2d = EBU(jo // 2)[:, 256 * (jo % 2):256 * (jo % 2) + 256]
        g.step("dve", lambda eng, jo=jo, tmp=tmp, h2d=h2d: eng.tensor_scalar(
            h2d, tmp, c.SCOL[:, 48 + jo:49 + jo], None, ALU.mult),
            reads=[("QBF", 0), ("SCOL",)], writes=[("EBUF", jo // 2)],
            tag=f"d_2w{jo}")

    if "dbg_h2" in t:
        g.step("sp", lambda eng: [
            eng.dma_start(out=t["dbg_h2"][128 * jo:128 * (jo + 1), :],
                          in_=EBU(jo // 2)[:, 256 * (jo % 2):256 * (jo % 2) + 256])
            for jo in range(16)],
            reads=[("EBUF", u) for u in range(8)], writes=[("dbg", 5)],
            dma="st", n_dma=16, tag="dbg_h2")

    def h2T(d):
        return EBU(d // 2)[:, 256 * (d % 2):256 * (d % 2) + 256]

    for jo in range(NT):
        g.step("sp", lambda eng, jo=jo: [
            eng.dma_start(out=t["h2in"][128 * jo:128 * (jo + 1), :], in_=h2T(jo))],
            reads=[("EBUF", jo // 2)], writes=[("h2in", jo)], dma="st", n_dma=1,
            tag=f"d_h2s{jo}")
    g.step("pool", lambda eng: [eng.collective_compute(
        "AllGather", ALU.bypass, replica_groups=[list(range(8))],
        ins=[t["h2in"].opt()], outs=[t["h2out"].opt()])],
        reads=[("h2in", jo) for jo in range(NT)], writes=[("h2out",)],
        dma="cc", n_dma=1, tag="d_agh")
    for dd in range(NT):
        dst3 = biga(c, dd).rearrange("p (r f) -> p r f", f=256)
        src3 = t["h2out"].rearrange("(r a p) f -> a p r f", r=8, p=128)[dd]
        g.step("sp", lambda eng, dst3=dst3, src3=src3: [
            eng.dma_start(out=dst3, in_=src3)],
            reads=[("h2out",)] + keyspan("BIGA", 4 * dd, 4),
            writes=keyspan("BIGA", 4 * dd, 4), dma="ld", n_dma=1,
            tag=f"d_h2l{dd}")

    gslots = [(c.WBUF[:, 0:S], keyspan("WBUF", 0, 4)),
              (BB(0, 4), keyspan("BIGB", 0, 4)),
              (BB(8, 4), keyspan("BIGB", 8, 4))]
    uslots = [(c.WBUF[:, S:2 * S], keyspan("WBUF", 4, 4)),
              (BB(4, 4), keyspan("BIGB", 4, 4)),
              (BB(12, 4), keyspan("BIGB", 12, 4))]
    for il in range(8):
        wtg, kwg = gslots[il % 3]
        gsrc = t["l2_wgs"].rearrange("(a p) (j f) -> j p a f", p=128, f=128)[il]
        wt3 = wtg.rearrange("p (a f) -> p a f", f=128)
        g.step("pool", lambda eng, gsrc=gsrc, wt3=wt3: [
            eng.dma_start(out=wt3, in_=gsrc)],
            writes=kwg, dma="wt", n_dma=1, tag=f"d_gw{il}")
        wtu, kwu = uslots[il % 3]
        usrc = t["l2_wus"].rearrange("(a p) (j f) -> j p a f", p=128, f=128)[il]
        wt3u = wtu.rearrange("p (a f) -> p a f", f=128)
        g.step("pool", lambda eng, usrc=usrc, wt3u=wt3u: [
            eng.dma_start(out=wt3u, in_=usrc)],
            writes=kwu, dma="wt", n_dma=1, tag=f"d_uw{il}")
        psg = c.PS_A if il % 2 == 0 else c.PS_B
        psgn = "PS_A" if il % 2 == 0 else "PS_B"

        def gmm(eng, wtg=wtg, psg=psg):
            last = None
            for d in range(NT):
                lhsT = wtg[:, 128 * d:128 * (d + 1)]
                for ch in range(4):
                    last = eng.matmul(psg[:, 512 * ch:512 * ch + 512], lhsT,
                                      biga(c, d)[:, 512 * ch:512 * ch + 512],
                                      start=(d == 0), stop=(d == NT - 1))
            return last
        g.step("pe", gmm, reads=kwg + keyspan("BIGA", 0, 64),
               writes=keyspan(psgn, 0, 4), tag=f"d_mg{il}")
        sil = c.XTI[:, S:2 * S]
        g.step("act", lambda eng, psg=psg, sil=sil: eng.activation(
            sil, psg[:, :], AF.Silu),
            reads=keyspan(psgn, 0, 4), writes=[("XTI", 1)], tag=f"d_ms{il}")
        psu = c.PS_B if il % 2 == 0 else c.PS_A
        psun = "PS_B" if il % 2 == 0 else "PS_A"

        def umm(eng, wtu=wtu, psu=psu):
            last = None
            for d in range(NT):
                lhsT = wtu[:, 128 * d:128 * (d + 1)]
                for ch in range(4):
                    last = eng.matmul(psu[:, 512 * ch:512 * ch + 512], lhsT,
                                      biga(c, d)[:, 512 * ch:512 * ch + 512],
                                      start=(d == 0), stop=(d == NT - 1))
            return last
        g.step("pe", umm, reads=kwu + keyspan("BIGA", 0, 64),
               writes=keyspan(psun, 0, 4), tag=f"d_mu{il}")
        adst = BB(32 + 4 * il, 4)
        adk = keyspan("BIGB", 32 + 4 * il, 4)
        g.step("dve", lambda eng, psu=psu, sil=sil, adst=adst: eng.tensor_mul(
            adst, psu[:, :], sil),
            reads=keyspan(psun, 0, 4) + [("XTI", 1)], writes=adk,
            tag=f"d_ma{il}")

    def aT(il):
        return BB(32 + 4 * il, 4)

    g.step("sp", lambda eng: [
        eng.dma_start(out=c.CAM[:, 256:512],
                      in_=rows_d[5:6, 0:256].partition_broadcast(128))],
        reads=[("rows_d", 5), ("CAM",)], writes=[("CAM",)], dma="ld", n_dma=1,
        tag="d_gpb")

    wdslots = [(c.WBUF[:, 0:S], keyspan("WBUF", 0, 4)),
               (c.WBUF[:, S:2 * S], keyspan("WBUF", 4, 4)),
               (BB(0, 4), keyspan("BIGB", 0, 4)),
               (BB(4, 4), keyspan("BIGB", 4, 4))]
    for jo in range(NT):
        wtd, kwd = wdslots[jo % 4]
        dsrc = t["l2_wds"].rearrange("(a p) (j f) -> j p a f", p=128, f=128)[jo]
        wt3d = wtd[:, 0:1024].rearrange("p (a f) -> p a f", f=128)
        g.step("pool", lambda eng, dsrc=dsrc, wt3d=wt3d: [
            eng.dma_start(out=wt3d, in_=dsrc)],
            writes=kwd, dma="wt", n_dma=1, tag=f"d_dw{jo}")
        pst = c.PS_A if jo % 2 == 0 else c.PS_B
        pstn = "PS_A" if jo % 2 == 0 else "PS_B"

        def dmm(eng, wtd=wtd, pst=pst):
            last = None
            for a in range(8):
                lhsT = wtd[:, 128 * a:128 * (a + 1)]
                for ch in range(4):
                    last = eng.matmul(pst[:, 512 * ch:512 * ch + 512], lhsT,
                                      aT(a)[:, 512 * ch:512 * ch + 512],
                                      start=(a == 0), stop=(a == 7))
            return last
        g.step("pe", dmm,
               reads=kwd + [("BIGB", 32 + u) for u in range(32)],
               writes=keyspan(pstn, 0, 4), tag=f"d_dm{jo}")
        stg = c.EBUF[:, (jo % 2) * S:(jo % 2) * S + S]
        stk = keyspan("EBUF", 4 * (jo % 2), 4)
        g.step("act", lambda eng, pst=pst, stg=stg: eng.copy(stg, pst[:, :]),
               reads=keyspan(pstn, 0, 4), writes=stk, tag=f"d_de{jo}")
        def dstore(eng, jo=jo, stg=stg):
            ins = []
            for ch in range(4):
                rb = ch // 1
                ins.append(eng.dma_start(
                    out=t["rsin"].rearrange("(b a p) f -> b a p f", b=8, p=128)
                    [2 * ch, jo] if False else
                    t["rsin"][2048 * (2 * ch) + 128 * jo:
                              2048 * (2 * ch) + 128 * (jo + 1), :],
                    in_=stg[:, 512 * ch:512 * ch + 256]))
                ins.append(eng.dma_start(
                    out=t["rsin"][2048 * (2 * ch + 1) + 128 * jo:
                                  2048 * (2 * ch + 1) + 128 * (jo + 1), :],
                    in_=stg[:, 512 * ch + 256:512 * ch + 512]))
            return ins
        g.step("sp", dstore, reads=stk, writes=[("rsin", jo)], dma="st",
               n_dma=8, tag=f"d_ds{jo}")
    g.step("pool", lambda eng: [eng.collective_compute(
        "ReduceScatter", ALU.add, replica_groups=[list(range(8))],
        ins=[t["rsin"].opt()], outs=[t["rsout"].opt()])],
        reads=[("rsin", jo) for jo in range(NT)], writes=[("rsout",)],
        dma="cc", n_dma=1, tag="d_rs")
    for jo in range(NT):
        mdst = BB(24 + jo // 2)[:, 256 * (jo % 2):256 * (jo % 2) + 256]
        g.step("sp", lambda eng, jo=jo, mdst=mdst: [
            eng.dma_start(out=mdst, in_=t["rsout"][128 * jo:128 * (jo + 1), :])],
            reads=[("rsout",)] + [("BIGB", 24 + jo // 2)],
            writes=[("BIGB", 24 + jo // 2)], dma="ld", n_dma=1, tag=f"d_ml{jo}")

    def mlpT(jo):
        return BB(24 + jo // 2)[:, 256 * (jo % 2):256 * (jo % 2) + 256]

    for jo in range(NT):
        tmpd = c.QBF[:, 0:256]
        g.step("dve", lambda eng, jo=jo, tmpd=tmpd: eng.tensor_add(
            tmpd, mlpT(jo), attnT(jo)),
            reads=[("BIGB", 24 + jo // 2), ("QCH", 8 + jo // 2)],
            writes=[("QBF", 0)], tag=f"d_dd{jo}")
        gpd = QCHU(jo // 2)[:, 256 * (jo % 2):256 * (jo % 2) + 256]
        g.step("dve", lambda eng, tmpd=tmpd, gpd=gpd: eng.tensor_mul(
            gpd, tmpd, c.CAM[:, 256:512]),
            reads=[("QBF", 0), ("CAM",)], writes=[("QCH", jo // 2)],
            tag=f"d_dg{jo}")

    def gpDT(jo):
        return QCHU(jo // 2)[:, 256 * (jo % 2):256 * (jo % 2) + 256]

    for pc in range(2):
        for quad in range(4):
            srcs, rkeys = [], []
            for sub in range(4):
                jo = 4 * quad + sub
                lhsT = gpDT(jo)[:, 128 * pc:128 * (pc + 1)]
                srcs.append((lhsT, 128 * sub))
                rkeys.append(("QCH", jo // 2))
            ps, pk = c.PS_A[:, 1024:1536], ("PS_A", 2)
            _transpose_pair_group(c, g, srcs, ps, [pk], list(set(rkeys)),
                                  f"d_gt{pc}_{quad}")
            dst = QCHU(8 + 4 * pc + quad)
            g.step("dve", lambda eng, ps=ps, dst=dst: eng.tensor_copy(dst, ps),
                   reads=[pk], writes=[("QCH", 8 + 4 * pc + quad)],
                   tag=f"d_ge2{pc}_{quad}")

    def gpd_tok(pc):
        return QCHU(8 + 4 * pc, 4)

    for pc in range(2):
        g.step("sp", lambda eng, pc=pc: [
            eng.dma_start(out=BB(4 * pc, 4),
                          in_=t["psel_d"][0:2048, 128 * pc:128 * (pc + 1)],
                          transpose=True)],
            reads=[("psel_d", T) for T in range(NT)] + keyspan("BIGB", 4 * pc, 4),
            writes=keyspan("BIGB", 4 * pc, 4), dma="tp", n_dma=1, tag=f"d_pt{pc}")

    for jo in range(NT):
        pst = c.PS_A if jo % 2 == 0 else c.PS_B
        pstn = "PS_A" if jo % 2 == 0 else "PS_B"
        for ch in range(4):
            psf, pkf = pst[:, 512 * ch:512 * (ch + 1)], (pstn, ch)

            def smm(eng, jo=jo, ch=ch, psf=psf):
                last = None
                for pc in range(2):
                    last = eng.matmul(psf, gpd_tok(pc)[:, 128 * jo:128 * (jo + 1)],
                                      BB(4 * pc, 4)[:, 512 * ch:512 * (ch + 1)],
                                      start=(pc == 0), stop=(pc == 1))
                return last
            g.step("pe", smm,
                   reads=[("QCH", 8 + u) for u in range(8)] +
                   keyspan("BIGB", 0, 8),
                   writes=[pkf], tag=f"d_sc{jo}_{ch}")
        xbuf, xk = [(c.PANA[:, :], ("PANA",)),
                    (c.PANB[:, :], ("PANB",)),
                    (c.XTI[:, 0:S], ("XTI", 0)),
                    (c.XTI[:, S:2 * S], ("XTI", 1))][jo % 4]
        g.step("sp", lambda eng, jo=jo, xbuf=xbuf: [
            eng.dma_start(out=xbuf, in_=xsrc[128 * jo:128 * (jo + 1), :])],
            reads=[("xm", jo)] + [("xm2", jo, s_) for s_ in range(4)] + [xk],
            writes=[xk], dma="ld", n_dma=1, tag=f"d_fx{jo}")
        g.step("pool", lambda eng, jo=jo, xbuf=xbuf: [
            eng.dma_start(out=xbuf, in_=yln[128 * jo:128 * (jo + 1), :],
                          accum_op=ALU.add)],
            reads=[("yd", jo), xk], writes=[xk], dma="wt", n_dma=1,
            tag=f"d_fy{jo}")
        psout = c.PS_A if jo % 2 == 0 else c.PS_B
        g.step("dve" if jo % 2 == 0 else "dve",
               lambda eng, xbuf=xbuf, psout=psout: eng.tensor_add(
                   xbuf, xbuf, psout[:, :]),
               reads=[xk] + keyspan("PS_A" if jo % 2 == 0 else "PS_B", 0, 4),
               writes=[xk], tag=f"d_fc{jo}")
        g.step("sp", lambda eng, jo=jo, xbuf=xbuf: [
            eng.dma_start(out=outT[128 * jo:128 * (jo + 1), :], in_=xbuf)],
            reads=[xk], writes=[("outT", jo)], dma="st", n_dma=1,
            tag=f"d_fo{jo}")


import ml_dtypes
from concourse.bass_utils import run_bass_kernel_spmd

FUSED = True
BF = ml_dtypes.bfloat16


def _colsT(a, n):
    return np.ascontiguousarray(np.asarray(a, np.float32).reshape(n, 128).T)


def _host_stage(inputs):
    inp = {k: np.asarray(v) for k, v in inputs.items()}
    sh = {}
    sh["l0_w_qkv"] = inp["L0_attn_c_attn_w"]
    sh["l0_b_qkv"] = _colsT(inp["L0_attn_c_attn_b"], 48)
    sh["l0_w_proj"] = inp["L0_attn_c_proj_w"]
    sh["l0_b_proj"] = _colsT(inp["L0_attn_c_proj_b"], 16)
    sh["l0_cam_gate"] = np.ascontiguousarray(inp["L0_attn_cam_gate"][:, None])
    sh["l0_cam_w"] = np.ascontiguousarray(
        np.stack([inp["L0_attn_cam_w0"][0], inp["L0_attn_cam_w1"][0]])[None, :])
    sh["l0_wgT"] = np.ascontiguousarray(inp["L0_ffn_wq_gate"].T)
    sh["l0_wuT"] = np.ascontiguousarray(inp["L0_ffn_wq_up"].T)
    sh["l0_wdT"] = np.ascontiguousarray(inp["L0_ffn_wq_down"].T)
    scol = np.zeros((128, 104), dtype=np.float32)
    scol[:, 0:44] = _colsT(inp["L0_ffn_s_gate"], 44)
    scol[:, 44:88] = _colsT(inp["L0_ffn_s_up"], 44)
    scol[:, 88:104] = _colsT(inp["L0_ffn_s_down"], 16)
    sh["l0_scol"] = scol
    sh["l1_wq"] = inp["L1_attn_wq"]
    sh["l1_wk"] = inp["L1_attn_wk"]
    sh["l1_wv"] = inp["L1_attn_wv"]
    sh["l1_wo"] = inp["L1_attn_wo"]
    half = 64
    invf = 1.0 / (100000.0 ** (np.arange(half, dtype=np.float64) / half))
    ang = np.arange(2048, dtype=np.float64)[:, None] * invf[None, :]
    cosT = np.cos(ang).T
    sinT = np.sin(ang).T
    sh["l1_c2T"] = np.concatenate([cosT, cosT], 0).astype(np.float32)
    sh["l1_s2T"] = np.concatenate([sinT, -sinT], 0).astype(np.float32)
    perm = np.zeros((128, 128), dtype=np.float32)
    for mc in range(128):
        perm[(mc + 64) % 128, mc] = 1.0
    sh["l1_perm64"] = perm.astype(BF)
    sh["l1_ones1f"] = np.ones((1, 128), dtype=np.float32)
    sh["l2_wq"] = inp["L1_ffn_wq"]
    sh["l2_wk"] = inp["L1_ffn_wk"]
    sh["l2_wv"] = inp["L1_ffn_wv"]
    sh["l2_wo"] = inp["L1_ffn_wo"]

    sh["l2_wr"] = _colsT(inp["L1_ffn_w_router"], 16)
    sh["l2_bq"] = _colsT(inp["L1_ffn_bq"], 16)
    sh["l2_bkv"] = np.concatenate(
        [_colsT(inp["L1_ffn_bk"], 8), _colsT(inp["L1_ffn_bv"], 8)], 1)
    sh["l2_ln1r"] = np.ascontiguousarray(
        np.asarray(inp["L1_ffn_ln1_w"], np.float32)[None, :])
    sh["l2_ln2c"] = _colsT(inp["L1_ffn_ln2_w"], 16)
    invf2 = 1.0 / (10000.0 ** (np.arange(0, 128, 2, dtype=np.float64) / 128.0))
    ang2 = np.arange(2048, dtype=np.float64)[:, None] * invf2[None, :]
    sh["l2_ctabd"] = np.concatenate(
        [np.cos(ang2), np.cos(ang2)], 1).astype(np.float32)
    sh["l2_stabd"] = np.concatenate(
        [-np.sin(ang2), np.sin(ang2)], 1).astype(np.float32)
    sh["l2_iota256"] = np.arange(256, dtype=np.float32)[None, :]
    sh["l2_idn"] = np.eye(128, dtype=BF)
    sh["l2_sut"] = np.triu(np.ones((128, 128)), 1).astype(BF)
    sh["l2_ons"] = np.ones((128, 128), dtype=BF)
    sh["ones_col"] = np.ones((128, 1), dtype=BF)
    sh["eps_col"] = np.tile(np.array([[1e-5, 1e-6]], np.float32), (128, 1))
    x = np.asarray(inp["L0_attn_hidden_states"], np.float32)
    xTs = [np.ascontiguousarray(x[r].T) for r in range(8)]
    wgs = [np.ascontiguousarray(inp["L1_ffn_w_gate"][:, 1024 * r:1024 * (r + 1)])
           for r in range(8)]
    wus = [np.ascontiguousarray(inp["L1_ffn_w_up"][:, 1024 * r:1024 * (r + 1)])
           for r in range(8)]
    wds = [np.ascontiguousarray(inp["L1_ffn_w_down"][1024 * r:1024 * (r + 1), :])
           for r in range(8)]
    sh["_wgs"], sh["_wus"], sh["_wds"] = wgs, wus, wds
    tt = np.arange(2048)[:, None]
    masks = [np.ascontiguousarray(
        (tt <= (256 * r + np.arange(256)[None, :])).astype(BF)) for r in range(8)]
    return sh, xTs, masks


_EI_SHAPES = {
    "l0_w_qkv": ([2048, 6144], F32), "l0_b_qkv": ([128, 48], F32),
    "l0_w_proj": ([2048, 2048], F32), "l0_b_proj": ([128, 16], F32),
    "l0_cam_gate": ([128, 1], F32), "l0_cam_w": ([1, 2], F32),
    "l0_wgT": ([2048, 5632], I32), "l0_wuT": ([2048, 5632], I32),
    "l0_wdT": ([5632, 2048], I32), "l0_scol": ([128, 104], F32),
    "l1_wq": ([2048, 2048], F32), "l1_wk": ([2048, 512], F32),
    "l1_wv": ([2048, 512], F32), "l1_wo": ([2048, 2048], F32),
    "l1_c2T": ([128, 2048], F32), "l1_s2T": ([128, 2048], F32),
    "l1_perm64": ([128, 128], BF16), "l1_ones1f": ([1, 128], F32),
    "l2_wq": ([2048, 2048], F32), "l2_wk": ([2048, 1024], F32),
    "l2_wv": ([2048, 1024], F32), "l2_wo": ([2048, 2048], F32),
    "l2_wgs": ([2048, 1024], F32), "l2_wus": ([2048, 1024], F32),
    "l2_wds": ([1024, 2048], F32), "l2_wr": ([128, 16], F32),
    "l2_bq": ([128, 16], F32), "l2_bkv": ([128, 16], F32),
    "l2_ln1r": ([1, 2048], F32), "l2_ln2c": ([128, 16], F32),
    "l2_ctabd": ([2048, 128], F32), "l2_stabd": ([2048, 128], F32),
    "l2_iota256": ([1, 256], F32), "l2_idn": ([128, 128], BF16),
    "l2_sut": ([128, 128], BF16), "l2_ons": ([128, 128], BF16),
    "l2_mask": ([2048, 256], BF16),
    "ones_col": ([128, 1], BF16), "eps_col": ([128, 2], F32),
}


def _declare(nc, t, names, xm_kind, out=False):
    from contextlib import ExitStack
    t["xT"] = nc.dram_tensor("xT", [2048, 2048], F32, kind="ExternalInput").ap()
    for n in names:
        shp, dt = _EI_SHAPES[n]
        t[n] = nc.dram_tensor(n, shp, dt, kind="ExternalInput").ap()
    t["xm"] = nc.dram_tensor("xm", [2048, 2048], F32, kind=xm_kind).ap()
    if out:
        t["outT"] = nc.dram_tensor("outT", [2048, 2048], F32,
                                   kind="ExternalOutput").ap()
    t["rows_d"] = nc.dram_tensor("rows_d", [8, 2048], F32).ap()
    return t


_L0A = ["l0_w_qkv", "l0_b_qkv", "l0_w_proj", "l0_b_proj", "l0_cam_gate",
        "l0_cam_w", "ones_col", "eps_col"]
_L0F = ["l0_wgT", "l0_wuT", "l0_wdT", "l0_scol", "ones_col", "eps_col"]
_L1A = ["l1_wq", "l1_wk", "l1_wv", "l1_wo", "l1_c2T", "l1_s2T", "l1_perm64",
        "l1_ones1f", "ones_col", "eps_col"]
_L1F = ["l2_wq", "l2_wk", "l2_wv", "l2_wo", "l2_wgs", "l2_wus", "l2_wds",
        "l2_wr", "l2_bq", "l2_bkv", "l2_ln1r", "l2_ln2c", "l2_ctabd",
        "l2_stabd", "l2_iota256", "l2_idn", "l2_sut", "l2_ons", "l2_mask",
        "l1_perm64", "l1_ones1f", "ones_col", "eps_col"]


def _add_d_scratch(nc, t):
    t["h2in"] = nc.dram_tensor("h2in", [2048, 256], BF16).ap()
    t["h2out"] = nc.dram_tensor("h2out", [16384, 256], BF16).ap()
    t["rsin"] = nc.dram_tensor("rsin", [16384, 256], BF16).ap()
    t["rsout"] = nc.dram_tensor("rsout", [2048, 256], BF16).ap()
    t["yln"] = nc.dram_tensor("yln", [2048, 2048], BF16).ap()
    t["psel_d"] = nc.dram_tensor("psel_d", [2048, 256], BF16).ap()
    t["kin"] = nc.dram_tensor("kin", [1024, 256], BF16).ap()
    t["kout"] = nc.dram_tensor("kout", [8192, 256], BF16).ap()
    t["vin"] = nc.dram_tensor("vin", [256, 1024], BF16).ap()
    t["vout"] = nc.dram_tensor("vout", [2048, 1024], BF16).ap()


def _build_launch(stage):
    from contextlib import ExitStack
    nc = bass.Bass("TRN2", target_bir_lowering=False)
    t = {}
    es = ExitStack()
    with es:
        if stage == "a":
            _declare(nc, t, _L0A, "ExternalOutput")
            t["qT_d"] = nc.dram_tensor("qT_d", [2048, 2048], BF16).ap()
            t["vT_d"] = nc.dram_tensor("vT_d", [2048, 2048], BF16).ap()
            c = alloc_common(nc, es)
            build_preload(c, t)
            build_ln(c, t["xT"], t["rows_d"], tag="ln1")
            build_l0_attn(c, t, xsrc=t["xT"])
            finalize(c, [("xm", j) for j in range(16)] +
                     [("xm2", j, s) for j in range(16) for s in range(4)])
        elif stage == "b":
            _declare(nc, t, _L0F, "ExternalOutput")
            t["dpart"] = nc.dram_tensor("dpart", [2048, 2048], F32).ap()
            c = alloc_common(nc, es)
            build_preload(c, t)
            build_ln(c, t["xT"], t["rows_d"], tag="ln2")
            build_l0_ffn(c, t, xsrc=t["xT"])
            finalize(c, [("xm", j) for j in range(16)])
        elif stage == "c":
            _declare(nc, t, _L1A, "ExternalOutput")
            t["qT_d"] = nc.dram_tensor("qT_d", [2048, 2048], BF16).ap()
            c = alloc_common(nc, es)
            build_preload(c, t)
            c.g.step("pool", lambda eng: [
                eng.dma_start(out=c.PERM[:, :], in_=t["l1_perm64"][:, :]),
                eng.dma_start(out=c.ONES1F[:, :], in_=t["l1_ones1f"][:, :])],
                writes=[("perm64",), ("ones1f",)], dma="wt", n_dma=2, tag="pre2")
            build_ln(c, t["xT"], t["rows_d"], tag="ln3")
            build_l1_attn(c, t, xsrc=t["xT"])
            finalize(c, [("xm", j) for j in range(16)] +
                     [("xm2", j, s) for j in range(16) for s in range(4)])
        else:
            _declare(nc, t, _L1F, "Internal", out=True)
            _add_d_scratch(nc, t)
            t["xm"] = t["xT"]
            c = alloc_common(nc, es)
            build_preload(c, t)
            c.g.step("pool", lambda eng: [
                eng.dma_start(out=c.PERM[:, :], in_=t["l1_perm64"][:, :]),
                eng.dma_start(out=c.ONES1F[:, :], in_=t["l1_ones1f"][:, :])],
                writes=[("perm64",), ("ones1f",)], dma="wt", n_dma=2, tag="pre2")
            build_ln(c, t["xT"], t["rows_d"], yd=t["yln"], tag="ln4")
            build_l1_ffn(c, t, xsrc=t["xT"])
            finalize(c, [("outT", j) for j in range(16)])
        c.g.emit(nc, es)
    return nc


def _build_fused():
    from contextlib import ExitStack
    nc = bass.Bass("TRN2", target_bir_lowering=False)
    t = {}
    names = sorted(set(_L0A + _L0F + _L1A + _L1F))
    es = ExitStack()
    with es:
        _declare(nc, t, names, "Internal", out=True)
        t["qT_d"] = nc.dram_tensor("qT_d", [2048, 2048], BF16).ap()
        t["vT_d"] = nc.dram_tensor("vT_d", [2048, 2048], BF16).ap()
        t["dpart"] = nc.dram_tensor("dpart", [2048, 2048], F32).ap()
        _add_d_scratch(nc, t)
        c = alloc_common(nc, es)
        build_preload(c, t)
        c.g.step("pool", lambda eng: [
            eng.dma_start(out=c.PERM[:, :], in_=t["l1_perm64"][:, :]),
            eng.dma_start(out=c.ONES1F[:, :], in_=t["l1_ones1f"][:, :])],
            writes=[("perm64",), ("ones1f",)], dma="wt", n_dma=2, tag="pre2")
        build_ln(c, t["xT"], t["rows_d"], tag="ln1")
        build_l0_attn(c, t, xsrc=t["xT"])
        build_ln(c, t["xm"], t["rows_d"], tag="ln2")
        build_l0_ffn(c, t, xsrc=t["xm"])
        build_ln(c, t["xm"], t["rows_d"], tag="ln3")
        build_l1_attn(c, t, xsrc=t["xm"])
        build_ln(c, t["xm"], t["rows_d"], yd=t["yln"], tag="ln4")
        build_l1_ffn(c, t, xsrc=t["xm"])
        finalize(c, [("outT", j) for j in range(16)])
        c.g.emit(nc, es)
    return nc


def kernel(**inputs):
    sh, xTs, masks = _host_stage(inputs)
    cores = list(range(8))
    if FUSED:
        nc = _build_fused()
        names = sorted(set(_L0A + _L0F + _L1A + _L1F))
        in_maps = []
        percore = {"l2_mask": None, "l2_wgs": None, "l2_wus": None,
                   "l2_wds": None}
        for r in range(8):
            m = {n: sh[n] for n in names if n not in percore and
                 not n.startswith("_")}
            m["l2_mask"] = masks[r]
            m["l2_wgs"] = sh["_wgs"][r]
            m["l2_wus"] = sh["_wus"][r]
            m["l2_wds"] = sh["_wds"][r]
            m["xT"] = xTs[r]
            in_maps.append(m)
        res = run_bass_kernel_spmd(nc, in_maps, core_ids=cores)
        outs = [res.results[r]["outT"].T for r in range(8)]
    else:
        cur = xTs
        for stage, names in (("a", _L0A), ("b", _L0F), ("c", _L1A)):
            nc = _build_launch(stage)
            in_maps = [dict({n: sh[n] for n in names}, xT=cur[r])
                       for r in range(8)]
            res = run_bass_kernel_spmd(nc, in_maps, core_ids=cores)
            cur = [np.ascontiguousarray(res.results[r]["xm"]) for r in range(8)]
        nc = _build_launch("d")
        in_maps = []
        for r in range(8):
            m = {n: sh[n] for n in _L1F if n not in
                 ("l2_mask", "l2_wgs", "l2_wus", "l2_wds")}
            m["l2_mask"] = masks[r]
            m["l2_wgs"] = sh["_wgs"][r]
            m["l2_wus"] = sh["_wus"][r]
            m["l2_wds"] = sh["_wds"][r]
            m["xT"] = cur[r]
            in_maps.append(m)
        res = run_bass_kernel_spmd(nc, in_maps, core_ids=cores)
        outs = [res.results[r]["outT"].T for r in range(8)]
    return np.ascontiguousarray(np.stack(outs, axis=0)).astype(np.float32)
